# version 19
# baseline (speedup 1.0000x reference)
import numpy as np
import ml_dtypes
import concourse.bass as bass
import concourse.mybir as mybir
from concourse.bass_utils import run_bass_kernel_spmd

F32 = mybir.dt.float32
BF16 = mybir.dt.bfloat16
U8 = mybir.dt.uint8
I32 = mybir.dt.int32
AF = mybir.ActivationFunctionType
OP = mybir.AluOpType
AX = mybir.AxisListType

P = 128
NQT = 8
NCT = 32
NDT = 8
NFT = 24
NGT = 32
SCALE_ATT = 0.03125
EPS = 1e-5
BT_GT = 25
BT_P = 77
KB = 1024

OFF_IDENT = 0
OFF_BQKV = 256
OFF_ONEC = 384
OFF_ONER = 416
OFF_DINH = 928
OFF_BSEL = 960
OFF_MAXP = 992
OFF_NEGM = 1024
OFF_SEP = 1056
OFF_SE = 1088
OFF_RECIP = 1120
OFF_MSUM = 1152
OFF_M = 1184
OFF_M2 = 1216
OFF_VAR = 1248
OFF_RSTD = 1280
OFF_SQS = 1312
OFF_RS0 = 1312
OFF_MSP = 1344
OFF_SQP = 1408
OFF_RSROW = 1472
OFF_BIAS2 = 5568
CONSTS_END = 6144
R0 = CONSTS_END
R1 = R0 + 64 * KB
R2 = R1 + 64 * KB
R3 = R2 + 16 * KB
R4 = R3 + 16 * KB
R5 = R4 + 24 * KB
SB_END = R5 + 8 * KB + 512

QKV_ORDER = list(range(8, 16)) + list(range(16, 24)) + list(range(0, 8))
WO_GROUPS = [[0, 1], [2, 3], [4, 5], [6, 7]]
WO_SLOTS = [[0, 1], [2, 0], [1, 2], [0, 1]]
PREV_ON_SLOT = {}
for _g, (_qs, _ss) in enumerate(zip(WO_GROUPS, WO_SLOTS)):
    for _q, _s in zip(_qs, _ss):
        hist = [qq for gg2 in range(_g) for qq, ss2 in
                zip(WO_GROUPS[gg2], WO_SLOTS[gg2]) if ss2 == _s]
        PREV_ON_SLOT[_q] = hist[-1] if hist else None


class _DryIns:
    def then_inc(self, sem, k=1):
        return self


_DRY = _DryIns()


class _DryEng:

    def __getattr__(self, name):
        return lambda *a, **k: _DRY


class _RecIns:
    def __init__(self, log):
        self.log = log

    def then_inc(self, sem, k=1):
        self.log.append(("inc", id(sem), k))
        return self


class _RecEng:

    def __init__(self, stream):
        self.stream = stream

    def __getattr__(self, name):
        if name == "wait_ge":
            def w(sem, n):
                self.stream.append(("wait", id(sem), n))
            return w

        def op(*a, **kw):
            log = []
            self.stream.append(("op", log))
            return _RecIns(log)
        return op


def _simulate(streams, semnames):
    vals = {}
    pcs = {e: 0 for e in streams}
    progress = True
    while progress:
        progress = False
        for e, ops in streams.items():
            while pcs[e] < len(ops):
                kind, a, *rest = ops[pcs[e]]
                if kind == "wait":
                    if vals.get(a, 0) < rest[0]:
                        break
                    pcs[e] += 1
                    progress = True
                else:
                    for _, semid, k in a:
                        vals[semid] = vals.get(semid, 0) + k
                    pcs[e] += 1
                    progress = True
    stuck = {e: pcs[e] for e in streams if pcs[e] < len(streams[e])}
    if stuck:
        msg = ["DEADLOCK DETECTED in program graph:"]
        for e, pc in stuck.items():
            kind, semid, n = streams[e][pc]
            msg.append(f"  {e} blocked at op#{pc} waiting "
                       f"{semnames.get(semid, semid)} >= {n} "
                       f"(currently {vals.get(semid, 0)})")
        raise RuntimeError("\n".join(msg))


class Ctr:
    def __init__(self, sem):
        self.sem = sem
        self.n = 0

    def tick(self, ins, k=1):
        ins.then_inc(self.sem, k)
        self.n += k
        return self.n

    def reset(self):
        self.n = 0


def build_nc(s0, s1, s2, debug=False):
    nc = bass.Bass("TRN2", target_bir_lowering=False)

    x_in = nc.dram_tensor("x", [1024, 4096], F32, kind="ExternalInput")
    wq_blk = nc.dram_tensor("wq_blk", [NFT, P, NCT, P], F32, kind="ExternalInput")
    wo_in = nc.dram_tensor("wo", [1024, 4096], F32, kind="ExternalInput")
    k0_blk = nc.dram_tensor("k0_blk", [NGT, P, NCT, P], U8, kind="ExternalInput")
    k1_blk = nc.dram_tensor("k1_blk", [NGT, P, NCT, P], U8, kind="ExternalInput")
    k2_in = nc.dram_tensor("k2", [4096, 4096], U8, kind="ExternalInput")
    bqkv_in = nc.dram_tensor("bqkv_col", [P, NFT], F32, kind="ExternalInput")
    bo_in = nc.dram_tensor("bo_rep", [P, 4096], F32, kind="ExternalInput")
    ident_in = nc.dram_tensor("ident", [P, P], BF16, kind="ExternalInput")
    onec_in = nc.dram_tensor("ones_col", [P, 1], BF16, kind="ExternalInput")
    oner_in = nc.dram_tensor("ones_row", [1, P], F32, kind="ExternalInput")
    dinh_in = nc.dram_tensor("dinh", [P, 1], F32, kind="ExternalInput")
    bsel_in = nc.dram_tensor("bsel", [1, 1], I32, kind="ExternalInput")
    out_dr = nc.dram_tensor("out", [1024, 4096], F32, kind="ExternalOutput")

    taps = {}
    if debug:
        for nm, shp, dt_ in [
                ("tap_xt", [P, 32768], BF16), ("tap_qt", [P, 8192], BF16),
                ("tap_kv", [P, 8192], BF16), ("tap_vs", [P, 8192], BF16),
                ("tap_kt", [P, 32768], BF16), ("tap_vf", [P, 32768], BF16),
                ("tap_p", [P, 4096], BF16), ("tap_ot", [P, 8192], BF16),
                ("tap_x1", [1024, 4096], F32), ("tap_b1", [P, 32768], BF16),
                ("tap_b2", [P, 32768], BF16), ("tap_rs", [1, 1024], F32)]:
            taps[nm] = nc.dram_tensor(nm, shp, dt_, kind="ExternalOutput")
    wobf = nc.dram_tensor("wobf", [1024, 4096], BF16)
    cc_in_k = nc.dram_tensor("cc_in_k", [1024, 1024], BF16)
    cc_in_v = nc.dram_tensor("cc_in_v", [1024, 1024], BF16)
    cc_out_k = nc.dram_tensor("cc_out_k", [8 * 1024, 1024], BF16,
                              addr_space="Shared")
    cc_out_v = nc.dram_tensor("cc_out_v", [8 * 1024, 1024], BF16,
                              addr_space="Shared")
    x1sp = nc.dram_tensor("x1sp", [1024, 4096], BF16)

    sb = nc.alloc_sbuf_tensor("sbarena", [P, SB_END], U8)
    psa = nc.alloc_psum_tensor("psarena", [P, 16384], U8)

    def sbv(off, cols, dt, p0=0, np_=P):
        bpe = mybir.dt.size(dt)
        return sb[p0:p0 + np_, off:off + cols * bpe].bitcast(dt)

    def sb3(off, n0, n1, dt):
        bpe = mybir.dt.size(dt)
        return sb[:, off:off + n0 * n1 * bpe].bitcast(dt).rearrange(
            "p (a b) -> p a b", a=n0)

    def bank_f32(b, cols=512):
        return psa[:, b * 2048: b * 2048 + cols * 4].bitcast(F32)

    def bank_slot_bf16(b, slot):
        o = b * 2048 + slot * 256
        return psa[:, o:o + 256].bitcast(BF16)

    def bank_row_f32(b, cols=512):
        return psa[0:1, b * 2048: b * 2048 + cols * 4].bitcast(F32)

    def bank3_bf16(b, n0, n1):
        return psa[:, b * 2048: b * 2048 + n0 * n1 * 2].bitcast(BF16).rearrange(
            "p (a b) -> p a b", a=n0)

    idt = sbv(OFF_IDENT, P, BF16)
    bqkv = sbv(OFF_BQKV, NFT, F32)
    onec = sbv(OFF_ONEC, 1, BF16)
    oner = sbv(OFF_ONER, P, F32, np_=1)
    dinh = sbv(OFF_DINH, 1, F32)
    bsel_sb = sb[0:1, OFF_BSEL:OFF_BSEL + 4].bitcast(I32)
    maxp = sbv(OFF_MAXP, 8, F32)
    negm = sbv(OFF_NEGM, 1, F32)
    sep = sbv(OFF_SEP, 8, F32)
    sumexp = sbv(OFF_SE, 1, F32)
    recip = sbv(OFF_RECIP, 1, F32)
    msum = sbv(OFF_MSUM, 1, F32)
    m_sb = sbv(OFF_M, 1, F32)
    m2_sb = sbv(OFF_M2, 1, F32)
    var_sb = sbv(OFF_VAR, 1, F32)
    rstd = sbv(OFF_RSTD, 1, F32)
    sqs = sbv(OFF_SQS, 1, F32)
    rs0c = sbv(5600, 2, F32)
    rs0b = sbv(5664, 2, BF16)
    msp = sbv(OFF_MSP, 16, F32)
    sqp = sbv(OFF_SQP, 16, F32)
    rsrow = sbv(OFF_RSROW, 1024, F32, np_=1)
    bias2 = sbv(OFF_BIAS2, 8, F32)

    XT = sb3(R0, NCT, 1024, BF16)
    VF = sb3(R0, NCT, 1024, BF16)
    X1T = sb3(R0, NCT, 1024, BF16)
    B2T = sb3(R0, NGT, 1024, BF16)

    xbf = [sbv(R1 + i * 8 * KB, 4096, BF16) for i in range(2)]
    kvK = sb3(R1 + 32 * KB, NDT, 1024, BF16)
    vstage = sb3(R1 + 48 * KB, NDT, 1024, BF16)
    KT = sb3(R1, NDT, 4096, BF16)
    zg = [sbv(R1 + i * 16 * KB, 4096, F32) for i in range(3)]
    xg = [sbv(R1 + i * 16 * KB + 8 * KB, 4096, BF16) for i in range(3)]
    wopan2 = [sb3(R2 + i * 4 * KB, 2, 1024, BF16) for i in range(2)]
    B1T = sb3(R1, NGT, 1024, BF16)
    z2g = [sbv(R1 + i * 16 * KB, 4096, F32) for i in range(4)]

    QT = sb3(R2, NDT, 1024, BF16)
    bo_rep = sbv(R2, 4096, F32)
    sq_dump2 = sbv(R2, 4096, BF16)

    bo_bf = sbv(R5, 4096, BF16, np_=1)
    ones_bf = sbv(R5 + 8 * KB, P, BF16, np_=1)
    OT = sb3(R3, NQT, NDT * P, BF16)
    mkpan2 = [sb3(R3, 2, 4096, BF16), sb3(R2, 2, 4096, BF16)]
    wqpan = [sbv(R3 + i * 8 * KB, 4096, BF16) for i in range(2)]

    vt_tmp = [sbv(R4 + i * 2 * KB, 1024, BF16) for i in range(2)]
    P_sb = sbv(R4, 4096, BF16)
    PT = sb3(R4 + 8 * KB, NCT, P, BF16)
    O_bf = sbv(R4 + 16 * KB, 1024, BF16)
    x1bf = [sbv(R4 + i * 8 * KB, 4096, BF16) for i in range(2)]
    sq_dump1 = sbv(R4 + 16 * KB, 4096, BF16)
    rsb_sb = sbv(R4, 1024, F32)
    tmp_ffn = [sbv(R4 + 4 * KB + i * 2 * KB, 512, F32) for i in range(4)]
    u2pan2 = [sb3(R3 + i * 4 * KB, 2, 1024, BF16) for i in range(2)]
    x1rel = [sbv(R3 + 8 * KB, 4096, BF16), sbv(R2 + 8 * KB, 4096, BF16),
             sbv(R4, 4096, BF16), sbv(R4 + 8 * KB, 4096, BF16)]
    ffn_t = [sbv(R4 + 16 * KB + i * 4 * KB, 1024, F32) for i in range(2)]

    import contextlib
    stack = contextlib.ExitStack()
    sems = {}
    for name in ["g16", "gx0", "gx1", "gq0", "gq1", "gm0", "gm1", "gu0",
                 "gu1", "skt", "svf", "s16", "sw0", "sw1", "st", "cc", "pe",
                 "act", "dve", "tp", "sx", "gsp0", "gsp1", "gsp2",
                 "sx0", "sx1", "sx2", "gp1", "so0", "so1", "so2", "so3",
                 "sv0", "sv1", "sv2", "sv3"]:
        sems[name] = stack.enter_context(nc.semaphore(name))
    g16 = Ctr(sems["g16"])
    gx = [Ctr(sems["gx0"]), Ctr(sems["gx1"])]
    gq = [Ctr(sems["gq0"]), Ctr(sems["gq1"])]
    gm = [Ctr(sems["gm0"]), Ctr(sems["gm1"])]
    gu = [Ctr(sems["gu0"]), Ctr(sems["gu1"])]
    skt = Ctr(sems["skt"])
    svf = Ctr(sems["svf"])
    s16 = Ctr(sems["s16"])
    sw = [Ctr(sems["sw0"]), Ctr(sems["sw1"])]
    st = Ctr(sems["st"])
    ccc = Ctr(sems["cc"])
    pe = Ctr(sems["pe"])
    act = Ctr(sems["act"])
    dve = Ctr(sems["dve"])
    tp = Ctr(sems["tp"])
    sx = Ctr(sems["sx"])
    gspl = [Ctr(sems["gsp0"]), Ctr(sems["gsp1"]), Ctr(sems["gsp2"])]
    sxl = [Ctr(sems["sx0"]), Ctr(sems["sx1"]), Ctr(sems["sx2"])]
    gp1 = Ctr(sems["gp1"])
    so = [Ctr(sems["so%d" % i]) for i in range(4)]
    sv = [Ctr(sems["sv%d" % i]) for i in range(4)]
    all_ctrs = [g16, gx[0], gx[1], gq[0], gq[1], gm[0], gm[1], gu[0], gu[1],
                skt, svf, s16, sw[0], sw[1], st, ccc, pe, act, dve, tp, sx,
                gspl[0], gspl[1], gspl[2], sxl[0], sxl[1], sxl[2], gp1,
                so[0], so[1], so[2], so[3], sv[0], sv[1], sv[2], sv[3]]

    M = {}

    def rec(key, val):
        if key in M:
            assert M[key] == val, f"milestone {key} changed: {M[key]} -> {val}"
        M[key] = val

    def tensor_body(t, dry):
        for xt in range(NQT):
            par = xt % 2
            t.wait_ge(gx[par].sem, 16 * (xt // 2 + 1))
            for bk in range(4):
                ti = 4 * xt + bk
                if ti >= 2:
                    t.wait_ge(dve.sem, M.get(("x2tev", ti - 2), 0))
                last = None
                for j in range(8):
                    ct = bk * 8 + j
                    last = t.transpose(bank_slot_bf16(bk % 2, j),
                                       xbf[par][:, ct * P:(ct + 1) * P], idt)
                rec(("x2t", ti), pe.tick(last))
        t.wait_ge(dve.sem, M.get(("x2tev", 31), 0))
        pend_vt = []
        for fi, ft in enumerate(QKV_ORDER):
            t.wait_ge(gq[fi % 2].sem, 16 * (fi // 2 + 1))
            if fi >= 2:
                t.wait_ge(act.sem, M.get(("qkvev", fi - 2), 0))
            bA = 2 + 2 * (fi % 2)
            last = None
            for ct in range(NCT):
                lhs = wqpan[fi % 2][:, ct * P:(ct + 1) * P]
                t.matmul(bank_f32(bA), lhs, XT[:, ct, 0:512],
                         start=ct == 0, stop=ct == NCT - 1)
                last = t.matmul(bank_f32(bA + 1), lhs, XT[:, ct, 512:1024],
                                start=ct == 0, stop=ct == NCT - 1)
            rec(("qkv", fi), pe.tick(last))
            if pend_vt:
                fv, fvi = pend_vt.pop(0)
                dtv = fv - 16
                t.wait_ge(act.sem, M.get(("qkvev", fvi), 0))
                if dtv >= 2:
                    t.wait_ge(dve.sem, M.get(("vtev", dtv - 2), 0))
                lastv = None
                for tt in range(8):
                    lastv = t.transpose(bank_slot_bf16(dtv % 2, tt),
                                        vt_tmp[fvi % 2][:, tt * P:(tt + 1) * P],
                                        idt)
                rec(("vt", dtv), pe.tick(lastv))
            if 16 <= ft < 24:
                pend_vt.append((ft, fi))
        while pend_vt:
            fv, fvi = pend_vt.pop(0)
            dtv = fv - 16
            t.wait_ge(act.sem, M.get(("qkvev", fvi), 0))
            if dtv >= 2:
                t.wait_ge(dve.sem, M.get(("vtev", dtv - 2), 0))
            lastv = None
            for tt in range(8):
                lastv = t.transpose(bank_slot_bf16(dtv % 2, tt),
                                    vt_tmp[fvi % 2][:, tt * P:(tt + 1) * P], idt)
            rec(("vt", dtv), pe.tick(lastv))
        rec("qkv_done", pe.n)
        for qt in range(NQT):
            if qt == 0:
                t.wait_ge(skt.sem, 64)
                t.wait_ge(act.sem, M.get(("qkvev", 23), 0))
                t.wait_ge(dve.sem, M.get("vstage_done", 0))
            else:
                t.wait_ge(act.sem, M.get(("exp", qt - 1, 7), 0))
                t.wait_ge(dve.sem, M.get(("otev", qt - 1), 0))
            for kcg in range(4):
                last = None
                for dt_ in range(NDT):
                    lhs = QT[:, dt_, qt * P:(qt + 1) * P]
                    t.matmul(bank_f32(2 * kcg), lhs,
                             KT[:, dt_, kcg * 1024:kcg * 1024 + 512],
                             start=dt_ == 0, stop=dt_ == NDT - 1)
                    last = t.matmul(bank_f32(2 * kcg + 1), lhs,
                                    KT[:, dt_, kcg * 1024 + 512:(kcg + 1) * 1024],
                                    start=dt_ == 0, stop=dt_ == NDT - 1)
                rec(("scores_p", qt, kcg), pe.tick(last))
            if qt == NQT - 1:
                rec("scores_done", pe.n)
            for rnd in range(4):
                t.wait_ge(act.sem, M.get(("exp", qt, 2 * rnd + 1), 0))
                lastp = None
                for j in range(8):
                    kt = rnd * 8 + j
                    lastp = t.transpose(bank_slot_bf16(rnd, j),
                                        P_sb[:, kt * P:(kt + 1) * P], idt)
                rec(("pt", qt, rnd), pe.tick(lastp))
            t.wait_ge(act.sem, M.get(("exp", qt, 7), 0))
            last = None
            for kt in range(NCT):
                if kt % 8 == 0:
                    t.wait_ge(dve.sem, M.get(("ptev", qt, kt // 8), 0))
                    if qt == 0:
                        t.wait_ge(sv[kt // 8].sem, 16)
                t.matmul(bank_f32(4), PT[:, kt, :], VF[:, kt, 0:512],
                         start=kt == 0, stop=kt == NCT - 1)
                last = t.matmul(bank_f32(5), PT[:, kt, :], VF[:, kt, 512:1024],
                                start=kt == 0, stop=kt == NCT - 1)
            rec(("attnv", qt), pe.tick(last))
            t.wait_ge(dve.sem, M.get(("oev", qt), 0))
            lasto = None
            for dt_ in range(NDT):
                lasto = t.transpose(bank_slot_bf16(6, dt_),
                                    O_bf[:, dt_ * P:(dt_ + 1) * P], idt)
            rec(("ot", qt), pe.tick(lasto))
        rec("att_done", pe.n)
        def emit_x1t(grp):
            for ql, qt in enumerate(grp):
                for bk in range(4):
                    t.wait_ge(dve.sem, M.get(("x1bf_c", qt, bk), 0))
                    ti = 4 * qt + bk
                    if ti >= 2:
                        t.wait_ge(dve.sem, M.get(("x1tev", ti - 2), 0))
                    lastt = None
                    for j in range(8):
                        ct = bk * 8 + j
                        lastt = t.transpose(bank_slot_bf16(6 + bk % 2, j),
                                            x1bf[qt % 2][:, ct * P:(ct + 1) * P],
                                            idt)
                    rec(("x1t", qt, bk), pe.tick(lastt))

        pan_i = 0
        for gi, grp in enumerate(WO_GROUPS):
            L = len(grp)
            for cp in range(4):
                if cp > 0:
                    t.wait_ge(dve.sem, M.get(("zadd", gi, cp - 1), 0))
                elif gi > 0:
                    t.wait_ge(dve.sem, M.get(("zadd", gi - 1, 3), 0))
                for dtp in range(NDT // 2):
                    par = pan_i % 2
                    t.wait_ge(sw[par].sem, 16 * (pan_i // 2 + 1))
                    if dtp == 0:
                        if gi == 0 and cp == 0:
                            t.wait_ge(g16.sem, M.get("bo_ld", 0))
                        for ql in range(len(grp)):
                            for ch in range(2):
                                t.matmul(bank_f32(2 * ql + ch), ones_bf,
                                         bo_bf[0:1, cp * 1024 + ch * 512:
                                               cp * 1024 + (ch + 1) * 512],
                                         start=True, stop=False)
                    lastm = None
                    for j2 in range(2):
                        dt_ = 2 * dtp + j2
                        for ql, qt in enumerate(grp):
                            lhs = OT[:, qt, dt_ * P:(dt_ + 1) * P]
                            t.matmul(bank_f32(2 * ql), lhs,
                                     wopan2[par][:, j2, 0:512],
                                     start=False, stop=dt_ == NDT - 1)
                            lastm = t.matmul(bank_f32(2 * ql + 1), lhs,
                                             wopan2[par][:, j2, 512:1024],
                                             start=False, stop=dt_ == NDT - 1)
                    rec(("wo_dt", pan_i), pe.tick(lastm))
                    pan_i += 1
                rec(("wo", gi, cp), pe.n)
                if cp == 0 and gi > 0:
                    emit_x1t(WO_GROUPS[gi - 1])
        emit_x1t(WO_GROUPS[-1])
        rec("wo_done", pe.n)
        for layer in range(2):
            IN3 = X1T if layer == 0 else B1T
            if layer == 0:
                t.wait_ge(dve.sem, M.get("x1t_all_ev", 0))
                last = None
                for qt in range(NQT):
                    t.wait_ge(dve.sem, M.get(("rs0b", qt), 0))
                    bb = 4 + qt // 4
                    cc = (qt % 4) * P
                    last = t.matmul(
                        bank_row_f32(bb)[0:1, cc:cc + P],
                        rs0b[:, qt % 2:qt % 2 + 1], idt,
                        start=True, stop=True)
            else:
                t.wait_ge(dve.sem, M.get(("ffn_ev_done", 0), 0))
                t.wait_ge(act.sem, M.get(("ffn_act_done", 0), 0))
                last = None
                for ct in range(NCT):
                    t.matmul(bank_row_f32(4), onec, IN3[:, ct, 0:512],
                             start=ct == 0, stop=ct == NCT - 1)
                    last = t.matmul(bank_row_f32(5), onec, IN3[:, ct, 512:1024],
                                    start=ct == 0, stop=ct == NCT - 1)
            rec(("rs", layer), pe.tick(last))
            t.wait_ge(dve.sem, M.get(("rshalf", layer), 0))
            t.matmul(bank_f32(6), oner, rsrow[0:1, 0:512], start=True, stop=True)
            last = t.matmul(bank_f32(7), oner, rsrow[0:1, 512:1024],
                            start=True, stop=True)
            rec(("rsb", layer), pe.tick(last))
            for gp in range(NGT // 2):
                par = gp % 2
                t.wait_ge(gm[par].sem, 16 * (layer * 8 + gp // 2 + 1))
                if gp >= 2:
                    t.wait_ge(dve.sem, M.get(("ffn_tt", layer, 2 * gp - 3, 1), 0))
                elif gp == 1:
                    t.wait_ge(dve.sem, M.get(("rsbev", layer), 0))
                bA = 4 * (gp % 2)
                last = None
                for ct in range(NCT):
                    for j2 in range(2):
                        lhs = mkpan2[par][:, j2, ct * P:(ct + 1) * P]
                        t.matmul(bank_f32(bA + 2 * j2), lhs, IN3[:, ct, 0:512],
                                 start=ct == 0, stop=ct == NCT - 1)
                        last = t.matmul(bank_f32(bA + 2 * j2 + 1), lhs,
                                        IN3[:, ct, 512:1024],
                                        start=ct == 0, stop=ct == NCT - 1)
                rec(("ffn_mm", layer, gp), pe.tick(last))
            rec(("ffn_mm_done", layer), pe.n)
        t.wait_ge(dve.sem, M.get(("ffn_ev_done", 1), 0))
        t.wait_ge(act.sem, M.get(("ffn_act_done", 1), 0))
        last = None
        for gt in range(NGT):
            t.matmul(bank_row_f32(4), onec, B2T[:, gt, 0:512],
                     start=gt == 0, stop=gt == NGT - 1)
            last = t.matmul(bank_row_f32(5), onec, B2T[:, gt, 512:1024],
                            start=gt == 0, stop=gt == NGT - 1)
        rec("rs2", pe.tick(last))
        t.wait_ge(dve.sem, M.get("rs2row_ev", 0))
        for qt in range(NQT):
            if qt > 0:
                t.wait_ge(dve.sem, M.get(("rs2col_ev", qt - 1), 0))
            lastc = t.matmul(bank_f32(6, cols=1),
                             rsrow[0:1, qt * P:(qt + 1) * P],
                             oner[0:1, 0:1], start=True, stop=True)
            rec(("rs2col", qt), pe.tick(lastc))
        pan2 = 0
        for q4 in range(2):
            for cp in range(4):
                for gt in range(NGT):
                    pr = pan2 % 2
                    j2 = gt % 2
                    if j2 == 0:
                        t.wait_ge(gu[pr].sem, 16 * (pan2 // 2 + 1))
                    if gt == 0 and q4 == 0 and cp == 0:
                        t.wait_ge(dve.sem, M.get("bias2_ev", 0))
                    last = None
                    for ql in range(4):
                        qt = q4 * 4 + ql
                        if gt == 0 and (q4 > 0 or cp > 0):
                            pq4, pcp = (q4, cp - 1) if cp > 0 else (q4 - 1, 3)
                            t.wait_ge(act.sem, M.get(("f2act", pq4, pcp, ql), 0))
                        t.matmul(bank_f32(2 * ql),
                                 B2T[:, gt, qt * P:(qt + 1) * P],
                                 u2pan2[pr][:, j2, 0:512],
                                 start=gt == 0, stop=gt == NGT - 1)
                        last = t.matmul(bank_f32(2 * ql + 1),
                                        B2T[:, gt, qt * P:(qt + 1) * P],
                                        u2pan2[pr][:, j2, 512:1024],
                                        start=gt == 0, stop=gt == NGT - 1)
                    rec(("f2g", pan2, j2), pe.tick(last))
                    if j2 == 1:
                        pan2 += 1
                rec(("f2mm", q4, cp), pe.n)
        rec("pe_end", pe.n)

    def gpsimd_body(g, dry):
        for ap_out, ap_in in [(idt, ident_in[:, :]), (bqkv, bqkv_in[:, :]),
                              (onec, onec_in[:, :]), (oner, oner_in[:, :]),
                              (dinh, dinh_in[:, :]), (bsel_sb, bsel_in[:, :])]:
            g16.tick(g.dma_start(out=ap_out, in_=ap_in), 16)
        rec("consts_ld", g16.n)
        g16.tick(g.dma_start(out=wobf[:, :], in_=wo_in[:, :]), 16)
        rec("wobf_ld", g16.n)
        for xt in range(NQT):
            if xt >= 2:
                g.wait_ge(pe.sem, M.get(("x2t", 4 * (xt - 2) + 3), 0))
            gx[xt % 2].tick(g.dma_start(out=xbf[xt % 2],
                                        in_=x_in[xt * P:(xt + 1) * P, :]), 16)
        def gather(gg, b, count, kv, ranks=range(4)):
            base = b * 4 * 1024
            if kv == 0:
                for r in ranks:
                    ins = gg.dma_start(
                        out=KT[:, :, r * 1024:(r + 1) * 1024],
                        in_=cc_out_k[base + r * 1024: base + (r + 1) * 1024, :]
                        .rearrange("(dt p) q -> p dt q", p=P))
                    if count:
                        skt.tick(ins, 16)
                    else:
                        ins.then_inc(skt.sem, 16)
            else:
                for r in ranks:
                    ins = gg.dma_start(
                        out=VF[:, r * 8:(r + 1) * 8, :],
                        in_=cc_out_v[base + r * 1024: base + (r + 1) * 1024, :]
                        .rearrange("(tt p) d -> p tt d", p=P))
                    if count:
                        sv[r].tick(ins, 16)
                    else:
                        ins.then_inc(sv[r].sem, 16)

        def branch_gather(kv, ranks, tag):
            if dry:
                gather(g, 0, True, kv, ranks)
            else:
                with g.register("rb%s" % tag) as rb:
                    g.reg_load(rb, bsel_sb[0:1, 0:1])
                    with g.If_eq(rb, 0):
                        gather(g, 0, True, kv, ranks)
                    with g.If_eq(rb, 1):
                        gather(g, 1, False, kv, ranks)

        for fi, ft in enumerate(QKV_ORDER):
            if fi >= 2:
                g.wait_ge(pe.sem, M.get(("qkv", fi - 2), 0))
            gq[fi % 2].tick(
                g.dma_start(out=wqpan[fi % 2],
                            in_=wq_blk[ft].rearrange("p a b -> p (a b)")), 16)
            if fi == 11:
                g.wait_ge(act.sem, M.get("actK_done", 0))
                g16.tick(g.dma_start(
                    out=cc_in_k[:, :].rearrange("(dt p) q -> p dt q", p=P),
                    in_=kvK[:, :, :]), 16)
                g.wait_ge(g16.sem, g16.n)
                ccc.tick(g.collective_compute(
                    "AllGather", OP.bypass, replica_groups=[list(range(8))],
                    ins=[cc_in_k[:, :]], outs=[cc_out_k[:, :]]))
            if fi == 17:
                g.wait_ge(dve.sem, M.get("vstage_done", 0))
                g16.tick(g.dma_start(
                    out=cc_in_v[:, :].rearrange("(tt p) d -> p tt d", p=P),
                    in_=vstage[:, :, :]), 16)
                g.wait_ge(g16.sem, g16.n)
                ccc.tick(g.collective_compute(
                    "AllGather", OP.bypass, replica_groups=[list(range(8))],
                    ins=[cc_in_v[:, :]], outs=[cc_out_v[:, :]]))
            if 18 <= fi <= 21:
                if fi == 18:
                    g.wait_ge(ccc.sem, 1)
                branch_gather(0, [fi - 18], "k%d" % (fi - 18))
        g.wait_ge(ccc.sem, 2)
        g.wait_ge(pe.sem, M.get("qkv_done", 0))
        branch_gather(1, range(4), "v")
        g16.tick(g.dma_start(out=bo_bf, in_=bo_in[0:1, :]), 16)
        g16.tick(g.dma_start(out=ones_bf, in_=oner_in[:, :]), 16)
        rec("bo_ld", g16.n)
        for gi, grp in enumerate(WO_GROUPS):
            for ql, qt in enumerate(grp):
                sl = WO_SLOTS[gi][ql]
                if gi == 0 and ql == 0:
                    g.wait_ge(pe.sem, M.get("scores_done", 0))
                pq = PREV_ON_SLOT[qt]
                if pq is not None:
                    g.wait_ge(act.sem, M.get(("part1", pq, 3), 0))
                    g.wait_ge(dve.sem, M.get(("x1bf_t", pq), 0))
                sxl[sl].tick(g.dma_start(out=xg[sl],
                                         in_=x_in[qt * P:(qt + 1) * P, :]), 16)
                rec(("xload", qt), sxl[sl].n)

        for layer in range(2):
            kblk = k0_blk if layer == 0 else k1_blk
            for gp in range(NGT // 2):
                li = layer * (NGT // 2) + gp
                par = gp % 2
                if li >= 2:
                    pl, pg = divmod(li - 2, NGT // 2)
                    g.wait_ge(pe.sem, M.get(("ffn_mm", pl, pg), 0))
                else:
                    g.wait_ge(pe.sem, M.get("wo_done", 0))
                gm[par].tick(g.dma_start(
                    out=mkpan2[par][:, :, :],
                    in_=kblk[2 * gp:2 * gp + 2].rearrange(
                        "a p c j -> p a (c j)")), 16)
        pan2 = 0
        for q4 in range(2):
            for cp in range(4):
                for g2 in range(NGT // 2):
                    par = pan2 % 2
                    if pan2 >= 2:
                        g.wait_ge(pe.sem, M.get(("f2g", pan2 - 2, 1), 0))
                    else:
                        g.wait_ge(pe.sem, M.get(("ffn_mm", 1, 14), 0))
                    gu[par].tick(g.dma_start(
                        out=u2pan2[par][:, :, :],
                        in_=k2_in[g2 * 256:(g2 + 1) * 256,
                                  cp * 1024:(cp + 1) * 1024].rearrange(
                                      "(a p) c -> p a c", p=P)), 16)
                    pan2 += 1

    def sync_body(sy, dry):
        pan_i = 0
        first_wopan = True
        for gi, grp in enumerate(WO_GROUPS):
            for cp in range(4):
                for dtp in range(NDT // 2):
                    par = pan_i % 2
                    if first_wopan:
                        sy.wait_ge(g16.sem, M.get("wobf_ld", 0))
                        sy.wait_ge(pe.sem, M.get("scores_done", 0))
                        first_wopan = False
                    if pan_i >= 2:
                        sy.wait_ge(pe.sem, M.get(("wo_dt", pan_i - 2), 0))
                    sw[par].tick(sy.dma_start(
                        out=wopan2[par][:, :, :],
                        in_=wobf[dtp * 256:(dtp + 1) * 256,
                                 cp * 1024:(cp + 1) * 1024].rearrange(
                                     "(a p) c -> p a c", p=P)), 16)
                    pan_i += 1
            if gi > 0:
                for ql, qt in enumerate(WO_GROUPS[gi - 1]):
                    sl = WO_SLOTS[gi - 1][ql]
                    sy.wait_ge(dve.sem, M.get(("x1bf_c", qt, 3), 0))
                    gspl[sl].tick(sy.dma_start(
                        out=x1sp[qt * P:(qt + 1) * P, :],
                        in_=x1bf[qt % 2]), 16)
                    rec(("spill", qt), gspl[sl].n)
        for ql, qt in enumerate(WO_GROUPS[-1]):
            sl = WO_SLOTS[-1][ql]
            sy.wait_ge(dve.sem, M.get(("x1bf_c", qt, 3), 0))
            gspl[sl].tick(sy.dma_start(out=x1sp[qt * P:(qt + 1) * P, :],
                                       in_=x1bf[qt % 2]), 16)
            rec(("spill", qt), gspl[sl].n)
        for _g in range(3):
            rec(("spill_last", _g), gspl[_g].n)
        rec("wopan_done", s16.n)
        for q4 in range(2):
            for ql in (0, 1):
                qt = q4 * 4 + ql
                if q4 > 0:
                    sy.wait_ge(so[ql].sem, 16 * q4)
                else:
                    sy.wait_ge(pe.sem, M.get(("ffn_mm_done", 1), 0))
                    for _g in range(3):
                        sy.wait_ge(gspl[_g].sem, M.get(("spill_last", _g), 0))
                s16.tick(sy.dma_start(out=x1rel[ql],
                                      in_=x1sp[qt * P:(qt + 1) * P, :]), 16)
            rec(("x1rel_q4y", q4), s16.n)
            for ql in (0, 2):
                qt = q4 * 4 + ql
                sy.wait_ge(dve.sem, M.get(("out_ready", qt), 0))
                so[ql].tick(sy.dma_start(out=out_dr[qt * P:(qt + 1) * P, :],
                                         in_=z2g[ql]), 16)
        for ql in range(4):
            sy.wait_ge(so[ql].sem, so[ql].n)

    def scalar_body(s, dry):
        for fi, ft in enumerate(QKV_ORDER):
            s.wait_ge(pe.sem, M.get(("qkv", fi), 0))
            bA = 2 + 2 * (fi % 2)
            if 8 <= ft < 16:
                d0 = kvK[:, ft - 8, :]
            elif ft >= 16:
                if ft >= 18:
                    s.wait_ge(pe.sem, M.get(("vt", ft - 18), 0))
                d0 = vt_tmp[fi % 2][:, :]
            else:
                d0 = QT[:, ft, :]
            s.activation(d0[:, 0:512], bank_f32(bA), AF.Identity,
                         bias=bqkv[:, ft:ft + 1], scale=np.float32(1.0))
            a2 = s.activation(d0[:, 512:1024], bank_f32(bA + 1), AF.Identity,
                              bias=bqkv[:, ft:ft + 1], scale=np.float32(1.0))
            rec(("qkvev", fi), act.tick(a2))
            if fi == 7:
                rec("actK_done", act.n)
        if debug:
            s.wait_ge(dve.sem, M.get("vstage_done", 0))
            tp.tick(s.dma_start(out=taps["tap_qt"][:, :],
                                in_=QT.rearrange("p a b -> p (a b)")), 16)
            tp.tick(s.dma_start(out=taps["tap_kv"][:, :],
                                in_=kvK.rearrange("p a b -> p (a b)")), 16)
            tp.tick(s.dma_start(out=taps["tap_vs"][:, :],
                                in_=vstage.rearrange("p a b -> p (a b)")), 16)
            s.wait_ge(dve.sem, M.get(("x2tev", 31), 0))
            tp.tick(s.dma_start(out=taps["tap_xt"][:, :],
                                in_=XT.rearrange("p a b -> p (a b)")), 16)
            s.wait_ge(skt.sem, 64)
            s.wait_ge(svf.sem, 64)
            tp.tick(s.dma_start(out=taps["tap_kt"][:, :],
                                in_=KT.rearrange("p a b -> p (a b)")), 16)
            tp.tick(s.dma_start(out=taps["tap_vf"][:, :],
                                in_=VF.rearrange("p a b -> p (a b)")), 16)
        for qt in range(NQT):
            s.wait_ge(dve.sem, M.get(("negm_t", qt), 0))
            for j in range(8):
                a = s.activation(P_sb[:, j * 512:(j + 1) * 512], bank_f32(j),
                                 AF.Exp, bias=negm[:, 0:1], scale=np.float32(SCALE_ATT),
                                 accum_out=sep[:, j:j + 1])
                rec(("exp", qt, j), act.tick(a))
        if debug:
            tp.tick(s.dma_start(out=taps["tap_p"][:, :], in_=P_sb), 16)
            s.wait_ge(dve.sem, M.get(("otev", 7), 0))
            tp.tick(s.dma_start(out=taps["tap_ot"][:, :],
                                in_=OT.rearrange("p a b -> p (a b)")), 16)
        for gi, grp in enumerate(WO_GROUPS):
            for cp in range(4):
                s.wait_ge(dve.sem, M.get(("zadd", gi, cp), 0))
                for ql, qt in enumerate(grp):
                    sl = WO_SLOTS[gi][ql]
                    col = ql * 4 + cp
                    sli = zg[sl][:, cp * 1024:(cp + 1) * 1024]
                    a = s.activation(sq_dump1[:, 0:1024], sli, AF.Copy,
                                     bias=0.0, scale=np.float32(1.0),
                                     accum_out=msp[:, col:col + 1])
                    rec(("part1", qt, cp), act.tick(a))
            for ql, qt in enumerate(grp):
                s.wait_ge(dve.sem, M.get(("x1bf_t", qt), 0))
                a = s.activation(sq_dump1, x1bf[qt % 2], AF.Copy, bias=0.0,
                                 scale=np.float32(1.0),
                                 accum_out=rs0c[:, qt % 2:qt % 2 + 1])
                rec(("rs0a", qt), act.tick(a))
        if debug:
            for _g in range(3):
                s.wait_ge(gspl[_g].sem, M.get(("spill_last", _g), 0))
            tp.tick(s.dma_start(out=taps["tap_x1"][:, :], in_=x1sp[:, :]), 16)
        for layer in range(2):
            s_lay = s0 if layer == 0 else s1
            OUT3 = B1T if layer == 0 else B2T
            first = True
            for gt in range(NGT):
                if gt > BT_GT:
                    continue
                for ch in range(2):
                    if first:
                        for _g in range(3):
                            s.wait_ge(gspl[_g].sem,
                                      M.get(("spill_last", _g), 0))
                        first = False
                    s.wait_ge(dve.sem, M.get(("ffn_tt", layer, gt, ch), 0))
                    a = s.activation(OUT3[:, gt, ch * 512:(ch + 1) * 512],
                                     tmp_ffn[2 * (gt % 2) + ch], AF.Relu,
                                     bias=np.float32(0.0), scale=np.float32(4.0 * s_lay))
                    rec(("ffn_relu", layer, gt, ch), act.tick(a))
            rec(("ffn_act_done", layer), act.n)
        if debug:
            s.wait_ge(dve.sem, M.get(("ffn_ev_done", 0), 0))
            tp.tick(s.dma_start(out=taps["tap_b1"][:, :],
                                in_=B1T.rearrange("p a b -> p (a b)")), 16)
            s.wait_ge(dve.sem, M.get(("ffn_ev_done", 1), 0))
            tp.tick(s.dma_start(out=taps["tap_b2"][:, :],
                                in_=B2T.rearrange("p a b -> p (a b)")), 16)
            s.wait_ge(dve.sem, M.get("rs2row_ev", 0))
            tp.tick(s.dma_start(out=taps["tap_rs"][:, :], in_=rsrow), 16)
        for q4 in range(2):
            for ql in (2, 3):
                qt = q4 * 4 + ql
                if q4 > 0:
                    s.wait_ge(so[ql].sem, 16 * q4)
                else:
                    s.wait_ge(pe.sem, M.get(("ffn_mm_done", 1), 0))
                    for _g in range(3):
                        s.wait_ge(gspl[_g].sem, M.get(("spill_last", _g), 0))
                if q4 == 0:
                    s.wait_ge(dve.sem, M.get(("ffn_ev_done", 1), 0))
                sx.tick(s.dma_start(out=x1rel[ql],
                                    in_=x1sp[qt * P:(qt + 1) * P, :]), 16)
            rec(("x1rel_q4s", q4), sx.n)
            for cp in range(4):
                for ql in range(4):
                    qt = q4 * 4 + ql
                    fl = (q4 * 4 + cp) * 4 + ql
                    s.wait_ge(pe.sem, M.get(("f2mm", q4, cp), 0))
                    s.wait_ge(dve.sem, M.get("bias2_ev", 0))
                    if fl >= 2:
                        s.wait_ge(dve.sem, M.get(("f2tt", fl - 2), 0))
                    a = s.activation(ffn_t[ql % 2], bank_f32(2 * ql, cols=1024),
                                     AF.Identity, bias=bias2[:, qt:qt + 1],
                                     scale=np.float32(2.0 * s2))
                    rec(("f2act", q4, cp, ql), act.tick(a))
                for ql in range(4):
                    qt = q4 * 4 + ql
                    fl = (q4 * 4 + cp) * 4 + ql
                    col = ql * 4 + cp
                    s.wait_ge(dve.sem, M.get(("f2tt", fl), 0))
                    sli = z2g[ql][:, cp * 1024:(cp + 1) * 1024]
                    a = s.activation(sq_dump2[:, 0:1024], sli, AF.Copy,
                                     bias=0.0, scale=np.float32(1.0),
                                     accum_out=msp[:, col:col + 1])
                    act.tick(a)
                    a = s.activation(sq_dump2[:, 0:1024], sli, AF.Square,
                                     accum_out=sqp[:, col:col + 1])
                    rec(("part2", qt, cp), act.tick(a))
            for ql in range(4):
                qt = q4 * 4 + ql
                s.wait_ge(dve.sem, M.get(("var2_t", qt), 0))
                a0 = s.activation(var_sb, var_sb, AF.Ln, bias=np.float32(EPS), scale=np.float32(1.0))
                n_ = act.tick(a0)
                s.wait_ge(act.sem, n_)
                a = s.activation(rstd, var_sb, AF.Exp, bias=np.float32(0.0), scale=np.float32(-0.5))
                rec(("rstd2", qt), act.tick(a))
            for ql in (1, 3):
                qt = q4 * 4 + ql
                s.wait_ge(dve.sem, M.get(("out_ready", qt), 0))
                so[ql].tick(s.dma_start(out=out_dr[qt * P:(qt + 1) * P, :],
                                        in_=z2g[ql]), 16)
        if debug:
            s.wait_ge(tp.sem, tp.n)

    def vector_body(v, dry):
        def vs(ins):
            n = dve.tick(ins)
            v.wait_ge(dve.sem, n)
        for xt in range(NQT):
            for bk in range(4):
                ti = 4 * xt + bk
                v.wait_ge(pe.sem, M.get(("x2t", ti), 0))
                c = v.tensor_copy(XT[:, bk * 8:(bk + 1) * 8, xt * P:(xt + 1) * P],
                                  bank3_bf16(bk % 2, 8, P))
                rec(("x2tev", ti), dve.tick(c))
        for dtv in range(8):
            v.wait_ge(pe.sem, M.get(("vt", dtv), 0))
            c = v.tensor_copy(vstage[:, :, dtv * P:(dtv + 1) * P],
                              bank3_bf16(dtv % 2, 8, P))
            rec(("vtev", dtv), dve.tick(c))
        rec("vstage_done", dve.n)
        for qt in range(NQT):
            for kcg in range(4):
                v.wait_ge(pe.sem, M.get(("scores_p", qt, kcg), 0))
                r_ = v.tensor_reduce(maxp[:, kcg:kcg + 1],
                                     bank_f32(2 * kcg, cols=1024), AX.X, OP.max)
                if kcg == 3:
                    vs(r_)
            vs(v.tensor_reduce(negm, maxp[:, 0:4], AX.X, OP.max))
            c = v.tensor_scalar(negm, negm, float(-SCALE_ATT), None,
                                op0=OP.mult)
            rec(("negm_t", qt), dve.tick(c))
            for rnd in range(4):
                v.wait_ge(pe.sem, M.get(("pt", qt, rnd), 0))
                c = v.tensor_copy(PT[:, rnd * 8:(rnd + 1) * 8, :],
                                  bank3_bf16(rnd, 8, P))
                rec(("ptev", qt, rnd), dve.tick(c))
            v.wait_ge(act.sem, M.get(("exp", qt, 7), 0))
            vs(v.tensor_reduce(sumexp, sep[:, 0:8], AX.X, OP.add))
            vs(v.reciprocal(recip, sumexp))
            v.wait_ge(pe.sem, M.get(("attnv", qt), 0))
            v.tensor_scalar(O_bf[:, 0:512], bank_f32(4), recip[:, 0:1], None,
                            op0=OP.mult)
            c = v.tensor_scalar(O_bf[:, 512:1024], bank_f32(5), recip[:, 0:1],
                                None, op0=OP.mult)
            rec(("oev", qt), dve.tick(c))
            v.wait_ge(pe.sem, M.get(("ot", qt), 0))
            c = v.tensor_copy(
                OT[:, qt, :],
                bank3_bf16(6, 8, P).rearrange("p a b -> p (a b)"))
            rec(("otev", qt), dve.tick(c))
        ti_ev = 0

        def emit_x1tev(grp):
            nonlocal_ti = []
            for ql, qt in enumerate(grp):
                for bk in range(4):
                    ti = 4 * qt + bk
                    v.wait_ge(pe.sem, M.get(("x1t", qt, bk), 0))
                    c = v.tensor_copy(
                        X1T[:, bk * 8:(bk + 1) * 8, qt * P:(qt + 1) * P],
                        bank3_bf16(6 + bk % 2, 8, P))
                    rec(("x1tev", ti), dve.tick(c))

        for gi, grp in enumerate(WO_GROUPS):
            for ql, qt in enumerate(grp):
                sl = WO_SLOTS[gi][ql]
                v.wait_ge(sxl[sl].sem, M.get(("xload", qt), 0))
            for cp in range(4):
                v.wait_ge(pe.sem, M.get(("wo", gi, cp), 0))
                cnt = None
                for ql, qt in enumerate(grp):
                    sl = WO_SLOTS[gi][ql]
                    cnt = v.tensor_add(zg[sl][:, cp * 1024:(cp + 1) * 1024],
                                       xg[sl][:, cp * 1024:(cp + 1) * 1024],
                                       bank_f32(2 * ql, cols=1024))
                rec(("zadd", gi, cp), dve.tick(cnt))
                if cp == 0 and gi > 0:
                    emit_x1tev(WO_GROUPS[gi - 1])
            for ql, qt in enumerate(grp):
                sl = WO_SLOTS[gi][ql]
                v.wait_ge(act.sem, M.get(("part1", qt, 3), 0))
                vs(v.tensor_reduce(msum, msp[:, 4 * ql:4 * ql + 4], AX.X,
                                   OP.add))
                vs(v.tensor_scalar(m_sb, msum, 1.0 / 4096.0, None, op0=OP.mult))
                if qt >= 2:
                    v.wait_ge(pe.sem, M.get(("x1t", qt - 2, 3), 0))
                if qt >= 2:
                    pg2, pl2 = divmod(qt - 2, 2)
                    psl2 = WO_SLOTS[pg2][pl2]
                    v.wait_ge(gspl[psl2].sem, M.get(("spill", qt - 2), 0))
                for bk in range(4):
                    c = v.tensor_scalar(
                        x1bf[qt % 2][:, bk * 1024:(bk + 1) * 1024],
                        zg[sl][:, bk * 1024:(bk + 1) * 1024], m_sb[:, 0:1],
                        None, op0=OP.subtract)
                    rec(("x1bf_c", qt, bk), dve.tick(c))
                rec(("x1bf_t", qt), dve.n)
            for ql, qt in enumerate(grp):
                v.wait_ge(act.sem, M.get(("rs0a", qt), 0))
                c = v.tensor_copy(rs0b[:, qt % 2:qt % 2 + 1],
                                  rs0c[:, qt % 2:qt % 2 + 1])
                rec(("rs0b", qt), dve.tick(c))
        emit_x1tev(WO_GROUPS[-1])
        rec("x1t_all_ev", dve.n)
        for layer in range(2):
            s_lay = s0 if layer == 0 else s1
            OUT3 = B1T if layer == 0 else B2T
            v.wait_ge(pe.sem, M.get(("rs", layer), 0))
            v.tensor_scalar(rsrow[0:1, 0:512], bank_row_f32(4), 0.5, None,
                            op0=OP.mult)
            c = v.tensor_scalar(rsrow[0:1, 512:1024], bank_row_f32(5), 0.5,
                                None, op0=OP.mult)
            rec(("rshalf", layer), dve.tick(c))
            v.wait_ge(pe.sem, M.get(("rsb", layer), 0))
            vs(v.tensor_copy(rsb_sb, bank_f32(6, cols=1024)))
            rec(("rsbev", layer), dve.n)
            for gp in range(NGT // 2):
                v.wait_ge(pe.sem, M.get(("ffn_mm", layer, gp), 0))
                bA = 4 * (gp % 2)
                for j2 in range(2):
                    gt = 2 * gp + j2
                    for ch in range(2):
                        tb = tmp_ffn[2 * (gt % 2) + ch]
                        if gt >= 2 and gt - 2 <= BT_GT:
                            v.wait_ge(act.sem,
                                      M.get(("ffn_relu", layer, gt - 2, ch), 0))
                        c = v.tensor_sub(tb, bank_f32(bA + 2 * j2 + ch),
                                         rsb_sb[:, ch * 512:(ch + 1) * 512])
                        rec(("ffn_tt", layer, gt, ch), dve.tick(c))
                    if gt == BT_GT:
                        for ch in range(2):
                            v.wait_ge(act.sem,
                                      M.get(("ffn_relu", layer, gt, ch), 0))
                            c = v.tensor_scalar(
                                OUT3[:, gt, ch * 512:(ch + 1) * 512],
                                OUT3[:, gt, ch * 512:(ch + 1) * 512],
                                dinh[:, 0:1], None, op0=OP.mult)
                            dve.tick(c)
                    elif gt > BT_GT:
                        v.wait_ge(dve.sem, M.get(("ffn_tt", layer, gt, 1), 0))
                        for ch in range(2):
                            c = v.tensor_scalar(
                                OUT3[:, gt, ch * 512:(ch + 1) * 512],
                                tmp_ffn[2 * (gt % 2) + ch],
                                float(-16.0 * s_lay), 0.0,
                                op0=OP.mult, op1=OP.min)
                            dve.tick(c)
            rec(("ffn_ev_done", layer), dve.n)
        v.wait_ge(pe.sem, M.get("rs2", 0))
        v.tensor_copy(rsrow[0:1, 0:512], bank_row_f32(4))
        c = v.tensor_copy(rsrow[0:1, 512:1024], bank_row_f32(5))
        rec("rs2row_ev", dve.tick(c))
        for qt in range(NQT):
            v.wait_ge(pe.sem, M.get(("rs2col", qt), 0))
            c = v.tensor_scalar(bias2[:, qt:qt + 1], bank_f32(6, cols=1),
                                float(-s2), None, op0=OP.mult)
            rec(("rs2col_ev", qt), dve.tick(c))
        rec("bias2_ev", dve.n)
        for q4 in range(2):
            for cp in range(4):
                for ql in range(4):
                    qt = q4 * 4 + ql
                    fl = (q4 * 4 + cp) * 4 + ql
                    v.wait_ge(act.sem, M.get(("f2act", q4, cp, ql), 0))
                    v.wait_ge(sx.sem, M.get(("x1rel_q4s", q4), 0))
                    v.wait_ge(s16.sem, M.get(("x1rel_q4y", q4), 0))
                    c = v.tensor_add(z2g[ql][:, cp * 1024:(cp + 1) * 1024],
                                     x1rel[ql][:, cp * 1024:(cp + 1) * 1024],
                                     ffn_t[ql % 2])
                    rec(("f2tt", fl), dve.tick(c))
            for ql in range(4):
                qt = q4 * 4 + ql
                v.wait_ge(act.sem, M.get(("part2", qt, 3), 0))
                vs(v.tensor_reduce(msum, msp[:, 4 * ql:4 * ql + 4], AX.X,
                                   OP.add))
                vs(v.tensor_reduce(sqs, sqp[:, 4 * ql:4 * ql + 4], AX.X,
                                   OP.add))
                vs(v.tensor_scalar(m_sb, msum, 1.0 / 4096.0, None, op0=OP.mult))
                vs(v.tensor_mul(m2_sb, m_sb, m_sb))
                if qt > 0:
                    v.wait_ge(act.sem, M.get(("rstd2", qt - 1), 0))
                c = v.tensor_scalar(var_sb, sqs, 1.0 / 4096.0, m2_sb[:, 0:1],
                                    op0=OP.mult, op1=OP.subtract)
                rec(("var2_t", qt), dve.tick(c))
                v.wait_ge(act.sem, M.get(("rstd2", qt), 0))
                c = v.tensor_scalar(z2g[ql], z2g[ql], m_sb[:, 0:1], rstd[:, 0:1],
                                    op0=OP.subtract, op1=OP.mult)
                rec(("out_ready", qt), dve.tick(c))

    bodies = [("tensor", tensor_body), ("gpsimd", gpsimd_body),
              ("sync", sync_body), ("scalar", scalar_body),
              ("vector", vector_body)]

    for _ in range(2):
        for c in all_ctrs:
            c.reset()
        dry_eng = _DryEng()
        for _, body in bodies:
            body(dry_eng, True)
    totals = {id(c): c.n for c in all_ctrs}
    for c in all_ctrs:
        c.reset()
    streams = {}
    for nm, body in bodies:
        streams[nm] = []
        body(_RecEng(streams[nm]), True)
    semnames = {id(c.sem): n for n, c in
                [("g16", g16), ("gx0", gx[0]), ("gx1", gx[1]),
                 ("gq0", gq[0]), ("gq1", gq[1]), ("gm0", gm[0]),
                 ("gm1", gm[1]), ("gu0", gu[0]), ("gu1", gu[1]),
                 ("skt", skt), ("svf", svf), ("s16", s16), ("sw0", sw[0]),
                 ("sw1", sw[1]), ("st", st), ("cc", ccc), ("pe", pe),
                 ("act", act), ("dve", dve), ("tp", tp), ("sx", sx),
                 ("gsp0", gspl[0]), ("gsp1", gspl[1]), ("gsp2", gspl[2]),
                 ("sx0", sxl[0]), ("sx1", sxl[1]), ("sx2", sxl[2]),
                 ("gp1", gp1), ("so0", so[0]), ("so1", so[1]),
                 ("so2", so[2]), ("so3", so[3]), ("sv0", sv[0]),
                 ("sv1", sv[1]), ("sv2", sv[2]), ("sv3", sv[3])]}
    _simulate(streams, semnames)
    for c in all_ctrs:
        c.reset()
    with nc.Block("main", no_gpsimd_drain=True) as blk:
        blk.tensor(lambda e: tensor_body(e, False))
        blk.gpsimd(lambda e: gpsimd_body(e, False))
        blk.sync(lambda e: sync_body(e, False))
        blk.scalar(lambda e: scalar_body(e, False))
        blk.vector(lambda e: vector_body(e, False))
    for c in all_ctrs:
        assert totals[id(c)] == c.n, "emission mismatch vs dry run"
    stack.close()
    return nc


def kernel(L0_attn_x, L0_attn_Wqkv, L0_attn_bqkv, L0_attn_Wo, L0_attn_bo,
           L0_ffn_k0, L0_ffn_s0, L0_ffn_k1, L0_ffn_s1, L0_ffn_k2, L0_ffn_s2):
    x = np.asarray(L0_attn_x, dtype=np.float32)
    wq = np.ascontiguousarray(np.asarray(L0_attn_Wqkv, dtype=np.float32))
    bqkv = np.asarray(L0_attn_bqkv, dtype=np.float32)
    wo = np.ascontiguousarray(np.asarray(L0_attn_Wo, dtype=np.float32))
    bo = np.asarray(L0_attn_bo, dtype=np.float32)
    k0 = np.asarray(L0_ffn_k0)
    k1 = np.asarray(L0_ffn_k1)
    k2 = np.asarray(L0_ffn_k2)
    s0 = float(np.asarray(L0_ffn_s0))
    s1 = float(np.asarray(L0_ffn_s1))
    s2 = float(np.asarray(L0_ffn_s2))

    def blk4(w, nft):
        return np.ascontiguousarray(
            w.reshape(NCT, P, nft, P).transpose(2, 1, 0, 3))

    wq_blkA = blk4(wq, NFT)
    k0_blkA = blk4(np.ascontiguousarray(k0).astype(np.uint8), NGT)
    k1_blkA = blk4(np.ascontiguousarray(k1).astype(np.uint8), NGT)
    k2_u8 = np.ascontiguousarray(np.asarray(k2).astype(np.uint8))
    bqkv_col = np.ascontiguousarray(bqkv.reshape(NFT, P).T)
    bo_repA = np.ascontiguousarray(np.broadcast_to(bo, (P, 4096)))
    ident = np.eye(P, dtype=ml_dtypes.bfloat16)
    ones_col = np.ones((P, 1), dtype=ml_dtypes.bfloat16)
    ones_row = np.ones((1, P), dtype=np.float32)
    dinh_v = np.where(np.arange(P)[:, None] < BT_P, 1.0, -4.0).astype(np.float32)

    nc = build_nc(s0, s1, s2)
    in_maps = []
    for c in range(8):
        b, qs = c // 4, (c % 4) * 1024
        in_maps.append({
            "x": np.ascontiguousarray(x[b, qs:qs + 1024, :]),
            "wq_blk": wq_blkA, "wo": wo,
            "k0_blk": k0_blkA, "k1_blk": k1_blkA, "k2": k2_u8,
            "bqkv_col": bqkv_col, "bo_rep": bo_repA,
            "ident": ident, "ones_col": ones_col, "ones_row": ones_row,
            "dinh": dinh_v,
            "bsel": np.array([[b]], dtype=np.int32),
        })
    res = run_bass_kernel_spmd(nc, in_maps, core_ids=list(range(8)))
    out = np.empty((2, 4096, 4096), dtype=np.float32)
    for c in range(8):
        b, qs = c // 4, (c % 4) * 1024
        out[b, qs:qs + 1024, :] = res.results[c]["out"]
    return out


# version 21
# speedup vs baseline: 1.0218x; 1.0218x over previous
import numpy as np
import ml_dtypes
import concourse.bass as bass
import concourse.mybir as mybir
from concourse.bass_utils import run_bass_kernel_spmd

F32 = mybir.dt.float32
BF16 = mybir.dt.bfloat16
U8 = mybir.dt.uint8
I32 = mybir.dt.int32
AF = mybir.ActivationFunctionType
OP = mybir.AluOpType
AX = mybir.AxisListType

P = 128
NQT = 8
NCT = 32
NDT = 8
NFT = 24
NGT = 32
SCALE_ATT = 0.03125
EPS = 1e-5
BT_GT = 25
BT_P = 77
KB = 1024

OFF_IDENT = 0
OFF_BQKV = 256
OFF_ONEC = 384
OFF_ONER = 416
OFF_DINH = 928
OFF_BSEL = 960
OFF_MAXP = 992
OFF_NEGM = 1024
OFF_SEP = 1056
OFF_SE = 1088
OFF_RECIP = 1120
OFF_MSUM = 1152
OFF_M = 1184
OFF_M2 = 1216
OFF_VAR = 1248
OFF_RSTD = 1280
OFF_SQS = 1312
OFF_RS0 = 1312
OFF_MSP = 1344
OFF_RSROW = 1472
OFF_BIAS2 = 5568
OFF_SQP = 5696
CONSTS_END = 6144
R0 = CONSTS_END
R1 = R0 + 64 * KB
R2 = R1 + 64 * KB
R3 = R2 + 16 * KB
R4 = R3 + 16 * KB
R5 = R4 + 24 * KB
SB_END = R5 + 8 * KB + 512

QKV_ORDER = list(range(8, 16)) + list(range(16, 24)) + list(range(0, 8))


class _DryIns:
    def then_inc(self, sem, k=1):
        return self


_DRY = _DryIns()


class _DryEng:

    def __getattr__(self, name):
        return lambda *a, **k: _DRY


class _RecIns:
    def __init__(self, log):
        self.log = log

    def then_inc(self, sem, k=1):
        self.log.append(("inc", id(sem), k))
        return self


class _RecEng:

    def __init__(self, stream):
        self.stream = stream

    def __getattr__(self, name):
        if name == "wait_ge":
            def w(sem, n):
                self.stream.append(("wait", id(sem), n))
            return w

        def op(*a, **kw):
            log = []
            self.stream.append(("op", log))
            return _RecIns(log)
        return op


def _simulate(streams, semnames):
    vals = {}
    pcs = {e: 0 for e in streams}
    progress = True
    while progress:
        progress = False
        for e, ops in streams.items():
            while pcs[e] < len(ops):
                kind, a, *rest = ops[pcs[e]]
                if kind == "wait":
                    if vals.get(a, 0) < rest[0]:
                        break
                    pcs[e] += 1
                    progress = True
                else:
                    for _, semid, k in a:
                        vals[semid] = vals.get(semid, 0) + k
                    pcs[e] += 1
                    progress = True
    stuck = {e: pcs[e] for e in streams if pcs[e] < len(streams[e])}
    if stuck:
        msg = ["DEADLOCK DETECTED in program graph:"]
        for e, pc in stuck.items():
            kind, semid, n = streams[e][pc]
            msg.append(f"  {e} blocked at op#{pc} waiting "
                       f"{semnames.get(semid, semid)} >= {n} "
                       f"(currently {vals.get(semid, 0)})")
        raise RuntimeError("\n".join(msg))


class Ctr:
    def __init__(self, sem):
        self.sem = sem
        self.n = 0

    def tick(self, ins, k=1):
        ins.then_inc(self.sem, k)
        self.n += k
        return self.n

    def reset(self):
        self.n = 0


def build_nc(s0, s1, s2, debug=False):
    nc = bass.Bass("TRN2", target_bir_lowering=False)

    x_in = nc.dram_tensor("x", [1024, 4096], F32, kind="ExternalInput")
    wq_blk = nc.dram_tensor("wq_blk", [NFT, P, NCT, P], F32, kind="ExternalInput")
    wo_in = nc.dram_tensor("wo", [1024, 4096], F32, kind="ExternalInput")
    k0_blk = nc.dram_tensor("k0_blk", [NGT, P, NCT, P], U8, kind="ExternalInput")
    k1_blk = nc.dram_tensor("k1_blk", [NGT, P, NCT, P], U8, kind="ExternalInput")
    k2_in = nc.dram_tensor("k2", [4096, 4096], U8, kind="ExternalInput")
    bqkv_in = nc.dram_tensor("bqkv_col", [P, NFT], F32, kind="ExternalInput")
    bo_in = nc.dram_tensor("bo_rep", [P, 4096], F32, kind="ExternalInput")
    ident_in = nc.dram_tensor("ident", [P, P], BF16, kind="ExternalInput")
    onec_in = nc.dram_tensor("ones_col", [P, 1], BF16, kind="ExternalInput")
    oner_in = nc.dram_tensor("ones_row", [1, P], F32, kind="ExternalInput")
    dinh_in = nc.dram_tensor("dinh", [P, 1], F32, kind="ExternalInput")
    bsel_in = nc.dram_tensor("bsel", [1, 1], I32, kind="ExternalInput")
    out_dr = nc.dram_tensor("out", [1024, 4096], F32, kind="ExternalOutput")

    taps = {}
    if debug:
        for nm, shp, dt_ in [
                ("tap_xt", [P, 32768], BF16), ("tap_qt", [P, 8192], BF16),
                ("tap_kv", [P, 8192], BF16), ("tap_vs", [P, 8192], BF16),
                ("tap_kt", [P, 32768], BF16), ("tap_vf", [P, 32768], BF16),
                ("tap_p", [P, 4096], BF16), ("tap_ot", [P, 8192], BF16),
                ("tap_x1", [1024, 4096], BF16), ("tap_b1", [P, 32768], BF16),
                ("tap_b2", [P, 32768], BF16), ("tap_rs", [1, 1024], F32)]:
            taps[nm] = nc.dram_tensor(nm, shp, dt_, kind="ExternalOutput")
    wobf = nc.dram_tensor("wobf", [1024, 4096], BF16)
    cc_in_k = nc.dram_tensor("cc_in_k", [1024, 1024], BF16)
    cc_in_v = nc.dram_tensor("cc_in_v", [1024, 1024], BF16)
    cc_out_k = nc.dram_tensor("cc_out_k", [8 * 1024, 1024], BF16,
                              addr_space="Shared")
    cc_out_v = nc.dram_tensor("cc_out_v", [8 * 1024, 1024], BF16,
                              addr_space="Shared")
    x1sp = nc.dram_tensor("x1sp", [1024, 4096], BF16)
    xbfd = nc.dram_tensor("xbfd", [1024, 4096], BF16)

    sb = nc.alloc_sbuf_tensor("sbarena", [P, SB_END], U8)
    psa = nc.alloc_psum_tensor("psarena", [P, 16384], U8)

    def sbv(off, cols, dt, p0=0, np_=P):
        bpe = mybir.dt.size(dt)
        return sb[p0:p0 + np_, off:off + cols * bpe].bitcast(dt)

    def sb3(off, n0, n1, dt):
        bpe = mybir.dt.size(dt)
        return sb[:, off:off + n0 * n1 * bpe].bitcast(dt).rearrange(
            "p (a b) -> p a b", a=n0)

    def bank_f32(b, cols=512):
        return psa[:, b * 2048: b * 2048 + cols * 4].bitcast(F32)

    def bank_slot_bf16(b, slot):
        o = b * 2048 + slot * 256
        return psa[:, o:o + 256].bitcast(BF16)

    def bank_row_f32(b, cols=512):
        return psa[0:1, b * 2048: b * 2048 + cols * 4].bitcast(F32)

    def bank3_bf16(b, n0, n1):
        return psa[:, b * 2048: b * 2048 + n0 * n1 * 2].bitcast(BF16).rearrange(
            "p (a b) -> p a b", a=n0)

    idt = sbv(OFF_IDENT, P, BF16)
    bqkv = sbv(OFF_BQKV, NFT, F32)
    onec = sbv(OFF_ONEC, 1, BF16)
    oner = sbv(OFF_ONER, P, F32, np_=1)
    dinh = sbv(OFF_DINH, 1, F32)
    bsel_sb = sb[0:1, OFF_BSEL:OFF_BSEL + 4].bitcast(I32)
    maxp = sbv(OFF_MAXP, 8, F32)
    negm = sbv(OFF_NEGM, 1, F32)
    sep = sbv(OFF_SEP, 8, F32)
    sumexp = sbv(OFF_SE, 1, F32)
    recip = sbv(OFF_RECIP, 1, F32)
    msum = sbv(OFF_MSUM, 1, F32)
    m_sb = sbv(OFF_M, 1, F32)
    m2_sb = sbv(OFF_M2, 1, F32)
    var_sb = sbv(OFF_VAR, 1, F32)
    rstd = sbv(OFF_RSTD, 1, F32)
    sqs = sbv(OFF_SQS, 1, F32)
    rs0c = sbv(5600, 8, F32)
    rs0b = sbv(5664, 8, BF16)
    msp = sbv(OFF_MSP, 32, F32)
    sqp = sbv(OFF_SQP, 16, F32)
    rsrow = sbv(OFF_RSROW, 1024, F32, np_=1)
    bias2 = sbv(OFF_BIAS2, 8, F32)

    XT = sb3(R0, NCT, 1024, BF16)
    VF = sb3(R0, NCT, 1024, BF16)
    X1T = sb3(R0, NCT, 1024, BF16)
    B2T = sb3(R0, NGT, 1024, BF16)

    xbf = [sbv(R1 + i * 8 * KB, 4096, BF16) for i in range(2)]
    kvK = sb3(R1 + 32 * KB, NDT, 1024, BF16)
    vstage = sb3(R1 + 48 * KB, NDT, 1024, BF16)
    KT = sb3(R1, NDT, 4096, BF16)
    zbf8 = [sbv(R1 + i * 8 * KB, 4096, BF16) for i in range(NQT)]
    wopan3 = [sb3(R2 + i * 4 * KB, 4, 512, BF16) for i in range(2)]
    B1T = sb3(R1, NGT, 1024, BF16)
    z2g = [sbv(R1 + i * 16 * KB, 4096, F32) for i in range(4)]

    QT = sb3(R2, NDT, 1024, BF16)
    bo_rep = sbv(R2, 4096, F32)
    sq_dump2 = sbv(R2, 4096, BF16)

    bo_bf = sbv(R5, 4096, BF16, np_=1)
    ones_bf = sbv(R5 + 8 * KB, P, BF16, np_=1)
    OT = sb3(R3, NQT, NDT * P, BF16)
    mkpan2 = [sb3(R3, 2, 4096, BF16), sb3(R2, 2, 4096, BF16)]
    wqpan = [sbv(R3 + i * 8 * KB, 4096, BF16) for i in range(2)]

    vt_tmp = [sbv(R4 + i * 2 * KB, 1024, BF16) for i in range(2)]
    P_sb = sbv(R4, 4096, BF16)
    PT = sb3(R4 + 8 * KB, NCT, P, BF16)
    O_bf = sbv(R4 + 16 * KB, 1024, BF16)
    xgs = [sb3(R4 + i * 8 * KB, NQT, 512, BF16) for i in range(2)]
    x1bf = [sbv(R4 + i * 8 * KB, 4096, BF16) for i in range(2)]
    sq_dump1 = sbv(R4 + 16 * KB, 4096, BF16)
    rsb_sb = sbv(R4, 1024, F32)
    tmp_ffn = [sbv(R4 + 4 * KB + i * 2 * KB, 512, F32) for i in range(4)]
    u2pan2 = [sb3(R3 + i * 4 * KB, 2, 1024, BF16) for i in range(2)]
    x1rel = [sbv(R3 + 8 * KB, 4096, BF16), sbv(R2 + 8 * KB, 4096, BF16),
             sbv(R4, 4096, BF16), sbv(R4 + 8 * KB, 4096, BF16)]
    ffn_t = [sbv(R4 + 16 * KB + i * 4 * KB, 1024, F32) for i in range(2)]

    import contextlib
    stack = contextlib.ExitStack()
    sems = {}
    for name in ["g16", "gx0", "gx1", "gq0", "gq1", "gm0", "gm1", "gu0",
                 "gu1", "skt", "svf", "s16", "sw0", "sw1", "st", "cc", "pe",
                 "act", "dve", "tp", "sx", "gsp0", "gsp1", "gsp2",
                 "sx0", "sx1", "sx2", "gp1", "so0", "so1", "so2", "so3",
                 "sv0", "sv1", "sv2", "sv3"]:
        sems[name] = stack.enter_context(nc.semaphore(name))
    g16 = Ctr(sems["g16"])
    gx = [Ctr(sems["gx0"]), Ctr(sems["gx1"])]
    gq = [Ctr(sems["gq0"]), Ctr(sems["gq1"])]
    gm = [Ctr(sems["gm0"]), Ctr(sems["gm1"])]
    gu = [Ctr(sems["gu0"]), Ctr(sems["gu1"])]
    skt = Ctr(sems["skt"])
    svf = Ctr(sems["svf"])
    s16 = Ctr(sems["s16"])
    sw = [Ctr(sems["sw0"]), Ctr(sems["sw1"])]
    st = Ctr(sems["st"])
    ccc = Ctr(sems["cc"])
    pe = Ctr(sems["pe"])
    act = Ctr(sems["act"])
    dve = Ctr(sems["dve"])
    tp = Ctr(sems["tp"])
    sx = Ctr(sems["sx"])
    gspl = [Ctr(sems["gsp0"]), Ctr(sems["gsp1"]), Ctr(sems["gsp2"])]
    sxl = [Ctr(sems["sx0"]), Ctr(sems["sx1"]), Ctr(sems["sx2"])]
    gp1 = Ctr(sems["gp1"])
    so = [Ctr(sems["so%d" % i]) for i in range(4)]
    sv = [Ctr(sems["sv%d" % i]) for i in range(4)]
    all_ctrs = [g16, gx[0], gx[1], gq[0], gq[1], gm[0], gm[1], gu[0], gu[1],
                skt, svf, s16, sw[0], sw[1], st, ccc, pe, act, dve, tp, sx,
                gspl[0], gspl[1], gspl[2], sxl[0], sxl[1], sxl[2], gp1,
                so[0], so[1], so[2], so[3], sv[0], sv[1], sv[2], sv[3]]

    M = {}

    def rec(key, val):
        if key in M:
            assert M[key] == val, f"milestone {key} changed: {M[key]} -> {val}"
        M[key] = val

    def tensor_body(t, dry):
        for xt in range(NQT):
            par = xt % 2
            t.wait_ge(gx[par].sem, 16 * (xt // 2 + 1))
            for bk in range(4):
                ti = 4 * xt + bk
                if ti >= 2:
                    t.wait_ge(dve.sem, M.get(("x2tev", ti - 2), 0))
                last = None
                for j in range(8):
                    ct = bk * 8 + j
                    last = t.transpose(bank_slot_bf16(bk % 2, j),
                                       xbf[par][:, ct * P:(ct + 1) * P], idt)
                rec(("x2t", ti), pe.tick(last))
        t.wait_ge(dve.sem, M.get(("x2tev", 31), 0))
        pend_vt = []
        for fi, ft in enumerate(QKV_ORDER):
            t.wait_ge(gq[fi % 2].sem, 16 * (fi // 2 + 1))
            if fi >= 2:
                t.wait_ge(act.sem, M.get(("qkvev", fi - 2), 0))
            bA = 2 + 2 * (fi % 2)
            last = None
            for ct in range(NCT):
                lhs = wqpan[fi % 2][:, ct * P:(ct + 1) * P]
                t.matmul(bank_f32(bA), lhs, XT[:, ct, 0:512],
                         start=ct == 0, stop=ct == NCT - 1)
                last = t.matmul(bank_f32(bA + 1), lhs, XT[:, ct, 512:1024],
                                start=ct == 0, stop=ct == NCT - 1)
            rec(("qkv", fi), pe.tick(last))
            if pend_vt:
                fv, fvi = pend_vt.pop(0)
                dtv = fv - 16
                t.wait_ge(act.sem, M.get(("qkvev", fvi), 0))
                if dtv >= 2:
                    t.wait_ge(dve.sem, M.get(("vtev", dtv - 2), 0))
                lastv = None
                for tt in range(8):
                    lastv = t.transpose(bank_slot_bf16(dtv % 2, tt),
                                        vt_tmp[fvi % 2][:, tt * P:(tt + 1) * P],
                                        idt)
                rec(("vt", dtv), pe.tick(lastv))
            if 16 <= ft < 24:
                pend_vt.append((ft, fi))
        while pend_vt:
            fv, fvi = pend_vt.pop(0)
            dtv = fv - 16
            t.wait_ge(act.sem, M.get(("qkvev", fvi), 0))
            if dtv >= 2:
                t.wait_ge(dve.sem, M.get(("vtev", dtv - 2), 0))
            lastv = None
            for tt in range(8):
                lastv = t.transpose(bank_slot_bf16(dtv % 2, tt),
                                    vt_tmp[fvi % 2][:, tt * P:(tt + 1) * P], idt)
            rec(("vt", dtv), pe.tick(lastv))
        rec("qkv_done", pe.n)
        for qt in range(NQT):
            if qt == 0:
                t.wait_ge(skt.sem, 64)
                t.wait_ge(act.sem, M.get(("qkvev", 23), 0))
                t.wait_ge(dve.sem, M.get("vstage_done", 0))
            else:
                t.wait_ge(act.sem, M.get(("exp", qt - 1, 7), 0))
                t.wait_ge(dve.sem, M.get(("otev", qt - 1), 0))
            for kcg in range(4):
                last = None
                for dt_ in range(NDT):
                    lhs = QT[:, dt_, qt * P:(qt + 1) * P]
                    t.matmul(bank_f32(2 * kcg), lhs,
                             KT[:, dt_, kcg * 1024:kcg * 1024 + 512],
                             start=dt_ == 0, stop=dt_ == NDT - 1)
                    last = t.matmul(bank_f32(2 * kcg + 1), lhs,
                                    KT[:, dt_, kcg * 1024 + 512:(kcg + 1) * 1024],
                                    start=dt_ == 0, stop=dt_ == NDT - 1)
                rec(("scores_p", qt, kcg), pe.tick(last))
            if qt == NQT - 1:
                rec("scores_done", pe.n)
            for rnd in range(4):
                t.wait_ge(act.sem, M.get(("exp", qt, 2 * rnd + 1), 0))
                lastp = None
                for j in range(8):
                    kt = rnd * 8 + j
                    lastp = t.transpose(bank_slot_bf16(rnd, j),
                                        P_sb[:, kt * P:(kt + 1) * P], idt)
                rec(("pt", qt, rnd), pe.tick(lastp))
            t.wait_ge(act.sem, M.get(("exp", qt, 7), 0))
            last = None
            for kt in range(NCT):
                if kt % 8 == 0:
                    t.wait_ge(dve.sem, M.get(("ptev", qt, kt // 8), 0))
                    if qt == 0:
                        t.wait_ge(sv[kt // 8].sem, 16)
                t.matmul(bank_f32(4), PT[:, kt, :], VF[:, kt, 0:512],
                         start=kt == 0, stop=kt == NCT - 1)
                last = t.matmul(bank_f32(5), PT[:, kt, :], VF[:, kt, 512:1024],
                                start=kt == 0, stop=kt == NCT - 1)
            rec(("attnv", qt), pe.tick(last))
            t.wait_ge(dve.sem, M.get(("oev", qt), 0))
            lasto = None
            for dt_ in range(NDT):
                lasto = t.transpose(bank_slot_bf16(6, dt_),
                                    O_bf[:, dt_ * P:(dt_ + 1) * P], idt)
            rec(("ot", qt), pe.tick(lasto))
        rec("att_done", pe.n)
        pan_i = 0
        for cq in range(8):
            for pp in range(2):
                par = pan_i % 2
                t.wait_ge(sw[par].sem, 16 * (pan_i // 2 + 1))
                if pan_i == 0:
                    t.wait_ge(g16.sem, M.get("bo_ld", 0))
                    t.wait_ge(dve.sem, M.get(("otev", NQT - 1), 0))
                if pp == 0:
                    for qt in range(NQT):
                        if cq >= 1:
                            t.wait_ge(dve.sem, M.get(("zadd", cq - 1, qt), 0))
                        t.matmul(bank_f32(qt), ones_bf,
                                 bo_bf[0:1, cq * 512:(cq + 1) * 512],
                                 start=True, stop=False)
                lastm = None
                for a in range(4):
                    dt_ = 4 * pp + a
                    for qt in range(NQT):
                        lhs = OT[:, qt, dt_ * P:(dt_ + 1) * P]
                        lastm = t.matmul(bank_f32(qt), lhs,
                                         wopan3[par][:, a, :],
                                         start=False, stop=dt_ == NDT - 1)
                rec(("wo_pan", pan_i), pe.tick(lastm))
                pan_i += 1
            rec(("wo_cq", cq), pe.n)
        rec("wo_done", pe.n)
        for qt in range(NQT):
            for bk in range(4):
                t.wait_ge(dve.sem, M.get(("x1bf_c", qt, bk), 0))
                ti = 4 * qt + bk
                if ti >= 2:
                    t.wait_ge(dve.sem, M.get(("x1tev", ti - 2), 0))
                lastt = None
                for j in range(8):
                    ct = bk * 8 + j
                    lastt = t.transpose(bank_slot_bf16(6 + bk % 2, j),
                                        x1bf[qt % 2][:, ct * P:(ct + 1) * P],
                                        idt)
                rec(("x1t", qt, bk), pe.tick(lastt))
        for layer in range(2):
            IN3 = X1T if layer == 0 else B1T
            if layer == 0:
                t.wait_ge(dve.sem, M.get("x1t_all_ev", 0))
                last = None
                for qt in range(NQT):
                    t.wait_ge(dve.sem, M.get(("rs0b", qt), 0))
                    bb = 4 + qt // 4
                    cc = (qt % 4) * P
                    last = t.matmul(
                        bank_row_f32(bb)[0:1, cc:cc + P],
                        rs0b[:, qt:qt + 1], idt,
                        start=True, stop=True)
            else:
                t.wait_ge(dve.sem, M.get(("ffn_ev_done", 0), 0))
                t.wait_ge(act.sem, M.get(("ffn_act_done", 0), 0))
                last = None
                for ct in range(NCT):
                    t.matmul(bank_row_f32(4), onec, IN3[:, ct, 0:512],
                             start=ct == 0, stop=ct == NCT - 1)
                    last = t.matmul(bank_row_f32(5), onec, IN3[:, ct, 512:1024],
                                    start=ct == 0, stop=ct == NCT - 1)
            rec(("rs", layer), pe.tick(last))
            t.wait_ge(dve.sem, M.get(("rshalf", layer), 0))
            t.matmul(bank_f32(6), oner, rsrow[0:1, 0:512], start=True, stop=True)
            last = t.matmul(bank_f32(7), oner, rsrow[0:1, 512:1024],
                            start=True, stop=True)
            rec(("rsb", layer), pe.tick(last))
            for gp in range(NGT // 2):
                par = gp % 2
                t.wait_ge(gm[par].sem, 16 * (layer * 8 + gp // 2 + 1))
                if gp >= 2:
                    t.wait_ge(dve.sem, M.get(("ffn_tt", layer, 2 * gp - 3, 1), 0))
                elif gp == 1:
                    t.wait_ge(dve.sem, M.get(("rsbev", layer), 0))
                bA = 4 * (gp % 2)
                last = None
                for ct in range(NCT):
                    for j2 in range(2):
                        lhs = mkpan2[par][:, j2, ct * P:(ct + 1) * P]
                        t.matmul(bank_f32(bA + 2 * j2), lhs, IN3[:, ct, 0:512],
                                 start=ct == 0, stop=ct == NCT - 1)
                        last = t.matmul(bank_f32(bA + 2 * j2 + 1), lhs,
                                        IN3[:, ct, 512:1024],
                                        start=ct == 0, stop=ct == NCT - 1)
                rec(("ffn_mm", layer, gp), pe.tick(last))
            rec(("ffn_mm_done", layer), pe.n)
        t.wait_ge(dve.sem, M.get(("ffn_ev_done", 1), 0))
        t.wait_ge(act.sem, M.get(("ffn_act_done", 1), 0))
        last = None
        for gt in range(NGT):
            t.matmul(bank_row_f32(4), onec, B2T[:, gt, 0:512],
                     start=gt == 0, stop=gt == NGT - 1)
            last = t.matmul(bank_row_f32(5), onec, B2T[:, gt, 512:1024],
                            start=gt == 0, stop=gt == NGT - 1)
        rec("rs2", pe.tick(last))
        t.wait_ge(dve.sem, M.get("rs2row_ev", 0))
        for qt in range(NQT):
            if qt > 0:
                t.wait_ge(dve.sem, M.get(("rs2col_ev", qt - 1), 0))
            lastc = t.matmul(bank_f32(6, cols=1),
                             rsrow[0:1, qt * P:(qt + 1) * P],
                             oner[0:1, 0:1], start=True, stop=True)
            rec(("rs2col", qt), pe.tick(lastc))
        pan2 = 0
        for q4 in range(2):
            for cp in range(4):
                for gt in range(NGT):
                    pr = pan2 % 2
                    j2 = gt % 2
                    if j2 == 0:
                        t.wait_ge(gu[pr].sem, 16 * (pan2 // 2 + 1))
                    if gt == 0 and q4 == 0 and cp == 0:
                        t.wait_ge(dve.sem, M.get("bias2_ev", 0))
                    last = None
                    for ql in range(4):
                        qt = q4 * 4 + ql
                        if gt == 0 and (q4 > 0 or cp > 0):
                            pq4, pcp = (q4, cp - 1) if cp > 0 else (q4 - 1, 3)
                            t.wait_ge(act.sem, M.get(("f2act", pq4, pcp, ql), 0))
                        t.matmul(bank_f32(2 * ql),
                                 B2T[:, gt, qt * P:(qt + 1) * P],
                                 u2pan2[pr][:, j2, 0:512],
                                 start=gt == 0, stop=gt == NGT - 1)
                        last = t.matmul(bank_f32(2 * ql + 1),
                                        B2T[:, gt, qt * P:(qt + 1) * P],
                                        u2pan2[pr][:, j2, 512:1024],
                                        start=gt == 0, stop=gt == NGT - 1)
                    rec(("f2g", pan2, j2), pe.tick(last))
                    if j2 == 1:
                        pan2 += 1
                rec(("f2mm", q4, cp), pe.n)
        rec("pe_end", pe.n)

    def gpsimd_body(g, dry):
        for ap_out, ap_in in [(idt, ident_in[:, :]), (bqkv, bqkv_in[:, :]),
                              (onec, onec_in[:, :]), (oner, oner_in[:, :]),
                              (dinh, dinh_in[:, :]), (bsel_sb, bsel_in[:, :])]:
            g16.tick(g.dma_start(out=ap_out, in_=ap_in), 16)
        rec("consts_ld", g16.n)
        g16.tick(g.dma_start(out=wobf[:, :], in_=wo_in[:, :]), 16)
        rec("wobf_ld", g16.n)
        for xt in range(NQT):
            if xt >= 2:
                g.wait_ge(pe.sem, M.get(("x2t", 4 * (xt - 2) + 3), 0))
                g.wait_ge(sxl[2].sem, 16 * (xt - 1))
            gx[xt % 2].tick(g.dma_start(out=xbf[xt % 2],
                                        in_=x_in[xt * P:(xt + 1) * P, :]), 16)
        def gather(gg, b, count, kv, ranks=range(4)):
            base = b * 4 * 1024
            if kv == 0:
                for r in ranks:
                    ins = gg.dma_start(
                        out=KT[:, :, r * 1024:(r + 1) * 1024],
                        in_=cc_out_k[base + r * 1024: base + (r + 1) * 1024, :]
                        .rearrange("(dt p) q -> p dt q", p=P))
                    if count:
                        skt.tick(ins, 16)
                    else:
                        ins.then_inc(skt.sem, 16)
            else:
                for r in ranks:
                    ins = gg.dma_start(
                        out=VF[:, r * 8:(r + 1) * 8, :],
                        in_=cc_out_v[base + r * 1024: base + (r + 1) * 1024, :]
                        .rearrange("(tt p) d -> p tt d", p=P))
                    if count:
                        sv[r].tick(ins, 16)
                    else:
                        ins.then_inc(sv[r].sem, 16)

        def branch_gather(kv, ranks, tag):
            if dry:
                gather(g, 0, True, kv, ranks)
            else:
                with g.register("rb%s" % tag) as rb:
                    g.reg_load(rb, bsel_sb[0:1, 0:1])
                    with g.If_eq(rb, 0):
                        gather(g, 0, True, kv, ranks)
                    with g.If_eq(rb, 1):
                        gather(g, 1, False, kv, ranks)

        for fi, ft in enumerate(QKV_ORDER):
            if fi >= 2:
                g.wait_ge(pe.sem, M.get(("qkv", fi - 2), 0))
            gq[fi % 2].tick(
                g.dma_start(out=wqpan[fi % 2],
                            in_=wq_blk[ft].rearrange("p a b -> p (a b)")), 16)
            if fi == 11:
                g.wait_ge(act.sem, M.get("actK_done", 0))
                g16.tick(g.dma_start(
                    out=cc_in_k[:, :].rearrange("(dt p) q -> p dt q", p=P),
                    in_=kvK[:, :, :]), 16)
                g.wait_ge(g16.sem, g16.n)
                ccc.tick(g.collective_compute(
                    "AllGather", OP.bypass, replica_groups=[list(range(8))],
                    ins=[cc_in_k[:, :]], outs=[cc_out_k[:, :]]))
            if fi == 17:
                g.wait_ge(dve.sem, M.get("vstage_done", 0))
                g16.tick(g.dma_start(
                    out=cc_in_v[:, :].rearrange("(tt p) d -> p tt d", p=P),
                    in_=vstage[:, :, :]), 16)
                g.wait_ge(g16.sem, g16.n)
                ccc.tick(g.collective_compute(
                    "AllGather", OP.bypass, replica_groups=[list(range(8))],
                    ins=[cc_in_v[:, :]], outs=[cc_out_v[:, :]]))
            if 18 <= fi <= 21:
                if fi == 18:
                    g.wait_ge(ccc.sem, 1)
                branch_gather(0, [fi - 18], "k%d" % (fi - 18))
        g.wait_ge(ccc.sem, 2)
        g.wait_ge(pe.sem, M.get("qkv_done", 0))
        branch_gather(1, range(4), "v")
        g16.tick(g.dma_start(out=bo_bf, in_=bo_in[0:1, :]), 16)
        g16.tick(g.dma_start(out=ones_bf, in_=oner_in[:, :]), 16)
        rec("bo_ld", g16.n)
        for layer in range(2):
            kblk = k0_blk if layer == 0 else k1_blk
            for gp in range(NGT // 2):
                li = layer * (NGT // 2) + gp
                par = gp % 2
                if li >= 2:
                    pl, pg = divmod(li - 2, NGT // 2)
                    g.wait_ge(pe.sem, M.get(("ffn_mm", pl, pg), 0))
                else:
                    g.wait_ge(pe.sem, M.get("wo_done", 0))
                gm[par].tick(g.dma_start(
                    out=mkpan2[par][:, :, :],
                    in_=kblk[2 * gp:2 * gp + 2].rearrange(
                        "a p c j -> p a (c j)")), 16)
        pan2 = 0
        for q4 in range(2):
            for cp in range(4):
                for g2 in range(NGT // 2):
                    par = pan2 % 2
                    if pan2 >= 2:
                        g.wait_ge(pe.sem, M.get(("f2g", pan2 - 2, 1), 0))
                    else:
                        g.wait_ge(pe.sem, M.get(("ffn_mm", 1, 14), 0))
                    gu[par].tick(g.dma_start(
                        out=u2pan2[par][:, :, :],
                        in_=k2_in[g2 * 256:(g2 + 1) * 256,
                                  cp * 1024:(cp + 1) * 1024].rearrange(
                                      "(a p) c -> p a c", p=P)), 16)
                    pan2 += 1

    def sync_body(sy, dry):
        for xt in range(NQT):
            sy.wait_ge(gx[xt % 2].sem, 16 * (xt // 2 + 1))
            sxl[2].tick(sy.dma_start(out=xbfd[xt * P:(xt + 1) * P, :],
                                     in_=xbf[xt % 2]), 16)
        pan_i = 0
        for cq in range(8):
            for pp in range(2):
                par = pan_i % 2
                if pan_i == 0:
                    sy.wait_ge(g16.sem, M.get("wobf_ld", 0))
                    sy.wait_ge(pe.sem, M.get("scores_done", 0))
                if pan_i >= 2:
                    sy.wait_ge(pe.sem, M.get(("wo_pan", pan_i - 2), 0))
                sw[par].tick(sy.dma_start(
                    out=wopan3[par][:, :, :],
                    in_=wobf[pp * 512:(pp + 1) * 512,
                             cq * 512:(cq + 1) * 512].rearrange(
                                 "(a p) c -> p a c", p=P)), 16)
                pan_i += 1
        for qt in range(NQT):
            sy.wait_ge(dve.sem, M.get(("x1bf_c", qt, 3), 0))
            gspl[qt % 2].tick(sy.dma_start(out=x1sp[qt * P:(qt + 1) * P, :],
                                           in_=x1bf[qt % 2]), 16)
            rec(("spill", qt), gspl[qt % 2].n)
        for _g in range(2):
            rec(("spill_last", _g), gspl[_g].n)
        rec("wopan_done", s16.n)
        for q4 in range(2):
            for ql in (0, 1):
                qt = q4 * 4 + ql
                if q4 > 0:
                    sy.wait_ge(so[ql].sem, 16 * q4)
                else:
                    sy.wait_ge(pe.sem, M.get(("ffn_mm_done", 1), 0))
                    for _g in range(2):
                        sy.wait_ge(gspl[_g].sem, M.get(("spill_last", _g), 0))
                s16.tick(sy.dma_start(out=x1rel[ql],
                                      in_=x1sp[qt * P:(qt + 1) * P, :]), 16)
            rec(("x1rel_q4y", q4), s16.n)
            for ql in (0, 2):
                qt = q4 * 4 + ql
                sy.wait_ge(dve.sem, M.get(("out_ready", qt), 0))
                so[ql].tick(sy.dma_start(out=out_dr[qt * P:(qt + 1) * P, :],
                                         in_=z2g[ql]), 16)
        for ql in range(4):
            sy.wait_ge(so[ql].sem, so[ql].n)

    def scalar_body(s, dry):
        for fi, ft in enumerate(QKV_ORDER):
            s.wait_ge(pe.sem, M.get(("qkv", fi), 0))
            bA = 2 + 2 * (fi % 2)
            if 8 <= ft < 16:
                d0 = kvK[:, ft - 8, :]
            elif ft >= 16:
                if ft >= 18:
                    s.wait_ge(pe.sem, M.get(("vt", ft - 18), 0))
                d0 = vt_tmp[fi % 2][:, :]
            else:
                d0 = QT[:, ft, :]
            s.activation(d0[:, 0:512], bank_f32(bA), AF.Identity,
                         bias=bqkv[:, ft:ft + 1], scale=np.float32(1.0))
            a2 = s.activation(d0[:, 512:1024], bank_f32(bA + 1), AF.Identity,
                              bias=bqkv[:, ft:ft + 1], scale=np.float32(1.0))
            rec(("qkvev", fi), act.tick(a2))
            if fi == 7:
                rec("actK_done", act.n)
        if debug:
            s.wait_ge(dve.sem, M.get("vstage_done", 0))
            tp.tick(s.dma_start(out=taps["tap_qt"][:, :],
                                in_=QT.rearrange("p a b -> p (a b)")), 16)
            tp.tick(s.dma_start(out=taps["tap_kv"][:, :],
                                in_=kvK.rearrange("p a b -> p (a b)")), 16)
            tp.tick(s.dma_start(out=taps["tap_vs"][:, :],
                                in_=vstage.rearrange("p a b -> p (a b)")), 16)
            s.wait_ge(dve.sem, M.get(("x2tev", 31), 0))
            tp.tick(s.dma_start(out=taps["tap_xt"][:, :],
                                in_=XT.rearrange("p a b -> p (a b)")), 16)
            s.wait_ge(skt.sem, 64)
            for _r in range(4):
                s.wait_ge(sv[_r].sem, 16)
            tp.tick(s.dma_start(out=taps["tap_kt"][:, :],
                                in_=KT.rearrange("p a b -> p (a b)")), 16)
            tp.tick(s.dma_start(out=taps["tap_vf"][:, :],
                                in_=VF.rearrange("p a b -> p (a b)")), 16)
        for qt in range(NQT):
            s.wait_ge(dve.sem, M.get(("negm_t", qt), 0))
            for j in range(8):
                a = s.activation(P_sb[:, j * 512:(j + 1) * 512], bank_f32(j),
                                 AF.Exp, bias=negm[:, 0:1], scale=np.float32(SCALE_ATT),
                                 accum_out=sep[:, j:j + 1])
                rec(("exp", qt, j), act.tick(a))
        if debug:
            tp.tick(s.dma_start(out=taps["tap_p"][:, :], in_=P_sb), 16)
            s.wait_ge(dve.sem, M.get(("otev", 7), 0))
            tp.tick(s.dma_start(out=taps["tap_ot"][:, :],
                                in_=OT.rearrange("p a b -> p (a b)")), 16)
        def xgs_load(cq):
            if cq < 2:
                s.wait_ge(pe.sem, M.get(("attnv", NQT - 1), 0))
            else:
                s.wait_ge(dve.sem, M.get(("zadd", cq - 2, NQT - 1), 0))
            sxl[cq % 2].tick(s.dma_start(
                out=xgs[cq % 2][:, :, :],
                in_=xbfd[:, cq * 512:(cq + 1) * 512].rearrange(
                    "(a p) c -> p a c", p=P)), 16)
        for cq in range(4):
            xgs_load(cq)
        for cp in range(4):
            if cp < 2:
                xgs_load(4 + 2 * cp)
                xgs_load(5 + 2 * cp)
            for qt in range(NQT):
                s.wait_ge(dve.sem, M.get(("zadd", 2 * cp + 1, qt), 0))
                a = s.activation(sq_dump1[:, 0:1024],
                                 zbf8[qt][:, cp * 1024:(cp + 1) * 1024],
                                 AF.Copy, bias=0.0, scale=np.float32(1.0),
                                 accum_out=msp[:, qt * 4 + cp:qt * 4 + cp + 1])
                rec(("part1", qt, cp), act.tick(a))
        for qt in range(NQT):
            s.wait_ge(dve.sem, M.get(("x1bf_t", qt), 0))
            a = s.activation(sq_dump1, x1bf[qt % 2], AF.Copy, bias=0.0,
                             scale=np.float32(1.0),
                             accum_out=rs0c[:, qt:qt + 1])
            rec(("rs0a", qt), act.tick(a))
        if debug:
            for _g in range(2):
                s.wait_ge(gspl[_g].sem, M.get(("spill_last", _g), 0))
            tp.tick(s.dma_start(out=taps["tap_x1"][:, :], in_=x1sp[:, :]), 16)
        for layer in range(2):
            s_lay = s0 if layer == 0 else s1
            OUT3 = B1T if layer == 0 else B2T
            first = True
            for gt in range(NGT):
                if gt > BT_GT:
                    continue
                for ch in range(2):
                    if first:
                        for _g in range(2):
                            s.wait_ge(gspl[_g].sem,
                                      M.get(("spill_last", _g), 0))
                        first = False
                    s.wait_ge(dve.sem, M.get(("ffn_tt", layer, gt, ch), 0))
                    a = s.activation(OUT3[:, gt, ch * 512:(ch + 1) * 512],
                                     tmp_ffn[2 * (gt % 2) + ch], AF.Relu,
                                     bias=np.float32(0.0), scale=np.float32(4.0 * s_lay))
                    rec(("ffn_relu", layer, gt, ch), act.tick(a))
            rec(("ffn_act_done", layer), act.n)
        if debug:
            s.wait_ge(dve.sem, M.get(("ffn_ev_done", 0), 0))
            tp.tick(s.dma_start(out=taps["tap_b1"][:, :],
                                in_=B1T.rearrange("p a b -> p (a b)")), 16)
            s.wait_ge(dve.sem, M.get(("ffn_ev_done", 1), 0))
            tp.tick(s.dma_start(out=taps["tap_b2"][:, :],
                                in_=B2T.rearrange("p a b -> p (a b)")), 16)
            s.wait_ge(dve.sem, M.get("rs2row_ev", 0))
            tp.tick(s.dma_start(out=taps["tap_rs"][:, :], in_=rsrow), 16)
        for q4 in range(2):
            for ql in (2, 3):
                qt = q4 * 4 + ql
                if q4 > 0:
                    s.wait_ge(so[ql].sem, 16 * q4)
                else:
                    s.wait_ge(pe.sem, M.get(("ffn_mm_done", 1), 0))
                    for _g in range(2):
                        s.wait_ge(gspl[_g].sem, M.get(("spill_last", _g), 0))
                if q4 == 0:
                    s.wait_ge(dve.sem, M.get(("ffn_ev_done", 1), 0))
                sx.tick(s.dma_start(out=x1rel[ql],
                                    in_=x1sp[qt * P:(qt + 1) * P, :]), 16)
            rec(("x1rel_q4s", q4), sx.n)
            for cp in range(4):
                for ql in range(4):
                    qt = q4 * 4 + ql
                    fl = (q4 * 4 + cp) * 4 + ql
                    s.wait_ge(pe.sem, M.get(("f2mm", q4, cp), 0))
                    s.wait_ge(dve.sem, M.get("bias2_ev", 0))
                    if fl >= 2:
                        s.wait_ge(dve.sem, M.get(("f2tt", fl - 2), 0))
                    a = s.activation(ffn_t[ql % 2], bank_f32(2 * ql, cols=1024),
                                     AF.Identity, bias=bias2[:, qt:qt + 1],
                                     scale=np.float32(2.0 * s2))
                    rec(("f2act", q4, cp, ql), act.tick(a))
                for ql in range(4):
                    qt = q4 * 4 + ql
                    fl = (q4 * 4 + cp) * 4 + ql
                    col = ql * 4 + cp
                    s.wait_ge(dve.sem, M.get(("f2tt", fl), 0))
                    sli = z2g[ql][:, cp * 1024:(cp + 1) * 1024]
                    a = s.activation(sq_dump2[:, 0:1024], sli, AF.Copy,
                                     bias=0.0, scale=np.float32(1.0),
                                     accum_out=msp[:, col:col + 1])
                    act.tick(a)
                    a = s.activation(sq_dump2[:, 0:1024], sli, AF.Square,
                                     accum_out=sqp[:, col:col + 1])
                    rec(("part2", qt, cp), act.tick(a))
            for ql in range(4):
                qt = q4 * 4 + ql
                s.wait_ge(dve.sem, M.get(("var2_t", qt), 0))
                a0 = s.activation(var_sb, var_sb, AF.Ln, bias=np.float32(EPS), scale=np.float32(1.0))
                n_ = act.tick(a0)
                s.wait_ge(act.sem, n_)
                a = s.activation(rstd, var_sb, AF.Exp, bias=np.float32(0.0), scale=np.float32(-0.5))
                rec(("rstd2", qt), act.tick(a))
            for ql in (1, 3):
                qt = q4 * 4 + ql
                s.wait_ge(dve.sem, M.get(("out_ready", qt), 0))
                so[ql].tick(s.dma_start(out=out_dr[qt * P:(qt + 1) * P, :],
                                        in_=z2g[ql]), 16)
        if debug:
            s.wait_ge(tp.sem, tp.n)

    def vector_body(v, dry):
        def vs(ins):
            n = dve.tick(ins)
            v.wait_ge(dve.sem, n)
        for xt in range(NQT):
            for bk in range(4):
                ti = 4 * xt + bk
                v.wait_ge(pe.sem, M.get(("x2t", ti), 0))
                c = v.tensor_copy(XT[:, bk * 8:(bk + 1) * 8, xt * P:(xt + 1) * P],
                                  bank3_bf16(bk % 2, 8, P))
                rec(("x2tev", ti), dve.tick(c))
        for dtv in range(8):
            v.wait_ge(pe.sem, M.get(("vt", dtv), 0))
            c = v.tensor_copy(vstage[:, :, dtv * P:(dtv + 1) * P],
                              bank3_bf16(dtv % 2, 8, P))
            rec(("vtev", dtv), dve.tick(c))
        rec("vstage_done", dve.n)
        for qt in range(NQT):
            for kcg in range(4):
                v.wait_ge(pe.sem, M.get(("scores_p", qt, kcg), 0))
                r_ = v.tensor_reduce(maxp[:, kcg:kcg + 1],
                                     bank_f32(2 * kcg, cols=1024), AX.X, OP.max)
                if kcg == 3:
                    vs(r_)
            vs(v.tensor_reduce(negm, maxp[:, 0:4], AX.X, OP.max))
            c = v.tensor_scalar(negm, negm, float(-SCALE_ATT), None,
                                op0=OP.mult)
            rec(("negm_t", qt), dve.tick(c))
            for rnd in range(4):
                v.wait_ge(pe.sem, M.get(("pt", qt, rnd), 0))
                c = v.tensor_copy(PT[:, rnd * 8:(rnd + 1) * 8, :],
                                  bank3_bf16(rnd, 8, P))
                rec(("ptev", qt, rnd), dve.tick(c))
            v.wait_ge(act.sem, M.get(("exp", qt, 7), 0))
            vs(v.tensor_reduce(sumexp, sep[:, 0:8], AX.X, OP.add))
            vs(v.reciprocal(recip, sumexp))
            v.wait_ge(pe.sem, M.get(("attnv", qt), 0))
            v.tensor_scalar(O_bf[:, 0:512], bank_f32(4), recip[:, 0:1], None,
                            op0=OP.mult)
            c = v.tensor_scalar(O_bf[:, 512:1024], bank_f32(5), recip[:, 0:1],
                                None, op0=OP.mult)
            rec(("oev", qt), dve.tick(c))
            v.wait_ge(pe.sem, M.get(("ot", qt), 0))
            c = v.tensor_copy(
                OT[:, qt, :],
                bank3_bf16(6, 8, P).rearrange("p a b -> p (a b)"))
            rec(("otev", qt), dve.tick(c))
        for cq in range(8):
            v.wait_ge(sxl[cq % 2].sem, 16 * (cq // 2 + 1))
            v.wait_ge(pe.sem, M.get(("wo_cq", cq), 0))
            for qt in range(NQT):
                cnt = v.tensor_add(zbf8[qt][:, cq * 512:(cq + 1) * 512],
                                   xgs[cq % 2][:, qt, :], bank_f32(qt))
                rec(("zadd", cq, qt), dve.tick(cnt))

        def emit_x1tev(qt):
            for bk in range(4):
                ti = 4 * qt + bk
                v.wait_ge(pe.sem, M.get(("x1t", qt, bk), 0))
                c = v.tensor_copy(
                    X1T[:, bk * 8:(bk + 1) * 8, qt * P:(qt + 1) * P],
                    bank3_bf16(6 + bk % 2, 8, P))
                rec(("x1tev", ti), dve.tick(c))

        for qt in range(NQT):
            v.wait_ge(act.sem, M.get(("part1", qt, 3), 0))
            vs(v.tensor_reduce(msum, msp[:, 4 * qt:4 * qt + 4], AX.X,
                               OP.add))
            vs(v.tensor_scalar(m_sb, msum, 1.0 / 4096.0, None, op0=OP.mult))
            if qt >= 2:
                v.wait_ge(pe.sem, M.get(("x1t", qt - 2, 3), 0))
                v.wait_ge(gspl[qt % 2].sem, M.get(("spill", qt - 2), 0))
            for bk in range(4):
                c = v.tensor_scalar(
                    x1bf[qt % 2][:, bk * 1024:(bk + 1) * 1024],
                    zbf8[qt][:, bk * 1024:(bk + 1) * 1024], m_sb[:, 0:1],
                    None, op0=OP.subtract)
                rec(("x1bf_c", qt, bk), dve.tick(c))
            rec(("x1bf_t", qt), dve.n)
            if qt > 0:
                emit_x1tev(qt - 1)
                v.wait_ge(act.sem, M.get(("rs0a", qt - 1), 0))
                c = v.tensor_copy(rs0b[:, qt - 1:qt],
                                  rs0c[:, qt - 1:qt])
                rec(("rs0b", qt - 1), dve.tick(c))
        emit_x1tev(NQT - 1)
        v.wait_ge(act.sem, M.get(("rs0a", NQT - 1), 0))
        c = v.tensor_copy(rs0b[:, NQT - 1:NQT],
                          rs0c[:, NQT - 1:NQT])
        rec(("rs0b", NQT - 1), dve.tick(c))
        rec("x1t_all_ev", dve.n)
        for layer in range(2):
            s_lay = s0 if layer == 0 else s1
            OUT3 = B1T if layer == 0 else B2T
            v.wait_ge(pe.sem, M.get(("rs", layer), 0))
            v.tensor_scalar(rsrow[0:1, 0:512], bank_row_f32(4), 0.5, None,
                            op0=OP.mult)
            c = v.tensor_scalar(rsrow[0:1, 512:1024], bank_row_f32(5), 0.5,
                                None, op0=OP.mult)
            rec(("rshalf", layer), dve.tick(c))
            v.wait_ge(pe.sem, M.get(("rsb", layer), 0))
            vs(v.tensor_copy(rsb_sb, bank_f32(6, cols=1024)))
            rec(("rsbev", layer), dve.n)
            for gp in range(NGT // 2):
                v.wait_ge(pe.sem, M.get(("ffn_mm", layer, gp), 0))
                bA = 4 * (gp % 2)
                for j2 in range(2):
                    gt = 2 * gp + j2
                    for ch in range(2):
                        tb = tmp_ffn[2 * (gt % 2) + ch]
                        if gt >= 2 and gt - 2 <= BT_GT:
                            v.wait_ge(act.sem,
                                      M.get(("ffn_relu", layer, gt - 2, ch), 0))
                        c = v.tensor_sub(tb, bank_f32(bA + 2 * j2 + ch),
                                         rsb_sb[:, ch * 512:(ch + 1) * 512])
                        rec(("ffn_tt", layer, gt, ch), dve.tick(c))
                    if gt == BT_GT:
                        for ch in range(2):
                            v.wait_ge(act.sem,
                                      M.get(("ffn_relu", layer, gt, ch), 0))
                            c = v.tensor_scalar(
                                OUT3[:, gt, ch * 512:(ch + 1) * 512],
                                OUT3[:, gt, ch * 512:(ch + 1) * 512],
                                dinh[:, 0:1], None, op0=OP.mult)
                            dve.tick(c)
                    elif gt > BT_GT:
                        v.wait_ge(dve.sem, M.get(("ffn_tt", layer, gt, 1), 0))
                        for ch in range(2):
                            c = v.tensor_scalar(
                                OUT3[:, gt, ch * 512:(ch + 1) * 512],
                                tmp_ffn[2 * (gt % 2) + ch],
                                float(-16.0 * s_lay), 0.0,
                                op0=OP.mult, op1=OP.min)
                            dve.tick(c)
            rec(("ffn_ev_done", layer), dve.n)
        v.wait_ge(pe.sem, M.get("rs2", 0))
        v.tensor_copy(rsrow[0:1, 0:512], bank_row_f32(4))
        c = v.tensor_copy(rsrow[0:1, 512:1024], bank_row_f32(5))
        rec("rs2row_ev", dve.tick(c))
        for qt in range(NQT):
            v.wait_ge(pe.sem, M.get(("rs2col", qt), 0))
            c = v.tensor_scalar(bias2[:, qt:qt + 1], bank_f32(6, cols=1),
                                float(-s2), None, op0=OP.mult)
            rec(("rs2col_ev", qt), dve.tick(c))
        rec("bias2_ev", dve.n)
        for q4 in range(2):
            for cp in range(4):
                for ql in range(4):
                    qt = q4 * 4 + ql
                    fl = (q4 * 4 + cp) * 4 + ql
                    v.wait_ge(act.sem, M.get(("f2act", q4, cp, ql), 0))
                    v.wait_ge(sx.sem, M.get(("x1rel_q4s", q4), 0))
                    v.wait_ge(s16.sem, M.get(("x1rel_q4y", q4), 0))
                    c = v.tensor_add(z2g[ql][:, cp * 1024:(cp + 1) * 1024],
                                     x1rel[ql][:, cp * 1024:(cp + 1) * 1024],
                                     ffn_t[ql % 2])
                    rec(("f2tt", fl), dve.tick(c))
            for ql in range(4):
                qt = q4 * 4 + ql
                v.wait_ge(act.sem, M.get(("part2", qt, 3), 0))
                vs(v.tensor_reduce(msum, msp[:, 4 * ql:4 * ql + 4], AX.X,
                                   OP.add))
                vs(v.tensor_reduce(sqs, sqp[:, 4 * ql:4 * ql + 4], AX.X,
                                   OP.add))
                vs(v.tensor_scalar(m_sb, msum, 1.0 / 4096.0, None, op0=OP.mult))
                vs(v.tensor_mul(m2_sb, m_sb, m_sb))
                if qt > 0:
                    v.wait_ge(act.sem, M.get(("rstd2", qt - 1), 0))
                c = v.tensor_scalar(var_sb, sqs, 1.0 / 4096.0, m2_sb[:, 0:1],
                                    op0=OP.mult, op1=OP.subtract)
                rec(("var2_t", qt), dve.tick(c))
                v.wait_ge(act.sem, M.get(("rstd2", qt), 0))
                c = v.tensor_scalar(z2g[ql], z2g[ql], m_sb[:, 0:1], rstd[:, 0:1],
                                    op0=OP.subtract, op1=OP.mult)
                rec(("out_ready", qt), dve.tick(c))

    bodies = [("tensor", tensor_body), ("gpsimd", gpsimd_body),
              ("sync", sync_body), ("scalar", scalar_body),
              ("vector", vector_body)]

    for _ in range(2):
        for c in all_ctrs:
            c.reset()
        dry_eng = _DryEng()
        for _, body in bodies:
            body(dry_eng, True)
    totals = {id(c): c.n for c in all_ctrs}
    for c in all_ctrs:
        c.reset()
    streams = {}
    for nm, body in bodies:
        streams[nm] = []
        body(_RecEng(streams[nm]), True)
    semnames = {id(c.sem): n for n, c in
                [("g16", g16), ("gx0", gx[0]), ("gx1", gx[1]),
                 ("gq0", gq[0]), ("gq1", gq[1]), ("gm0", gm[0]),
                 ("gm1", gm[1]), ("gu0", gu[0]), ("gu1", gu[1]),
                 ("skt", skt), ("svf", svf), ("s16", s16), ("sw0", sw[0]),
                 ("sw1", sw[1]), ("st", st), ("cc", ccc), ("pe", pe),
                 ("act", act), ("dve", dve), ("tp", tp), ("sx", sx),
                 ("gsp0", gspl[0]), ("gsp1", gspl[1]), ("gsp2", gspl[2]),
                 ("sx0", sxl[0]), ("sx1", sxl[1]), ("sx2", sxl[2]),
                 ("gp1", gp1), ("so0", so[0]), ("so1", so[1]),
                 ("so2", so[2]), ("so3", so[3]), ("sv0", sv[0]),
                 ("sv1", sv[1]), ("sv2", sv[2]), ("sv3", sv[3])]}
    _simulate(streams, semnames)
    for c in all_ctrs:
        c.reset()
    with nc.Block("main", no_gpsimd_drain=True) as blk:
        blk.tensor(lambda e: tensor_body(e, False))
        blk.gpsimd(lambda e: gpsimd_body(e, False))
        blk.sync(lambda e: sync_body(e, False))
        blk.scalar(lambda e: scalar_body(e, False))
        blk.vector(lambda e: vector_body(e, False))
    for c in all_ctrs:
        assert totals[id(c)] == c.n, "emission mismatch vs dry run"
    stack.close()
    return nc


def kernel(L0_attn_x, L0_attn_Wqkv, L0_attn_bqkv, L0_attn_Wo, L0_attn_bo,
           L0_ffn_k0, L0_ffn_s0, L0_ffn_k1, L0_ffn_s1, L0_ffn_k2, L0_ffn_s2):
    x = np.asarray(L0_attn_x, dtype=np.float32)
    wq = np.ascontiguousarray(np.asarray(L0_attn_Wqkv, dtype=np.float32))
    bqkv = np.asarray(L0_attn_bqkv, dtype=np.float32)
    wo = np.ascontiguousarray(np.asarray(L0_attn_Wo, dtype=np.float32))
    bo = np.asarray(L0_attn_bo, dtype=np.float32)
    k0 = np.asarray(L0_ffn_k0)
    k1 = np.asarray(L0_ffn_k1)
    k2 = np.asarray(L0_ffn_k2)
    s0 = float(np.asarray(L0_ffn_s0))
    s1 = float(np.asarray(L0_ffn_s1))
    s2 = float(np.asarray(L0_ffn_s2))

    def blk4(w, nft):
        return np.ascontiguousarray(
            w.reshape(NCT, P, nft, P).transpose(2, 1, 0, 3))

    wq_blkA = blk4(wq, NFT)
    k0_blkA = blk4(np.ascontiguousarray(k0).astype(np.uint8), NGT)
    k1_blkA = blk4(np.ascontiguousarray(k1).astype(np.uint8), NGT)
    k2_u8 = np.ascontiguousarray(np.asarray(k2).astype(np.uint8))
    bqkv_col = np.ascontiguousarray(bqkv.reshape(NFT, P).T)
    bo_repA = np.ascontiguousarray(np.broadcast_to(bo, (P, 4096)))
    ident = np.eye(P, dtype=ml_dtypes.bfloat16)
    ones_col = np.ones((P, 1), dtype=ml_dtypes.bfloat16)
    ones_row = np.ones((1, P), dtype=np.float32)
    dinh_v = np.where(np.arange(P)[:, None] < BT_P, 1.0, -4.0).astype(np.float32)

    nc = build_nc(s0, s1, s2)
    in_maps = []
    for c in range(8):
        b, qs = c // 4, (c % 4) * 1024
        in_maps.append({
            "x": np.ascontiguousarray(x[b, qs:qs + 1024, :]),
            "wq_blk": wq_blkA, "wo": wo,
            "k0_blk": k0_blkA, "k1_blk": k1_blkA, "k2": k2_u8,
            "bqkv_col": bqkv_col, "bo_rep": bo_repA,
            "ident": ident, "ones_col": ones_col, "ones_row": ones_row,
            "dinh": dinh_v,
            "bsel": np.array([[b]], dtype=np.int32),
        })
    res = run_bass_kernel_spmd(nc, in_maps, core_ids=list(range(8)))
    out = np.empty((2, 4096, 4096), dtype=np.float32)
    for c in range(8):
        b, qs = c // 4, (c % 4) * 1024
        out[b, qs:qs + 1024, :] = res.results[c]["out"]
    return out


# version 22
# speedup vs baseline: 1.0270x; 1.0051x over previous
import numpy as np
import ml_dtypes
import concourse.bass as bass
import concourse.mybir as mybir
from concourse.bass_utils import run_bass_kernel_spmd

F32 = mybir.dt.float32
BF16 = mybir.dt.bfloat16
U8 = mybir.dt.uint8
I32 = mybir.dt.int32
AF = mybir.ActivationFunctionType
OP = mybir.AluOpType
AX = mybir.AxisListType

P = 128
NQT = 8
NCT = 32
NDT = 8
NFT = 24
NGT = 32
SCALE_ATT = 0.03125
EPS = 1e-5
BT_GT = 25
BT_P = 77
KB = 1024

OFF_IDENT = 0
OFF_BQKV = 256
OFF_ONEC = 384
OFF_ONER = 416
OFF_DINH = 928
OFF_BSEL = 960
OFF_MAXP = 992
OFF_NEGM = 1024
OFF_SEP = 1056
OFF_SE = 1088
OFF_RECIP = 1120
OFF_MSUM = 1152
OFF_M = 1184
OFF_M2 = 1216
OFF_VAR = 1248
OFF_RSTD = 1280
OFF_SQS = 1312
OFF_RS0 = 1312
OFF_MSP = 1344
OFF_RSROW = 1472
OFF_BIAS2 = 5568
OFF_SQP = 5696
CONSTS_END = 6144
R0 = CONSTS_END
R1 = R0 + 64 * KB
R2 = R1 + 64 * KB
R3 = R2 + 16 * KB
R4 = R3 + 16 * KB
R5 = R4 + 24 * KB
SB_END = R5 + 8 * KB + 512

QKV_ORDER = list(range(8, 16)) + list(range(16, 24)) + list(range(0, 8))


class _DryIns:
    def then_inc(self, sem, k=1):
        return self


_DRY = _DryIns()


class _DryEng:

    def __getattr__(self, name):
        return lambda *a, **k: _DRY


class _RecIns:
    def __init__(self, log):
        self.log = log

    def then_inc(self, sem, k=1):
        self.log.append(("inc", id(sem), k))
        return self


class _RecEng:

    def __init__(self, stream):
        self.stream = stream

    def __getattr__(self, name):
        if name == "wait_ge":
            def w(sem, n):
                self.stream.append(("wait", id(sem), n))
            return w

        def op(*a, **kw):
            log = []
            self.stream.append(("op", log))
            return _RecIns(log)
        return op


def _simulate(streams, semnames):
    vals = {}
    pcs = {e: 0 for e in streams}
    progress = True
    while progress:
        progress = False
        for e, ops in streams.items():
            while pcs[e] < len(ops):
                kind, a, *rest = ops[pcs[e]]
                if kind == "wait":
                    if vals.get(a, 0) < rest[0]:
                        break
                    pcs[e] += 1
                    progress = True
                else:
                    for _, semid, k in a:
                        vals[semid] = vals.get(semid, 0) + k
                    pcs[e] += 1
                    progress = True
    stuck = {e: pcs[e] for e in streams if pcs[e] < len(streams[e])}
    if stuck:
        msg = ["DEADLOCK DETECTED in program graph:"]
        for e, pc in stuck.items():
            kind, semid, n = streams[e][pc]
            msg.append(f"  {e} blocked at op#{pc} waiting "
                       f"{semnames.get(semid, semid)} >= {n} "
                       f"(currently {vals.get(semid, 0)})")
        raise RuntimeError("\n".join(msg))


class Ctr:
    def __init__(self, sem):
        self.sem = sem
        self.n = 0

    def tick(self, ins, k=1):
        ins.then_inc(self.sem, k)
        self.n += k
        return self.n

    def reset(self):
        self.n = 0


def build_nc(s0, s1, s2, debug=False):
    nc = bass.Bass("TRN2", target_bir_lowering=False)

    x_in = nc.dram_tensor("x", [1024, 4096], F32, kind="ExternalInput")
    wq_blk = nc.dram_tensor("wq_blk", [NFT, P, NCT, P], F32, kind="ExternalInput")
    wo_in = nc.dram_tensor("wo", [1024, 4096], F32, kind="ExternalInput")
    k0_blk = nc.dram_tensor("k0_blk", [NGT, P, NCT, P], U8, kind="ExternalInput")
    k1_blk = nc.dram_tensor("k1_blk", [NGT, P, NCT, P], U8, kind="ExternalInput")
    k2_in = nc.dram_tensor("k2", [4096, 4096], U8, kind="ExternalInput")
    bqkv_in = nc.dram_tensor("bqkv_col", [P, NFT], F32, kind="ExternalInput")
    bo_in = nc.dram_tensor("bo_rep", [P, 4096], F32, kind="ExternalInput")
    ident_in = nc.dram_tensor("ident", [P, P], BF16, kind="ExternalInput")
    onec_in = nc.dram_tensor("ones_col", [P, 1], BF16, kind="ExternalInput")
    oner_in = nc.dram_tensor("ones_row", [1, P], F32, kind="ExternalInput")
    dinh_in = nc.dram_tensor("dinh", [P, 1], F32, kind="ExternalInput")
    bsel_in = nc.dram_tensor("bsel", [1, 1], I32, kind="ExternalInput")
    out_dr = nc.dram_tensor("out", [1024, 4096], F32, kind="ExternalOutput")

    taps = {}
    if debug:
        for nm, shp, dt_ in [
                ("tap_xt", [P, 32768], BF16), ("tap_qt", [P, 8192], BF16),
                ("tap_kv", [P, 8192], BF16), ("tap_vs", [P, 8192], BF16),
                ("tap_kt", [P, 32768], BF16), ("tap_vf", [P, 32768], BF16),
                ("tap_p", [P, 4096], BF16), ("tap_ot", [P, 8192], BF16),
                ("tap_x1", [1024, 4096], BF16), ("tap_b1", [P, 32768], BF16),
                ("tap_b2", [P, 32768], BF16), ("tap_rs", [1, 1024], F32)]:
            taps[nm] = nc.dram_tensor(nm, shp, dt_, kind="ExternalOutput")
    wobf = nc.dram_tensor("wobf", [1024, 4096], BF16)
    cc_in_k = nc.dram_tensor("cc_in_k", [1024, 1024], BF16)
    cc_in_v = nc.dram_tensor("cc_in_v", [1024, 1024], BF16)
    cc_out_k = nc.dram_tensor("cc_out_k", [8 * 1024, 1024], BF16,
                              addr_space="Shared")
    cc_out_v = nc.dram_tensor("cc_out_v", [8 * 1024, 1024], BF16,
                              addr_space="Shared")
    x1sp = nc.dram_tensor("x1sp", [1024, 4096], BF16)
    xbfd = nc.dram_tensor("xbfd", [1024, 4096], BF16)

    sb = nc.alloc_sbuf_tensor("sbarena", [P, SB_END], U8)
    psa = nc.alloc_psum_tensor("psarena", [P, 16384], U8)

    def sbv(off, cols, dt, p0=0, np_=P):
        bpe = mybir.dt.size(dt)
        return sb[p0:p0 + np_, off:off + cols * bpe].bitcast(dt)

    def sb3(off, n0, n1, dt):
        bpe = mybir.dt.size(dt)
        return sb[:, off:off + n0 * n1 * bpe].bitcast(dt).rearrange(
            "p (a b) -> p a b", a=n0)

    def bank_f32(b, cols=512):
        return psa[:, b * 2048: b * 2048 + cols * 4].bitcast(F32)

    def bank_slot_bf16(b, slot):
        o = b * 2048 + slot * 256
        return psa[:, o:o + 256].bitcast(BF16)

    def bank_row_f32(b, cols=512):
        return psa[0:1, b * 2048: b * 2048 + cols * 4].bitcast(F32)

    def bank3_bf16(b, n0, n1):
        return psa[:, b * 2048: b * 2048 + n0 * n1 * 2].bitcast(BF16).rearrange(
            "p (a b) -> p a b", a=n0)

    idt = sbv(OFF_IDENT, P, BF16)
    bqkv = sbv(OFF_BQKV, NFT, F32)
    onec = sbv(OFF_ONEC, 1, BF16)
    oner = sbv(OFF_ONER, P, F32, np_=1)
    dinh = sbv(OFF_DINH, 1, F32)
    bsel_sb = sb[0:1, OFF_BSEL:OFF_BSEL + 4].bitcast(I32)
    maxp = sbv(OFF_MAXP, 8, F32)
    negm = sbv(OFF_NEGM, 1, F32)
    sep = sbv(OFF_SEP, 8, F32)
    sumexp = sbv(OFF_SE, 1, F32)
    recip = sbv(OFF_RECIP, 1, F32)
    msum = sbv(OFF_MSUM, 1, F32)
    m_sb = sbv(OFF_M, 1, F32)
    m2_sb = sbv(OFF_M2, 1, F32)
    var_sb = sbv(OFF_VAR, 1, F32)
    rstd = sbv(OFF_RSTD, 1, F32)
    sqs = sbv(OFF_SQS, 1, F32)
    rs0c = sbv(5600, 8, F32)
    rs0b = sbv(5664, 8, BF16)
    msp = sbv(OFF_MSP, 32, F32)
    sqp = sbv(OFF_SQP, 16, F32)
    rsrow = sbv(OFF_RSROW, 1024, F32, np_=1)
    bias2 = sbv(OFF_BIAS2, 8, F32)

    XT = sb3(R0, NCT, 1024, BF16)
    VF = sb3(R0, NCT, 1024, BF16)
    X1T = sb3(R0, NCT, 1024, BF16)
    B2T = sb3(R0, NGT, 1024, BF16)

    xbf = [sbv(R1 + i * 8 * KB, 4096, BF16) for i in range(4)]
    kvK = sb3(R1 + 32 * KB, NDT, 1024, BF16)
    vstage = sb3(R1 + 48 * KB, NDT, 1024, BF16)
    KT = sb3(R1, NDT, 4096, BF16)
    zbf8 = [sbv(R1 + i * 8 * KB, 4096, BF16) for i in range(NQT)]
    wopan3 = [sb3(R2 + i * 4 * KB, 4, 512, BF16) for i in range(2)]
    B1T = sb3(R1, NGT, 1024, BF16)
    z2g = [sbv(R1 + i * 16 * KB, 4096, F32) for i in range(4)]

    QT = sb3(R2, NDT, 1024, BF16)
    bo_rep = sbv(R2, 4096, F32)
    sq_dump2 = sbv(R2, 4096, BF16)

    bo_bf = sbv(R5, 4096, BF16, np_=1)
    ones_bf = sbv(R5 + 8 * KB, P, BF16, np_=1)
    OT = sb3(R3, NQT, NDT * P, BF16)
    mkpan2 = [sb3(R3, 2, 4096, BF16), sb3(R2, 2, 4096, BF16)]
    wqpan = [sbv(R3 + i * 8 * KB, 4096, BF16) for i in range(2)]

    vt_tmp = [sbv(R4 + i * 2 * KB, 1024, BF16) for i in range(2)]
    P_sb = sbv(R4, 4096, BF16)
    PT = sb3(R4 + 8 * KB, NCT, P, BF16)
    O_bf = sbv(R4 + 16 * KB, 1024, BF16)
    xgs = [sb3(R4 + i * 8 * KB, NQT, 512, BF16) for i in range(2)]
    x1bf = [sbv(R4 + i * 8 * KB, 4096, BF16) for i in range(2)]
    sq_dump1 = sbv(R4 + 16 * KB, 4096, BF16)
    rsb_sb = sbv(R4, 1024, F32)
    tmp_ffn = [sbv(R4 + 4 * KB + i * 2 * KB, 512, F32) for i in range(4)]
    u2pan2 = [sb3(R3 + i * 4 * KB, 2, 1024, BF16) for i in range(2)]
    x1rel = [sbv(R3 + 8 * KB, 4096, BF16), sbv(R2 + 8 * KB, 4096, BF16),
             sbv(R4, 4096, BF16), sbv(R4 + 8 * KB, 4096, BF16)]
    ffn_t = [sbv(R4 + 16 * KB + i * 4 * KB, 1024, F32) for i in range(2)]

    import contextlib
    stack = contextlib.ExitStack()
    sems = {}
    for name in ["g16", "gx0", "gx1", "gq0", "gq1", "gm0", "gm1", "gu0",
                 "gu1", "skt", "svf", "s16", "sw0", "sw1", "st", "cc", "pe",
                 "act", "dve", "tp", "sx", "gsp0", "gsp1", "gsp2",
                 "sx0", "sx1", "sx2", "gp1", "so0", "so1", "so2", "so3",
                 "sv0", "sv1", "sv2", "sv3"]:
        sems[name] = stack.enter_context(nc.semaphore(name))
    g16 = Ctr(sems["g16"])
    gx = [Ctr(sems["gx0"]), Ctr(sems["gx1"])]
    gq = [Ctr(sems["gq0"]), Ctr(sems["gq1"])]
    gm = [Ctr(sems["gm0"]), Ctr(sems["gm1"])]
    gu = [Ctr(sems["gu0"]), Ctr(sems["gu1"])]
    skt = Ctr(sems["skt"])
    svf = Ctr(sems["svf"])
    s16 = Ctr(sems["s16"])
    sw = [Ctr(sems["sw0"]), Ctr(sems["sw1"])]
    st = Ctr(sems["st"])
    ccc = Ctr(sems["cc"])
    pe = Ctr(sems["pe"])
    act = Ctr(sems["act"])
    dve = Ctr(sems["dve"])
    tp = Ctr(sems["tp"])
    sx = Ctr(sems["sx"])
    gspl = [Ctr(sems["gsp0"]), Ctr(sems["gsp1"]), Ctr(sems["gsp2"])]
    sxl = [Ctr(sems["sx0"]), Ctr(sems["sx1"]), Ctr(sems["sx2"])]
    gp1 = Ctr(sems["gp1"])
    so = [Ctr(sems["so%d" % i]) for i in range(4)]
    sv = [Ctr(sems["sv%d" % i]) for i in range(4)]
    all_ctrs = [g16, gx[0], gx[1], gq[0], gq[1], gm[0], gm[1], gu[0], gu[1],
                skt, svf, s16, sw[0], sw[1], st, ccc, pe, act, dve, tp, sx,
                gspl[0], gspl[1], gspl[2], sxl[0], sxl[1], sxl[2], gp1,
                so[0], so[1], so[2], so[3], sv[0], sv[1], sv[2], sv[3]]

    M = {}

    def rec(key, val):
        if key in M:
            assert M[key] == val, f"milestone {key} changed: {M[key]} -> {val}"
        M[key] = val

    def tensor_body(t, dry):
        for xt in range(NQT):
            par = xt % 2
            t.wait_ge(gx[par].sem, 16 * (xt // 2 + 1))
            for bk in range(4):
                ti = 4 * xt + bk
                if ti >= 2:
                    t.wait_ge(dve.sem, M.get(("x2tev", ti - 2), 0))
                last = None
                for j in range(8):
                    ct = bk * 8 + j
                    last = t.transpose(bank_slot_bf16(bk % 2, j),
                                       xbf[xt % 4][:, ct * P:(ct + 1) * P], idt)
                rec(("x2t", ti), pe.tick(last))
        t.wait_ge(dve.sem, M.get(("x2tev", 31), 0))
        pend_vt = []
        for fi, ft in enumerate(QKV_ORDER):
            t.wait_ge(gq[fi % 2].sem, 16 * (fi // 2 + 1))
            if fi >= 2:
                t.wait_ge(act.sem, M.get(("qkvev", fi - 2), 0))
            bA = 2 + 2 * (fi % 2)
            last = None
            for ct in range(NCT):
                lhs = wqpan[fi % 2][:, ct * P:(ct + 1) * P]
                t.matmul(bank_f32(bA), lhs, XT[:, ct, 0:512],
                         start=ct == 0, stop=ct == NCT - 1)
                last = t.matmul(bank_f32(bA + 1), lhs, XT[:, ct, 512:1024],
                                start=ct == 0, stop=ct == NCT - 1)
            rec(("qkv", fi), pe.tick(last))
            if pend_vt:
                fv, fvi = pend_vt.pop(0)
                dtv = fv - 16
                t.wait_ge(act.sem, M.get(("qkvev", fvi), 0))
                if dtv >= 2:
                    t.wait_ge(dve.sem, M.get(("vtev", dtv - 2), 0))
                lastv = None
                for tt in range(8):
                    lastv = t.transpose(bank_slot_bf16(dtv % 2, tt),
                                        vt_tmp[fvi % 2][:, tt * P:(tt + 1) * P],
                                        idt)
                rec(("vt", dtv), pe.tick(lastv))
            if 16 <= ft < 24:
                pend_vt.append((ft, fi))
        while pend_vt:
            fv, fvi = pend_vt.pop(0)
            dtv = fv - 16
            t.wait_ge(act.sem, M.get(("qkvev", fvi), 0))
            if dtv >= 2:
                t.wait_ge(dve.sem, M.get(("vtev", dtv - 2), 0))
            lastv = None
            for tt in range(8):
                lastv = t.transpose(bank_slot_bf16(dtv % 2, tt),
                                    vt_tmp[fvi % 2][:, tt * P:(tt + 1) * P], idt)
            rec(("vt", dtv), pe.tick(lastv))
        rec("qkv_done", pe.n)
        for qt in range(NQT):
            if qt == 0:
                t.wait_ge(skt.sem, 64)
                t.wait_ge(act.sem, M.get(("qkvev", 23), 0))
                t.wait_ge(dve.sem, M.get("vstage_done", 0))
            else:
                t.wait_ge(act.sem, M.get(("exp", qt - 1, 7), 0))
                t.wait_ge(dve.sem, M.get(("otev", qt - 1), 0))
            for kcg in range(4):
                last = None
                for dt_ in range(NDT):
                    lhs = QT[:, dt_, qt * P:(qt + 1) * P]
                    t.matmul(bank_f32(2 * kcg), lhs,
                             KT[:, dt_, kcg * 1024:kcg * 1024 + 512],
                             start=dt_ == 0, stop=dt_ == NDT - 1)
                    last = t.matmul(bank_f32(2 * kcg + 1), lhs,
                                    KT[:, dt_, kcg * 1024 + 512:(kcg + 1) * 1024],
                                    start=dt_ == 0, stop=dt_ == NDT - 1)
                rec(("scores_p", qt, kcg), pe.tick(last))
            if qt == NQT - 1:
                rec("scores_done", pe.n)
            for rnd in range(4):
                t.wait_ge(act.sem, M.get(("exp", qt, 2 * rnd + 1), 0))
                lastp = None
                for j in range(8):
                    kt = rnd * 8 + j
                    lastp = t.transpose(bank_slot_bf16(rnd, j),
                                        P_sb[:, kt * P:(kt + 1) * P], idt)
                rec(("pt", qt, rnd), pe.tick(lastp))
            t.wait_ge(act.sem, M.get(("exp", qt, 7), 0))
            last = None
            for kt in range(NCT):
                if kt % 8 == 0:
                    t.wait_ge(dve.sem, M.get(("ptev", qt, kt // 8), 0))
                    if qt == 0:
                        t.wait_ge(sv[kt // 8].sem, 16)
                t.matmul(bank_f32(4), PT[:, kt, :], VF[:, kt, 0:512],
                         start=kt == 0, stop=kt == NCT - 1)
                last = t.matmul(bank_f32(5), PT[:, kt, :], VF[:, kt, 512:1024],
                                start=kt == 0, stop=kt == NCT - 1)
            rec(("attnv", qt), pe.tick(last))
            t.wait_ge(dve.sem, M.get(("oev", qt), 0))
            lasto = None
            for dt_ in range(NDT):
                lasto = t.transpose(bank_slot_bf16(6, dt_),
                                    O_bf[:, dt_ * P:(dt_ + 1) * P], idt)
            rec(("ot", qt), pe.tick(lasto))
        rec("att_done", pe.n)
        pan_i = 0
        for cq in range(8):
            for pp in range(2):
                par = pan_i % 2
                t.wait_ge(sw[par].sem, 16 * (pan_i // 2 + 1))
                if pan_i == 0:
                    t.wait_ge(g16.sem, M.get("bo_ld", 0))
                    t.wait_ge(dve.sem, M.get(("otev", NQT - 1), 0))
                if pp == 0:
                    for qt in range(NQT):
                        if cq >= 1:
                            t.wait_ge(dve.sem, M.get(("zadd", cq - 1, qt), 0))
                        t.matmul(bank_f32(qt), ones_bf,
                                 bo_bf[0:1, cq * 512:(cq + 1) * 512],
                                 start=True, stop=False)
                lastm = None
                for a in range(4):
                    dt_ = 4 * pp + a
                    for qt in range(NQT):
                        lhs = OT[:, qt, dt_ * P:(dt_ + 1) * P]
                        lastm = t.matmul(bank_f32(qt), lhs,
                                         wopan3[par][:, a, :],
                                         start=False, stop=dt_ == NDT - 1)
                rec(("wo_pan", pan_i), pe.tick(lastm))
                pan_i += 1
            rec(("wo_cq", cq), pe.n)
        rec("wo_done", pe.n)
        for qt in range(NQT):
            for bk in range(4):
                t.wait_ge(dve.sem, M.get(("x1bf_c", qt, bk), 0))
                ti = 4 * qt + bk
                if ti >= 2:
                    t.wait_ge(dve.sem, M.get(("x1tev", ti - 2), 0))
                lastt = None
                for j in range(8):
                    ct = bk * 8 + j
                    lastt = t.transpose(bank_slot_bf16(6 + bk % 2, j),
                                        x1bf[qt % 2][:, ct * P:(ct + 1) * P],
                                        idt)
                rec(("x1t", qt, bk), pe.tick(lastt))
        for layer in range(2):
            IN3 = X1T if layer == 0 else B1T
            if layer == 0:
                t.wait_ge(dve.sem, M.get("x1t_all_ev", 0))
                last = None
                for qt in range(NQT):
                    t.wait_ge(dve.sem, M.get(("rs0b", qt), 0))
                    bb = 4 + qt // 4
                    cc = (qt % 4) * P
                    last = t.matmul(
                        bank_row_f32(bb)[0:1, cc:cc + P],
                        rs0b[:, qt:qt + 1], idt,
                        start=True, stop=True)
            else:
                t.wait_ge(dve.sem, M.get(("ffn_ev_done", 0), 0))
                t.wait_ge(act.sem, M.get(("ffn_act_done", 0), 0))
                last = None
                for ct in range(NCT):
                    t.matmul(bank_row_f32(4), onec, IN3[:, ct, 0:512],
                             start=ct == 0, stop=ct == NCT - 1)
                    last = t.matmul(bank_row_f32(5), onec, IN3[:, ct, 512:1024],
                                    start=ct == 0, stop=ct == NCT - 1)
            rec(("rs", layer), pe.tick(last))
            t.wait_ge(dve.sem, M.get(("rshalf", layer), 0))
            t.matmul(bank_f32(6), oner, rsrow[0:1, 0:512], start=True, stop=True)
            last = t.matmul(bank_f32(7), oner, rsrow[0:1, 512:1024],
                            start=True, stop=True)
            rec(("rsb", layer), pe.tick(last))
            for gp in range(NGT // 2):
                par = gp % 2
                t.wait_ge(gm[par].sem, 16 * (layer * 8 + gp // 2 + 1))
                if gp >= 2:
                    t.wait_ge(dve.sem, M.get(("ffn_tt", layer, 2 * gp - 3, 1), 0))
                elif gp == 1:
                    t.wait_ge(dve.sem, M.get(("rsbev", layer), 0))
                bA = 4 * (gp % 2)
                last = None
                for ct in range(NCT):
                    for j2 in range(2):
                        lhs = mkpan2[par][:, j2, ct * P:(ct + 1) * P]
                        t.matmul(bank_f32(bA + 2 * j2), lhs, IN3[:, ct, 0:512],
                                 start=ct == 0, stop=ct == NCT - 1)
                        last = t.matmul(bank_f32(bA + 2 * j2 + 1), lhs,
                                        IN3[:, ct, 512:1024],
                                        start=ct == 0, stop=ct == NCT - 1)
                rec(("ffn_mm", layer, gp), pe.tick(last))
            rec(("ffn_mm_done", layer), pe.n)
        t.wait_ge(dve.sem, M.get(("ffn_ev_done", 1), 0))
        t.wait_ge(act.sem, M.get(("ffn_act_done", 1), 0))
        last = None
        for gt in range(NGT):
            t.matmul(bank_row_f32(4), onec, B2T[:, gt, 0:512],
                     start=gt == 0, stop=gt == NGT - 1)
            last = t.matmul(bank_row_f32(5), onec, B2T[:, gt, 512:1024],
                            start=gt == 0, stop=gt == NGT - 1)
        rec("rs2", pe.tick(last))
        t.wait_ge(dve.sem, M.get("rs2row_ev", 0))
        lastc = None
        for qt in range(NQT):
            lastc = t.matmul(bank_f32(qt, cols=1),
                             rsrow[0:1, qt * P:(qt + 1) * P],
                             oner[0:1, 0:1], start=True, stop=True)
        rec("rs2col_all", pe.tick(lastc))
        pan2 = 0
        for q4 in range(2):
            for cp in range(4):
                for gt in range(NGT):
                    pr = pan2 % 2
                    j2 = gt % 2
                    if j2 == 0:
                        t.wait_ge(gu[pr].sem, 16 * (pan2 // 2 + 1))
                    if gt == 0 and q4 == 0 and cp == 0:
                        t.wait_ge(dve.sem, M.get("bias2_ev", 0))
                    last = None
                    for ql in range(4):
                        qt = q4 * 4 + ql
                        if gt == 0 and (q4 > 0 or cp > 0):
                            pq4, pcp = (q4, cp - 1) if cp > 0 else (q4 - 1, 3)
                            t.wait_ge(act.sem, M.get(("f2act", pq4, pcp, ql), 0))
                        t.matmul(bank_f32(2 * ql),
                                 B2T[:, gt, qt * P:(qt + 1) * P],
                                 u2pan2[pr][:, j2, 0:512],
                                 start=gt == 0, stop=gt == NGT - 1)
                        last = t.matmul(bank_f32(2 * ql + 1),
                                        B2T[:, gt, qt * P:(qt + 1) * P],
                                        u2pan2[pr][:, j2, 512:1024],
                                        start=gt == 0, stop=gt == NGT - 1)
                    rec(("f2g", pan2, j2), pe.tick(last))
                    if j2 == 1:
                        pan2 += 1
                rec(("f2mm", q4, cp), pe.n)
        rec("pe_end", pe.n)

    def gpsimd_body(g, dry):
        for ap_out, ap_in in [(idt, ident_in[:, :]), (bqkv, bqkv_in[:, :]),
                              (onec, onec_in[:, :]), (oner, oner_in[:, :]),
                              (dinh, dinh_in[:, :]), (bsel_sb, bsel_in[:, :])]:
            g16.tick(g.dma_start(out=ap_out, in_=ap_in), 16)
        rec("consts_ld", g16.n)
        g16.tick(g.dma_start(out=wobf[:, :], in_=wo_in[:, :]), 16)
        rec("wobf_ld", g16.n)
        for xt in range(NQT):
            if xt >= 4:
                g.wait_ge(pe.sem, M.get(("x2t", 4 * (xt - 4) + 3), 0))
                g.wait_ge(sxl[2].sem, 16 * (xt - 3))
            gx[xt % 2].tick(g.dma_start(out=xbf[xt % 4],
                                        in_=x_in[xt * P:(xt + 1) * P, :]), 16)
        def gather(gg, b, count, kv, ranks=range(4)):
            base = b * 4 * 1024
            if kv == 0:
                for r in ranks:
                    ins = gg.dma_start(
                        out=KT[:, :, r * 1024:(r + 1) * 1024],
                        in_=cc_out_k[base + r * 1024: base + (r + 1) * 1024, :]
                        .rearrange("(dt p) q -> p dt q", p=P))
                    if count:
                        skt.tick(ins, 16)
                    else:
                        ins.then_inc(skt.sem, 16)
            else:
                for r in ranks:
                    ins = gg.dma_start(
                        out=VF[:, r * 8:(r + 1) * 8, :],
                        in_=cc_out_v[base + r * 1024: base + (r + 1) * 1024, :]
                        .rearrange("(tt p) d -> p tt d", p=P))
                    if count:
                        sv[r].tick(ins, 16)
                    else:
                        ins.then_inc(sv[r].sem, 16)

        def branch_gather(kv, ranks, tag):
            if dry:
                gather(g, 0, True, kv, ranks)
            else:
                with g.register("rb%s" % tag) as rb:
                    g.reg_load(rb, bsel_sb[0:1, 0:1])
                    with g.If_eq(rb, 0):
                        gather(g, 0, True, kv, ranks)
                    with g.If_eq(rb, 1):
                        gather(g, 1, False, kv, ranks)

        for fi, ft in enumerate(QKV_ORDER):
            if fi >= 2:
                g.wait_ge(pe.sem, M.get(("qkv", fi - 2), 0))
            gq[fi % 2].tick(
                g.dma_start(out=wqpan[fi % 2],
                            in_=wq_blk[ft].rearrange("p a b -> p (a b)")), 16)
            if fi == 11:
                g.wait_ge(act.sem, M.get("actK_done", 0))
                g16.tick(g.dma_start(
                    out=cc_in_k[:, :].rearrange("(dt p) q -> p dt q", p=P),
                    in_=kvK[:, :, :]), 16)
                g.wait_ge(g16.sem, g16.n)
                ccc.tick(g.collective_compute(
                    "AllGather", OP.bypass, replica_groups=[list(range(8))],
                    ins=[cc_in_k[:, :]], outs=[cc_out_k[:, :]]))
            if fi == 17:
                g.wait_ge(dve.sem, M.get("vstage_done", 0))
                g16.tick(g.dma_start(
                    out=cc_in_v[:, :].rearrange("(tt p) d -> p tt d", p=P),
                    in_=vstage[:, :, :]), 16)
                g.wait_ge(g16.sem, g16.n)
                ccc.tick(g.collective_compute(
                    "AllGather", OP.bypass, replica_groups=[list(range(8))],
                    ins=[cc_in_v[:, :]], outs=[cc_out_v[:, :]]))
            if 18 <= fi <= 21:
                if fi == 18:
                    g.wait_ge(ccc.sem, 1)
                branch_gather(0, [fi - 18], "k%d" % (fi - 18))
        g.wait_ge(ccc.sem, 2)
        g.wait_ge(pe.sem, M.get("qkv_done", 0))
        branch_gather(1, range(4), "v")
        g16.tick(g.dma_start(out=bo_bf, in_=bo_in[0:1, :]), 16)
        g16.tick(g.dma_start(out=ones_bf, in_=oner_in[:, :]), 16)
        rec("bo_ld", g16.n)
        for layer in range(2):
            kblk = k0_blk if layer == 0 else k1_blk
            for gp in range(NGT // 2):
                li = layer * (NGT // 2) + gp
                par = gp % 2
                if li >= 2:
                    pl, pg = divmod(li - 2, NGT // 2)
                    g.wait_ge(pe.sem, M.get(("ffn_mm", pl, pg), 0))
                else:
                    g.wait_ge(pe.sem, M.get("wo_done", 0))
                gm[par].tick(g.dma_start(
                    out=mkpan2[par][:, :, :],
                    in_=kblk[2 * gp:2 * gp + 2].rearrange(
                        "a p c j -> p a (c j)")), 16)
        pan2 = 0
        for q4 in range(2):
            for cp in range(4):
                for g2 in range(NGT // 2):
                    par = pan2 % 2
                    if pan2 >= 2:
                        g.wait_ge(pe.sem, M.get(("f2g", pan2 - 2, 1), 0))
                    else:
                        g.wait_ge(pe.sem, M.get(("ffn_mm", 1, 14), 0))
                    gu[par].tick(g.dma_start(
                        out=u2pan2[par][:, :, :],
                        in_=k2_in[g2 * 256:(g2 + 1) * 256,
                                  cp * 1024:(cp + 1) * 1024].rearrange(
                                      "(a p) c -> p a c", p=P)), 16)
                    pan2 += 1

    def sync_body(sy, dry):
        for xt in range(NQT):
            sy.wait_ge(gx[xt % 2].sem, 16 * (xt // 2 + 1))
            sxl[2].tick(sy.dma_start(out=xbfd[xt * P:(xt + 1) * P, :],
                                     in_=xbf[xt % 4]), 16)
        pan_i = 0
        for cq in range(8):
            for pp in range(2):
                par = pan_i % 2
                if pan_i == 0:
                    sy.wait_ge(g16.sem, M.get("wobf_ld", 0))
                    sy.wait_ge(pe.sem, M.get("scores_done", 0))
                if pan_i >= 2:
                    sy.wait_ge(pe.sem, M.get(("wo_pan", pan_i - 2), 0))
                sw[par].tick(sy.dma_start(
                    out=wopan3[par][:, :, :],
                    in_=wobf[pp * 512:(pp + 1) * 512,
                             cq * 512:(cq + 1) * 512].rearrange(
                                 "(a p) c -> p a c", p=P)), 16)
                pan_i += 1
        for qt in range(NQT):
            sy.wait_ge(dve.sem, M.get(("x1bf_c", qt, 3), 0))
            gspl[qt % 2].tick(sy.dma_start(out=x1sp[qt * P:(qt + 1) * P, :],
                                           in_=x1bf[qt % 2]), 16)
            rec(("spill", qt), gspl[qt % 2].n)
        for _g in range(2):
            rec(("spill_last", _g), gspl[_g].n)
        rec("wopan_done", s16.n)
        for q4 in range(2):
            for ql in (0, 1):
                qt = q4 * 4 + ql
                if q4 > 0:
                    sy.wait_ge(so[ql].sem, 16 * q4)
                else:
                    sy.wait_ge(pe.sem, M.get(("ffn_mm_done", 1), 0))
                    for _g in range(2):
                        sy.wait_ge(gspl[_g].sem, M.get(("spill_last", _g), 0))
                s16.tick(sy.dma_start(out=x1rel[ql],
                                      in_=x1sp[qt * P:(qt + 1) * P, :]), 16)
            rec(("x1rel_q4y", q4), s16.n)
            for ql in (0, 2):
                qt = q4 * 4 + ql
                sy.wait_ge(dve.sem, M.get(("out_ready", qt), 0))
                so[ql].tick(sy.dma_start(out=out_dr[qt * P:(qt + 1) * P, :],
                                         in_=z2g[ql]), 16)
        for ql in range(4):
            sy.wait_ge(so[ql].sem, so[ql].n)

    def scalar_body(s, dry):
        for fi, ft in enumerate(QKV_ORDER):
            s.wait_ge(pe.sem, M.get(("qkv", fi), 0))
            bA = 2 + 2 * (fi % 2)
            if 8 <= ft < 16:
                d0 = kvK[:, ft - 8, :]
            elif ft >= 16:
                if ft >= 18:
                    s.wait_ge(pe.sem, M.get(("vt", ft - 18), 0))
                d0 = vt_tmp[fi % 2][:, :]
            else:
                d0 = QT[:, ft, :]
            s.activation(d0[:, 0:512], bank_f32(bA), AF.Identity,
                         bias=bqkv[:, ft:ft + 1], scale=np.float32(1.0))
            a2 = s.activation(d0[:, 512:1024], bank_f32(bA + 1), AF.Identity,
                              bias=bqkv[:, ft:ft + 1], scale=np.float32(1.0))
            rec(("qkvev", fi), act.tick(a2))
            if fi == 7:
                rec("actK_done", act.n)
        if debug:
            s.wait_ge(dve.sem, M.get("vstage_done", 0))
            tp.tick(s.dma_start(out=taps["tap_qt"][:, :],
                                in_=QT.rearrange("p a b -> p (a b)")), 16)
            tp.tick(s.dma_start(out=taps["tap_kv"][:, :],
                                in_=kvK.rearrange("p a b -> p (a b)")), 16)
            tp.tick(s.dma_start(out=taps["tap_vs"][:, :],
                                in_=vstage.rearrange("p a b -> p (a b)")), 16)
            s.wait_ge(dve.sem, M.get(("x2tev", 31), 0))
            tp.tick(s.dma_start(out=taps["tap_xt"][:, :],
                                in_=XT.rearrange("p a b -> p (a b)")), 16)
            s.wait_ge(skt.sem, 64)
            for _r in range(4):
                s.wait_ge(sv[_r].sem, 16)
            tp.tick(s.dma_start(out=taps["tap_kt"][:, :],
                                in_=KT.rearrange("p a b -> p (a b)")), 16)
            tp.tick(s.dma_start(out=taps["tap_vf"][:, :],
                                in_=VF.rearrange("p a b -> p (a b)")), 16)
        for qt in range(NQT):
            s.wait_ge(dve.sem, M.get(("negm_t", qt), 0))
            for j in range(8):
                a = s.activation(P_sb[:, j * 512:(j + 1) * 512], bank_f32(j),
                                 AF.Exp, bias=negm[:, 0:1], scale=np.float32(SCALE_ATT),
                                 accum_out=sep[:, j:j + 1])
                rec(("exp", qt, j), act.tick(a))
        if debug:
            tp.tick(s.dma_start(out=taps["tap_p"][:, :], in_=P_sb), 16)
            s.wait_ge(dve.sem, M.get(("otev", 7), 0))
            tp.tick(s.dma_start(out=taps["tap_ot"][:, :],
                                in_=OT.rearrange("p a b -> p (a b)")), 16)
        def xgs_load(cq):
            if cq < 2:
                s.wait_ge(pe.sem, M.get(("attnv", NQT - 1), 0))
            else:
                s.wait_ge(dve.sem, M.get(("zadd", cq - 2, NQT - 1), 0))
            sxl[cq % 2].tick(s.dma_start(
                out=xgs[cq % 2][:, :, :],
                in_=xbfd[:, cq * 512:(cq + 1) * 512].rearrange(
                    "(a p) c -> p a c", p=P)), 16)
        for cq in range(4):
            xgs_load(cq)
        for cp in range(4):
            if cp < 2:
                xgs_load(4 + 2 * cp)
                xgs_load(5 + 2 * cp)
            for qt in range(NQT):
                s.wait_ge(dve.sem, M.get(("zadd", 2 * cp + 1, qt), 0))
                a = s.activation(sq_dump1[:, 0:1024],
                                 zbf8[qt][:, cp * 1024:(cp + 1) * 1024],
                                 AF.Copy, bias=0.0, scale=np.float32(1.0),
                                 accum_out=msp[:, qt * 4 + cp:qt * 4 + cp + 1])
                rec(("part1", qt, cp), act.tick(a))
        for qt in range(NQT):
            s.wait_ge(dve.sem, M.get(("x1bf_t", qt), 0))
            a = s.activation(sq_dump1, x1bf[qt % 2], AF.Copy, bias=0.0,
                             scale=np.float32(1.0),
                             accum_out=rs0c[:, qt:qt + 1])
            rec(("rs0a", qt), act.tick(a))
        if debug:
            for _g in range(2):
                s.wait_ge(gspl[_g].sem, M.get(("spill_last", _g), 0))
            tp.tick(s.dma_start(out=taps["tap_x1"][:, :], in_=x1sp[:, :]), 16)
        for layer in range(2):
            s_lay = s0 if layer == 0 else s1
            OUT3 = B1T if layer == 0 else B2T
            first = True
            for gt in range(NGT):
                if gt > BT_GT:
                    continue
                for ch in range(2):
                    if first:
                        for _g in range(2):
                            s.wait_ge(gspl[_g].sem,
                                      M.get(("spill_last", _g), 0))
                        first = False
                    s.wait_ge(dve.sem, M.get(("ffn_tt", layer, gt, ch), 0))
                    a = s.activation(OUT3[:, gt, ch * 512:(ch + 1) * 512],
                                     tmp_ffn[2 * (gt % 2) + ch], AF.Relu,
                                     bias=np.float32(0.0), scale=np.float32(4.0 * s_lay))
                    rec(("ffn_relu", layer, gt, ch), act.tick(a))
            rec(("ffn_act_done", layer), act.n)
        if debug:
            s.wait_ge(dve.sem, M.get(("ffn_ev_done", 0), 0))
            tp.tick(s.dma_start(out=taps["tap_b1"][:, :],
                                in_=B1T.rearrange("p a b -> p (a b)")), 16)
            s.wait_ge(dve.sem, M.get(("ffn_ev_done", 1), 0))
            tp.tick(s.dma_start(out=taps["tap_b2"][:, :],
                                in_=B2T.rearrange("p a b -> p (a b)")), 16)
            s.wait_ge(dve.sem, M.get("rs2row_ev", 0))
            tp.tick(s.dma_start(out=taps["tap_rs"][:, :], in_=rsrow), 16)
        for q4 in range(2):
            for ql in (2, 3):
                qt = q4 * 4 + ql
                if q4 > 0:
                    s.wait_ge(so[ql].sem, 16 * q4)
                else:
                    s.wait_ge(pe.sem, M.get(("ffn_mm_done", 1), 0))
                    for _g in range(2):
                        s.wait_ge(gspl[_g].sem, M.get(("spill_last", _g), 0))
                if q4 == 0:
                    s.wait_ge(dve.sem, M.get(("ffn_ev_done", 1), 0))
                sx.tick(s.dma_start(out=x1rel[ql],
                                    in_=x1sp[qt * P:(qt + 1) * P, :]), 16)
            rec(("x1rel_q4s", q4), sx.n)
            for cp in range(4):
                for ql in range(4):
                    qt = q4 * 4 + ql
                    fl = (q4 * 4 + cp) * 4 + ql
                    s.wait_ge(pe.sem, M.get(("f2mm", q4, cp), 0))
                    s.wait_ge(dve.sem, M.get("bias2_ev", 0))
                    if fl >= 2:
                        s.wait_ge(dve.sem, M.get(("f2tt", fl - 2), 0))
                    a = s.activation(ffn_t[ql % 2], bank_f32(2 * ql, cols=1024),
                                     AF.Identity, bias=bias2[:, qt:qt + 1],
                                     scale=np.float32(2.0 * s2))
                    rec(("f2act", q4, cp, ql), act.tick(a))
                for ql in range(4):
                    qt = q4 * 4 + ql
                    fl = (q4 * 4 + cp) * 4 + ql
                    col = ql * 4 + cp
                    s.wait_ge(dve.sem, M.get(("f2tt", fl), 0))
                    sli = z2g[ql][:, cp * 1024:(cp + 1) * 1024]
                    a = s.activation(sq_dump2[:, 0:1024], sli, AF.Copy,
                                     bias=0.0, scale=np.float32(1.0),
                                     accum_out=msp[:, col:col + 1])
                    act.tick(a)
                    a = s.activation(sq_dump2[:, 0:1024], sli, AF.Square,
                                     accum_out=sqp[:, col:col + 1])
                    rec(("part2", qt, cp), act.tick(a))
            for ql in range(4):
                qt = q4 * 4 + ql
                s.wait_ge(dve.sem, M.get(("var2_t", qt), 0))
                a0 = s.activation(var_sb, var_sb, AF.Ln, bias=np.float32(EPS), scale=np.float32(1.0))
                n_ = act.tick(a0)
                s.wait_ge(act.sem, n_)
                a = s.activation(rstd, var_sb, AF.Exp, bias=np.float32(0.0), scale=np.float32(-0.5))
                rec(("rstd2", qt), act.tick(a))
            for ql in (1, 3):
                qt = q4 * 4 + ql
                s.wait_ge(dve.sem, M.get(("out_ready", qt), 0))
                so[ql].tick(s.dma_start(out=out_dr[qt * P:(qt + 1) * P, :],
                                        in_=z2g[ql]), 16)
        if debug:
            s.wait_ge(tp.sem, tp.n)

    def vector_body(v, dry):
        def vs(ins):
            n = dve.tick(ins)
            v.wait_ge(dve.sem, n)
        for xt in range(NQT):
            for bk in range(4):
                ti = 4 * xt + bk
                v.wait_ge(pe.sem, M.get(("x2t", ti), 0))
                c = v.tensor_copy(XT[:, bk * 8:(bk + 1) * 8, xt * P:(xt + 1) * P],
                                  bank3_bf16(bk % 2, 8, P))
                rec(("x2tev", ti), dve.tick(c))
        for dtv in range(8):
            v.wait_ge(pe.sem, M.get(("vt", dtv), 0))
            c = v.tensor_copy(vstage[:, :, dtv * P:(dtv + 1) * P],
                              bank3_bf16(dtv % 2, 8, P))
            rec(("vtev", dtv), dve.tick(c))
        rec("vstage_done", dve.n)
        for qt in range(NQT):
            for kcg in range(4):
                v.wait_ge(pe.sem, M.get(("scores_p", qt, kcg), 0))
                r_ = v.tensor_reduce(maxp[:, kcg:kcg + 1],
                                     bank_f32(2 * kcg, cols=1024), AX.X, OP.max)
                if kcg == 3:
                    vs(r_)
            vs(v.tensor_reduce(negm, maxp[:, 0:4], AX.X, OP.max))
            c = v.tensor_scalar(negm, negm, float(-SCALE_ATT), None,
                                op0=OP.mult)
            rec(("negm_t", qt), dve.tick(c))
            for rnd in range(4):
                v.wait_ge(pe.sem, M.get(("pt", qt, rnd), 0))
                c = v.tensor_copy(PT[:, rnd * 8:(rnd + 1) * 8, :],
                                  bank3_bf16(rnd, 8, P))
                rec(("ptev", qt, rnd), dve.tick(c))
            v.wait_ge(act.sem, M.get(("exp", qt, 7), 0))
            vs(v.tensor_reduce(sumexp, sep[:, 0:8], AX.X, OP.add))
            vs(v.reciprocal(recip, sumexp))
            v.wait_ge(pe.sem, M.get(("attnv", qt), 0))
            v.tensor_scalar(O_bf[:, 0:512], bank_f32(4), recip[:, 0:1], None,
                            op0=OP.mult)
            c = v.tensor_scalar(O_bf[:, 512:1024], bank_f32(5), recip[:, 0:1],
                                None, op0=OP.mult)
            rec(("oev", qt), dve.tick(c))
            v.wait_ge(pe.sem, M.get(("ot", qt), 0))
            c = v.tensor_copy(
                OT[:, qt, :],
                bank3_bf16(6, 8, P).rearrange("p a b -> p (a b)"))
            rec(("otev", qt), dve.tick(c))
        for cq in range(8):
            v.wait_ge(sxl[cq % 2].sem, 16 * (cq // 2 + 1))
            v.wait_ge(pe.sem, M.get(("wo_cq", cq), 0))
            for qt in range(NQT):
                cnt = v.tensor_add(zbf8[qt][:, cq * 512:(cq + 1) * 512],
                                   xgs[cq % 2][:, qt, :], bank_f32(qt))
                rec(("zadd", cq, qt), dve.tick(cnt))

        def emit_x1tev(qt):
            for bk in range(4):
                ti = 4 * qt + bk
                v.wait_ge(pe.sem, M.get(("x1t", qt, bk), 0))
                c = v.tensor_copy(
                    X1T[:, bk * 8:(bk + 1) * 8, qt * P:(qt + 1) * P],
                    bank3_bf16(6 + bk % 2, 8, P))
                rec(("x1tev", ti), dve.tick(c))

        for qt in range(NQT):
            v.wait_ge(act.sem, M.get(("part1", qt, 3), 0))
            vs(v.tensor_reduce(msum, msp[:, 4 * qt:4 * qt + 4], AX.X,
                               OP.add))
            vs(v.tensor_scalar(m_sb, msum, 1.0 / 4096.0, None, op0=OP.mult))
            if qt >= 2:
                v.wait_ge(pe.sem, M.get(("x1t", qt - 2, 3), 0))
                v.wait_ge(gspl[qt % 2].sem, M.get(("spill", qt - 2), 0))
            for bk in range(4):
                c = v.tensor_scalar(
                    x1bf[qt % 2][:, bk * 1024:(bk + 1) * 1024],
                    zbf8[qt][:, bk * 1024:(bk + 1) * 1024], m_sb[:, 0:1],
                    None, op0=OP.subtract)
                rec(("x1bf_c", qt, bk), dve.tick(c))
            rec(("x1bf_t", qt), dve.n)
            if qt > 0:
                emit_x1tev(qt - 1)
                v.wait_ge(act.sem, M.get(("rs0a", qt - 1), 0))
                c = v.tensor_copy(rs0b[:, qt - 1:qt],
                                  rs0c[:, qt - 1:qt])
                rec(("rs0b", qt - 1), dve.tick(c))
        emit_x1tev(NQT - 1)
        v.wait_ge(act.sem, M.get(("rs0a", NQT - 1), 0))
        c = v.tensor_copy(rs0b[:, NQT - 1:NQT],
                          rs0c[:, NQT - 1:NQT])
        rec(("rs0b", NQT - 1), dve.tick(c))
        rec("x1t_all_ev", dve.n)
        for layer in range(2):
            s_lay = s0 if layer == 0 else s1
            OUT3 = B1T if layer == 0 else B2T
            v.wait_ge(pe.sem, M.get(("rs", layer), 0))
            v.tensor_scalar(rsrow[0:1, 0:512], bank_row_f32(4), 0.5, None,
                            op0=OP.mult)
            c = v.tensor_scalar(rsrow[0:1, 512:1024], bank_row_f32(5), 0.5,
                                None, op0=OP.mult)
            rec(("rshalf", layer), dve.tick(c))
            v.wait_ge(pe.sem, M.get(("rsb", layer), 0))
            vs(v.tensor_copy(rsb_sb, bank_f32(6, cols=1024)))
            rec(("rsbev", layer), dve.n)
            for gp in range(NGT // 2):
                v.wait_ge(pe.sem, M.get(("ffn_mm", layer, gp), 0))
                bA = 4 * (gp % 2)
                for j2 in range(2):
                    gt = 2 * gp + j2
                    for ch in range(2):
                        tb = tmp_ffn[2 * (gt % 2) + ch]
                        if gt >= 2 and gt - 2 <= BT_GT:
                            v.wait_ge(act.sem,
                                      M.get(("ffn_relu", layer, gt - 2, ch), 0))
                        c = v.tensor_sub(tb, bank_f32(bA + 2 * j2 + ch),
                                         rsb_sb[:, ch * 512:(ch + 1) * 512])
                        rec(("ffn_tt", layer, gt, ch), dve.tick(c))
                    if gt == BT_GT:
                        for ch in range(2):
                            v.wait_ge(act.sem,
                                      M.get(("ffn_relu", layer, gt, ch), 0))
                            c = v.tensor_scalar(
                                OUT3[:, gt, ch * 512:(ch + 1) * 512],
                                OUT3[:, gt, ch * 512:(ch + 1) * 512],
                                dinh[:, 0:1], None, op0=OP.mult)
                            dve.tick(c)
                    elif gt > BT_GT:
                        v.wait_ge(dve.sem, M.get(("ffn_tt", layer, gt, 1), 0))
                        for ch in range(2):
                            c = v.tensor_scalar(
                                OUT3[:, gt, ch * 512:(ch + 1) * 512],
                                tmp_ffn[2 * (gt % 2) + ch],
                                float(-16.0 * s_lay), 0.0,
                                op0=OP.mult, op1=OP.min)
                            dve.tick(c)
            rec(("ffn_ev_done", layer), dve.n)
        v.wait_ge(pe.sem, M.get("rs2", 0))
        v.tensor_copy(rsrow[0:1, 0:512], bank_row_f32(4))
        c = v.tensor_copy(rsrow[0:1, 512:1024], bank_row_f32(5))
        rec("rs2row_ev", dve.tick(c))
        v.wait_ge(pe.sem, M.get("rs2col_all", 0))
        for qt in range(NQT):
            c = v.tensor_scalar(bias2[:, qt:qt + 1], bank_f32(qt, cols=1),
                                float(-s2), None, op0=OP.mult)
            dve.tick(c)
        rec("bias2_ev", dve.n)
        for q4 in range(2):
            for cp in range(4):
                for ql in range(4):
                    qt = q4 * 4 + ql
                    fl = (q4 * 4 + cp) * 4 + ql
                    v.wait_ge(act.sem, M.get(("f2act", q4, cp, ql), 0))
                    v.wait_ge(sx.sem, M.get(("x1rel_q4s", q4), 0))
                    v.wait_ge(s16.sem, M.get(("x1rel_q4y", q4), 0))
                    c = v.tensor_add(z2g[ql][:, cp * 1024:(cp + 1) * 1024],
                                     x1rel[ql][:, cp * 1024:(cp + 1) * 1024],
                                     ffn_t[ql % 2])
                    rec(("f2tt", fl), dve.tick(c))
            for ql in range(4):
                qt = q4 * 4 + ql
                v.wait_ge(act.sem, M.get(("part2", qt, 3), 0))
                vs(v.tensor_reduce(msum, msp[:, 4 * ql:4 * ql + 4], AX.X,
                                   OP.add))
                vs(v.tensor_reduce(sqs, sqp[:, 4 * ql:4 * ql + 4], AX.X,
                                   OP.add))
                vs(v.tensor_scalar(m_sb, msum, 1.0 / 4096.0, None, op0=OP.mult))
                vs(v.tensor_mul(m2_sb, m_sb, m_sb))
                if qt > 0:
                    v.wait_ge(act.sem, M.get(("rstd2", qt - 1), 0))
                c = v.tensor_scalar(var_sb, sqs, 1.0 / 4096.0, m2_sb[:, 0:1],
                                    op0=OP.mult, op1=OP.subtract)
                rec(("var2_t", qt), dve.tick(c))
                v.wait_ge(act.sem, M.get(("rstd2", qt), 0))
                c = v.tensor_scalar(z2g[ql], z2g[ql], m_sb[:, 0:1], rstd[:, 0:1],
                                    op0=OP.subtract, op1=OP.mult)
                rec(("out_ready", qt), dve.tick(c))

    bodies = [("tensor", tensor_body), ("gpsimd", gpsimd_body),
              ("sync", sync_body), ("scalar", scalar_body),
              ("vector", vector_body)]

    for _ in range(2):
        for c in all_ctrs:
            c.reset()
        dry_eng = _DryEng()
        for _, body in bodies:
            body(dry_eng, True)
    totals = {id(c): c.n for c in all_ctrs}
    for c in all_ctrs:
        c.reset()
    streams = {}
    for nm, body in bodies:
        streams[nm] = []
        body(_RecEng(streams[nm]), True)
    semnames = {id(c.sem): n for n, c in
                [("g16", g16), ("gx0", gx[0]), ("gx1", gx[1]),
                 ("gq0", gq[0]), ("gq1", gq[1]), ("gm0", gm[0]),
                 ("gm1", gm[1]), ("gu0", gu[0]), ("gu1", gu[1]),
                 ("skt", skt), ("svf", svf), ("s16", s16), ("sw0", sw[0]),
                 ("sw1", sw[1]), ("st", st), ("cc", ccc), ("pe", pe),
                 ("act", act), ("dve", dve), ("tp", tp), ("sx", sx),
                 ("gsp0", gspl[0]), ("gsp1", gspl[1]), ("gsp2", gspl[2]),
                 ("sx0", sxl[0]), ("sx1", sxl[1]), ("sx2", sxl[2]),
                 ("gp1", gp1), ("so0", so[0]), ("so1", so[1]),
                 ("so2", so[2]), ("so3", so[3]), ("sv0", sv[0]),
                 ("sv1", sv[1]), ("sv2", sv[2]), ("sv3", sv[3])]}
    _simulate(streams, semnames)
    for c in all_ctrs:
        c.reset()
    with nc.Block("main", no_gpsimd_drain=True) as blk:
        blk.tensor(lambda e: tensor_body(e, False))
        blk.gpsimd(lambda e: gpsimd_body(e, False))
        blk.sync(lambda e: sync_body(e, False))
        blk.scalar(lambda e: scalar_body(e, False))
        blk.vector(lambda e: vector_body(e, False))
    for c in all_ctrs:
        assert totals[id(c)] == c.n, "emission mismatch vs dry run"
    stack.close()
    return nc


def kernel(L0_attn_x, L0_attn_Wqkv, L0_attn_bqkv, L0_attn_Wo, L0_attn_bo,
           L0_ffn_k0, L0_ffn_s0, L0_ffn_k1, L0_ffn_s1, L0_ffn_k2, L0_ffn_s2):
    x = np.asarray(L0_attn_x, dtype=np.float32)
    wq = np.ascontiguousarray(np.asarray(L0_attn_Wqkv, dtype=np.float32))
    bqkv = np.asarray(L0_attn_bqkv, dtype=np.float32)
    wo = np.ascontiguousarray(np.asarray(L0_attn_Wo, dtype=np.float32))
    bo = np.asarray(L0_attn_bo, dtype=np.float32)
    k0 = np.asarray(L0_ffn_k0)
    k1 = np.asarray(L0_ffn_k1)
    k2 = np.asarray(L0_ffn_k2)
    s0 = float(np.asarray(L0_ffn_s0))
    s1 = float(np.asarray(L0_ffn_s1))
    s2 = float(np.asarray(L0_ffn_s2))

    def blk4(w, nft):
        return np.ascontiguousarray(
            w.reshape(NCT, P, nft, P).transpose(2, 1, 0, 3))

    wq_blkA = blk4(wq, NFT)
    k0_blkA = blk4(np.ascontiguousarray(k0).astype(np.uint8), NGT)
    k1_blkA = blk4(np.ascontiguousarray(k1).astype(np.uint8), NGT)
    k2_u8 = np.ascontiguousarray(np.asarray(k2).astype(np.uint8))
    bqkv_col = np.ascontiguousarray(bqkv.reshape(NFT, P).T)
    bo_repA = np.ascontiguousarray(np.broadcast_to(bo, (P, 4096)))
    ident = np.eye(P, dtype=ml_dtypes.bfloat16)
    ones_col = np.ones((P, 1), dtype=ml_dtypes.bfloat16)
    ones_row = np.ones((1, P), dtype=np.float32)
    dinh_v = np.where(np.arange(P)[:, None] < BT_P, 1.0, -4.0).astype(np.float32)

    nc = build_nc(s0, s1, s2)
    in_maps = []
    for c in range(8):
        b, qs = c // 4, (c % 4) * 1024
        in_maps.append({
            "x": np.ascontiguousarray(x[b, qs:qs + 1024, :]),
            "wq_blk": wq_blkA, "wo": wo,
            "k0_blk": k0_blkA, "k1_blk": k1_blkA, "k2": k2_u8,
            "bqkv_col": bqkv_col, "bo_rep": bo_repA,
            "ident": ident, "ones_col": ones_col, "ones_row": ones_row,
            "dinh": dinh_v,
            "bsel": np.array([[b]], dtype=np.int32),
        })
    res = run_bass_kernel_spmd(nc, in_maps, core_ids=list(range(8)))
    out = np.empty((2, 4096, 4096), dtype=np.float32)
    for c in range(8):
        b, qs = c // 4, (c % 4) * 1024
        out[b, qs:qs + 1024, :] = res.results[c]["out"]
    return out


# version 23
# speedup vs baseline: 1.0334x; 1.0063x over previous
import numpy as np
import ml_dtypes
import concourse.bass as bass
import concourse.mybir as mybir
from concourse.bass_utils import run_bass_kernel_spmd

F32 = mybir.dt.float32
BF16 = mybir.dt.bfloat16
U8 = mybir.dt.uint8
I32 = mybir.dt.int32
AF = mybir.ActivationFunctionType
OP = mybir.AluOpType
AX = mybir.AxisListType

P = 128
NQT = 8
NCT = 32
NDT = 8
NFT = 24
NGT = 32
SCALE_ATT = 0.03125
EPS = 1e-5
BT_GT = 25
BT_P = 77
KB = 1024

OFF_IDENT = 0
OFF_BQKV = 256
OFF_ONEC = 384
OFF_ONER = 416
OFF_DINH = 928
OFF_BSEL = 960
OFF_MAXP = 992
OFF_NEGM = 1024
OFF_SEP = 1056
OFF_SE = 1088
OFF_RECIP = 1120
OFF_MSUM = 1152
OFF_M = 1184
OFF_M2 = 1216
OFF_VAR = 1248
OFF_RSTD = 1280
OFF_SQS = 1312
OFF_RS0 = 1312
OFF_MSP = 1344
OFF_RSROW = 1472
OFF_BIAS2 = 5568
OFF_SQP = 5696
CONSTS_END = 6144
R0 = CONSTS_END
R1 = R0 + 64 * KB
R2 = R1 + 64 * KB
R3 = R2 + 16 * KB
R4 = R3 + 16 * KB
R5 = R4 + 24 * KB
SB_END = R5 + 8 * KB + 512

QKV_ORDER = list(range(8, 16)) + list(range(16, 24)) + list(range(0, 8))


class _DryIns:
    def then_inc(self, sem, k=1):
        return self


_DRY = _DryIns()


class _DryEng:

    def __getattr__(self, name):
        return lambda *a, **k: _DRY


class _RecIns:
    def __init__(self, log):
        self.log = log

    def then_inc(self, sem, k=1):
        self.log.append(("inc", id(sem), k))
        return self


class _RecEng:

    def __init__(self, stream):
        self.stream = stream

    def __getattr__(self, name):
        if name == "wait_ge":
            def w(sem, n):
                self.stream.append(("wait", id(sem), n))
            return w

        def op(*a, **kw):
            log = []
            self.stream.append(("op", log))
            return _RecIns(log)
        return op


def _simulate(streams, semnames):
    vals = {}
    pcs = {e: 0 for e in streams}
    progress = True
    while progress:
        progress = False
        for e, ops in streams.items():
            while pcs[e] < len(ops):
                kind, a, *rest = ops[pcs[e]]
                if kind == "wait":
                    if vals.get(a, 0) < rest[0]:
                        break
                    pcs[e] += 1
                    progress = True
                else:
                    for _, semid, k in a:
                        vals[semid] = vals.get(semid, 0) + k
                    pcs[e] += 1
                    progress = True
    stuck = {e: pcs[e] for e in streams if pcs[e] < len(streams[e])}
    if stuck:
        msg = ["DEADLOCK DETECTED in program graph:"]
        for e, pc in stuck.items():
            kind, semid, n = streams[e][pc]
            msg.append(f"  {e} blocked at op#{pc} waiting "
                       f"{semnames.get(semid, semid)} >= {n} "
                       f"(currently {vals.get(semid, 0)})")
        raise RuntimeError("\n".join(msg))


class Ctr:
    def __init__(self, sem):
        self.sem = sem
        self.n = 0

    def tick(self, ins, k=1):
        ins.then_inc(self.sem, k)
        self.n += k
        return self.n

    def reset(self):
        self.n = 0


def build_nc(s0, s1, s2, debug=False):
    nc = bass.Bass("TRN2", target_bir_lowering=False)

    x_in = nc.dram_tensor("x", [1024, 4096], F32, kind="ExternalInput")
    wq_blk = nc.dram_tensor("wq_blk", [NFT, P, NCT, P], F32, kind="ExternalInput")
    wo_in = nc.dram_tensor("wo", [1024, 4096], F32, kind="ExternalInput")
    k0_blk = nc.dram_tensor("k0_blk", [NGT, P, NCT, P], U8, kind="ExternalInput")
    k1_blk = nc.dram_tensor("k1_blk", [NGT, P, NCT, P], U8, kind="ExternalInput")
    k2_in = nc.dram_tensor("k2", [4096, 4096], U8, kind="ExternalInput")
    bqkv_in = nc.dram_tensor("bqkv_col", [P, NFT], F32, kind="ExternalInput")
    bo_in = nc.dram_tensor("bo_rep", [P, 4096], F32, kind="ExternalInput")
    ident_in = nc.dram_tensor("ident", [P, P], BF16, kind="ExternalInput")
    onec_in = nc.dram_tensor("ones_col", [P, 1], BF16, kind="ExternalInput")
    oner_in = nc.dram_tensor("ones_row", [1, P], F32, kind="ExternalInput")
    dinh_in = nc.dram_tensor("dinh", [P, 1], F32, kind="ExternalInput")
    bsel_in = nc.dram_tensor("bsel", [1, 1], I32, kind="ExternalInput")
    out_dr = nc.dram_tensor("out", [1024, 4096], F32, kind="ExternalOutput")

    taps = {}
    if debug:
        for nm, shp, dt_ in [
                ("tap_xt", [P, 32768], BF16), ("tap_qt", [P, 8192], BF16),
                ("tap_kv", [P, 8192], BF16), ("tap_vs", [P, 8192], BF16),
                ("tap_kt", [P, 32768], BF16), ("tap_vf", [P, 32768], BF16),
                ("tap_p", [P, 4096], BF16), ("tap_ot", [P, 8192], BF16),
                ("tap_x1", [1024, 4096], BF16), ("tap_b1", [P, 32768], BF16),
                ("tap_b2", [P, 32768], BF16), ("tap_rs", [1, 1024], F32)]:
            taps[nm] = nc.dram_tensor(nm, shp, dt_, kind="ExternalOutput")
    wobf = nc.dram_tensor("wobf", [1024, 4096], BF16)
    cc_in_k = nc.dram_tensor("cc_in_k", [1024, 1024], BF16)
    cc_in_v = nc.dram_tensor("cc_in_v", [1024, 1024], BF16)
    cc_out_k = nc.dram_tensor("cc_out_k", [8 * 1024, 1024], BF16,
                              addr_space="Shared")
    cc_out_v = nc.dram_tensor("cc_out_v", [8 * 1024, 1024], BF16,
                              addr_space="Shared")
    x1sp = nc.dram_tensor("x1sp", [1024, 4096], BF16)
    xbfd = nc.dram_tensor("xbfd", [1024, 4096], BF16)

    sb = nc.alloc_sbuf_tensor("sbarena", [P, SB_END], U8)
    psa = nc.alloc_psum_tensor("psarena", [P, 16384], U8)

    def sbv(off, cols, dt, p0=0, np_=P):
        bpe = mybir.dt.size(dt)
        return sb[p0:p0 + np_, off:off + cols * bpe].bitcast(dt)

    def sb3(off, n0, n1, dt):
        bpe = mybir.dt.size(dt)
        return sb[:, off:off + n0 * n1 * bpe].bitcast(dt).rearrange(
            "p (a b) -> p a b", a=n0)

    def bank_f32(b, cols=512):
        return psa[:, b * 2048: b * 2048 + cols * 4].bitcast(F32)

    def bank_slot_bf16(b, slot):
        o = b * 2048 + slot * 256
        return psa[:, o:o + 256].bitcast(BF16)

    def bank_row_f32(b, cols=512):
        return psa[0:1, b * 2048: b * 2048 + cols * 4].bitcast(F32)

    def bank3_bf16(b, n0, n1):
        return psa[:, b * 2048: b * 2048 + n0 * n1 * 2].bitcast(BF16).rearrange(
            "p (a b) -> p a b", a=n0)

    idt = sbv(OFF_IDENT, P, BF16)
    bqkv = sbv(OFF_BQKV, NFT, F32)
    onec = sbv(OFF_ONEC, 1, BF16)
    oner = sbv(OFF_ONER, P, F32, np_=1)
    dinh = sbv(OFF_DINH, 1, F32)
    bsel_sb = sb[0:1, OFF_BSEL:OFF_BSEL + 4].bitcast(I32)
    maxp = sbv(OFF_MAXP, 8, F32)
    negm = sbv(OFF_NEGM, 1, F32)
    sep = sbv(OFF_SEP, 8, F32)
    sumexp = sbv(OFF_SE, 1, F32)
    recip = sbv(OFF_RECIP, 1, F32)
    msum = sbv(OFF_MSUM, 1, F32)
    m_sb = sbv(OFF_M, 1, F32)
    m2_sb = sbv(OFF_M2, 1, F32)
    var_sb = sbv(OFF_VAR, 1, F32)
    rstd = sbv(OFF_RSTD, 1, F32)
    sqs = sbv(OFF_SQS, 1, F32)
    rs0c = sbv(5600, 8, F32)
    rs0b = sbv(5664, 8, BF16)
    msp = sbv(OFF_MSP, 32, F32)
    sqp = sbv(OFF_SQP, 16, F32)
    rsrow = sbv(OFF_RSROW, 1024, F32, np_=1)
    bias2 = sbv(OFF_BIAS2, 8, F32)

    XT = sb3(R0, NCT, 1024, BF16)
    VF = sb3(R0, NCT, 1024, BF16)
    X1T = sb3(R0, NCT, 1024, BF16)
    B2T = sb3(R0, NGT, 1024, BF16)

    xbf = [sbv(R1 + i * 8 * KB, 4096, BF16) for i in range(4)]
    kvK = sb3(R1 + 32 * KB, NDT, 1024, BF16)
    vstage = sb3(R1 + 48 * KB, NDT, 1024, BF16)
    KT = sb3(R1, NDT, 4096, BF16)
    zbf8 = [sbv(R1 + i * 8 * KB, 4096, BF16) for i in range(NQT)]
    wopan3 = [sb3(R2 + i * 4 * KB, 4, 512, BF16) for i in range(2)]
    B1T = sb3(R1, NGT, 1024, BF16)
    z2g = [sbv(R1 + i * 16 * KB, 4096, F32) for i in range(4)]

    QT = sb3(R2, NDT, 1024, BF16)
    bo_rep = sbv(R2, 4096, F32)
    sq_dump2 = sbv(R2, 4096, BF16)

    bo_bf = sbv(R5, 4096, BF16, np_=1)
    ones_bf = sbv(R5 + 8 * KB, P, BF16, np_=1)
    OT = sb3(R3, NQT, NDT * P, BF16)
    mkpan2 = [sb3(R3, 2, 4096, BF16), sb3(R2, 2, 4096, BF16)]
    wqpan = [sbv(R3 + i * 8 * KB, 4096, BF16) for i in range(2)]

    vt_tmp = [sbv(R4 + i * 2 * KB, 1024, BF16) for i in range(2)]
    P_sb = sbv(R4, 4096, BF16)
    PT = sb3(R4 + 8 * KB, NCT, P, BF16)
    O_bf = sbv(R4 + 16 * KB, 1024, BF16)
    xgs = [sb3(R4 + i * 8 * KB, NQT, 512, BF16) for i in range(2)]
    x1bf = [sbv(R4 + i * 8 * KB, 4096, BF16) for i in range(2)]
    sq_dump1 = sbv(R4 + 16 * KB, 4096, BF16)
    rsb_sb = sbv(R4, 1024, F32)
    tmp_ffn = [sbv(R4 + 4 * KB + i * 2 * KB, 512, F32) for i in range(4)]
    u2pan2 = [sb3(R3 + i * 4 * KB, 2, 1024, BF16) for i in range(2)]
    x1rel = [sbv(R3 + 8 * KB, 4096, BF16), sbv(R2 + 8 * KB, 4096, BF16),
             sbv(R4, 4096, BF16), sbv(R4 + 8 * KB, 4096, BF16)]
    ffn_t = [sbv(R4 + 16 * KB + i * 4 * KB, 1024, F32) for i in range(2)]

    import contextlib
    stack = contextlib.ExitStack()
    sems = {}
    for name in ["g16", "gx0", "gx1", "gq0", "gq1", "gm0", "gm1", "gu0",
                 "gu1", "skt", "svf", "s16", "sw0", "sw1", "st", "cc", "pe",
                 "act", "dve", "tp", "sx", "gsp0", "gsp1", "gsp2",
                 "sx0", "sx1", "sx2", "gp1", "so0", "so1", "so2", "so3",
                 "sv0", "sv1", "sv2", "sv3"]:
        sems[name] = stack.enter_context(nc.semaphore(name))
    g16 = Ctr(sems["g16"])
    gx = [Ctr(sems["gx0"]), Ctr(sems["gx1"])]
    gq = [Ctr(sems["gq0"]), Ctr(sems["gq1"])]
    gm = [Ctr(sems["gm0"]), Ctr(sems["gm1"])]
    gu = [Ctr(sems["gu0"]), Ctr(sems["gu1"])]
    skt = Ctr(sems["skt"])
    svf = Ctr(sems["svf"])
    s16 = Ctr(sems["s16"])
    sw = [Ctr(sems["sw0"]), Ctr(sems["sw1"])]
    st = Ctr(sems["st"])
    ccc = Ctr(sems["cc"])
    pe = Ctr(sems["pe"])
    act = Ctr(sems["act"])
    dve = Ctr(sems["dve"])
    tp = Ctr(sems["tp"])
    sx = Ctr(sems["sx"])
    gspl = [Ctr(sems["gsp0"]), Ctr(sems["gsp1"]), Ctr(sems["gsp2"])]
    sxl = [Ctr(sems["sx0"]), Ctr(sems["sx1"]), Ctr(sems["sx2"])]
    gp1 = Ctr(sems["gp1"])
    so = [Ctr(sems["so%d" % i]) for i in range(4)]
    sv = [Ctr(sems["sv%d" % i]) for i in range(4)]
    all_ctrs = [g16, gx[0], gx[1], gq[0], gq[1], gm[0], gm[1], gu[0], gu[1],
                skt, svf, s16, sw[0], sw[1], st, ccc, pe, act, dve, tp, sx,
                gspl[0], gspl[1], gspl[2], sxl[0], sxl[1], sxl[2], gp1,
                so[0], so[1], so[2], so[3], sv[0], sv[1], sv[2], sv[3]]

    M = {}

    def rec(key, val):
        if key in M:
            assert M[key] == val, f"milestone {key} changed: {M[key]} -> {val}"
        M[key] = val

    def tensor_body(t, dry):
        for xt in range(NQT):
            par = xt % 2
            t.wait_ge(gx[par].sem, 16 * (xt // 2 + 1))
            for bk in range(4):
                ti = 4 * xt + bk
                if ti >= 2:
                    t.wait_ge(dve.sem, M.get(("x2tev", ti - 2), 0))
                last = None
                for j in range(8):
                    ct = bk * 8 + j
                    last = t.transpose(bank_slot_bf16(bk % 2, j),
                                       xbf[xt % 4][:, ct * P:(ct + 1) * P], idt)
                rec(("x2t", ti), pe.tick(last))
        pend_vt = []
        for fi, ft in enumerate(QKV_ORDER):
            t.wait_ge(gq[fi % 2].sem, 16 * (fi // 2 + 1))
            if fi >= 2:
                t.wait_ge(act.sem, M.get(("qkvev", fi - 2), 0))
            bA = 2 + 2 * (fi % 2)
            last = None
            if fi < 2:
                for ch in range(2):
                    t.wait_ge(dve.sem, M.get(("x2tev", 15 + 16 * ch), 0))
                    for ct in range(NCT):
                        lhs = wqpan[fi % 2][:, ct * P:(ct + 1) * P]
                        last = t.matmul(bank_f32(bA + ch), lhs,
                                        XT[:, ct, ch * 512:(ch + 1) * 512],
                                        start=ct == 0, stop=ct == NCT - 1)
            else:
                for ct in range(NCT):
                    lhs = wqpan[fi % 2][:, ct * P:(ct + 1) * P]
                    t.matmul(bank_f32(bA), lhs, XT[:, ct, 0:512],
                             start=ct == 0, stop=ct == NCT - 1)
                    last = t.matmul(bank_f32(bA + 1), lhs, XT[:, ct, 512:1024],
                                    start=ct == 0, stop=ct == NCT - 1)
            rec(("qkv", fi), pe.tick(last))
            if pend_vt:
                fv, fvi = pend_vt.pop(0)
                dtv = fv - 16
                t.wait_ge(act.sem, M.get(("qkvev", fvi), 0))
                if dtv >= 2:
                    t.wait_ge(dve.sem, M.get(("vtev", dtv - 2), 0))
                lastv = None
                for tt in range(8):
                    lastv = t.transpose(bank_slot_bf16(dtv % 2, tt),
                                        vt_tmp[fvi % 2][:, tt * P:(tt + 1) * P],
                                        idt)
                rec(("vt", dtv), pe.tick(lastv))
            if 16 <= ft < 24:
                pend_vt.append((ft, fi))
        while pend_vt:
            fv, fvi = pend_vt.pop(0)
            dtv = fv - 16
            t.wait_ge(act.sem, M.get(("qkvev", fvi), 0))
            if dtv >= 2:
                t.wait_ge(dve.sem, M.get(("vtev", dtv - 2), 0))
            lastv = None
            for tt in range(8):
                lastv = t.transpose(bank_slot_bf16(dtv % 2, tt),
                                    vt_tmp[fvi % 2][:, tt * P:(tt + 1) * P], idt)
            rec(("vt", dtv), pe.tick(lastv))
        rec("qkv_done", pe.n)
        for qt in range(NQT):
            if qt == 0:
                t.wait_ge(skt.sem, 64)
                t.wait_ge(act.sem, M.get(("qkvev", 23), 0))
                t.wait_ge(dve.sem, M.get("vstage_done", 0))
            else:
                t.wait_ge(act.sem, M.get(("exp", qt - 1, 7), 0))
                t.wait_ge(dve.sem, M.get(("otev", qt - 1), 0))
            for kcg in range(4):
                last = None
                for dt_ in range(NDT):
                    lhs = QT[:, dt_, qt * P:(qt + 1) * P]
                    t.matmul(bank_f32(2 * kcg), lhs,
                             KT[:, dt_, kcg * 1024:kcg * 1024 + 512],
                             start=dt_ == 0, stop=dt_ == NDT - 1)
                    last = t.matmul(bank_f32(2 * kcg + 1), lhs,
                                    KT[:, dt_, kcg * 1024 + 512:(kcg + 1) * 1024],
                                    start=dt_ == 0, stop=dt_ == NDT - 1)
                rec(("scores_p", qt, kcg), pe.tick(last))
            if qt == NQT - 1:
                rec("scores_done", pe.n)
            for rnd in range(4):
                t.wait_ge(act.sem, M.get(("exp", qt, 2 * rnd + 1), 0))
                lastp = None
                for j in range(8):
                    kt = rnd * 8 + j
                    lastp = t.transpose(bank_slot_bf16(rnd, j),
                                        P_sb[:, kt * P:(kt + 1) * P], idt)
                rec(("pt", qt, rnd), pe.tick(lastp))
            t.wait_ge(act.sem, M.get(("exp", qt, 7), 0))
            last = None
            for kt in range(NCT):
                if kt % 8 == 0:
                    t.wait_ge(dve.sem, M.get(("ptev", qt, kt // 8), 0))
                    if qt == 0:
                        t.wait_ge(sv[kt // 8].sem, 16)
                t.matmul(bank_f32(4), PT[:, kt, :], VF[:, kt, 0:512],
                         start=kt == 0, stop=kt == NCT - 1)
                last = t.matmul(bank_f32(5), PT[:, kt, :], VF[:, kt, 512:1024],
                                start=kt == 0, stop=kt == NCT - 1)
            rec(("attnv", qt), pe.tick(last))
            t.wait_ge(dve.sem, M.get(("oev", qt), 0))
            lasto = None
            for dt_ in range(NDT):
                lasto = t.transpose(bank_slot_bf16(6, dt_),
                                    O_bf[:, dt_ * P:(dt_ + 1) * P], idt)
            rec(("ot", qt), pe.tick(lasto))
        rec("att_done", pe.n)
        pan_i = 0
        for cq in range(8):
            for pp in range(2):
                par = pan_i % 2
                t.wait_ge(sw[par].sem, 16 * (pan_i // 2 + 1))
                if pan_i == 0:
                    t.wait_ge(g16.sem, M.get("bo_ld", 0))
                    t.wait_ge(dve.sem, M.get(("otev", NQT - 1), 0))
                if pp == 0:
                    for qt in range(NQT):
                        if cq >= 1:
                            t.wait_ge(dve.sem, M.get(("zadd", cq - 1, qt), 0))
                        t.matmul(bank_f32(qt), ones_bf,
                                 bo_bf[0:1, cq * 512:(cq + 1) * 512],
                                 start=True, stop=False)
                lastm = None
                for a in range(4):
                    dt_ = 4 * pp + a
                    for qt in range(NQT):
                        lhs = OT[:, qt, dt_ * P:(dt_ + 1) * P]
                        lastm = t.matmul(bank_f32(qt), lhs,
                                         wopan3[par][:, a, :],
                                         start=False, stop=dt_ == NDT - 1)
                rec(("wo_pan", pan_i), pe.tick(lastm))
                pan_i += 1
            rec(("wo_cq", cq), pe.n)
        rec("wo_done", pe.n)
        for qt in range(NQT):
            for bk in range(4):
                t.wait_ge(dve.sem, M.get(("x1bf_c", qt, bk), 0))
                ti = 4 * qt + bk
                if ti >= 2:
                    t.wait_ge(dve.sem, M.get(("x1tev", ti - 2), 0))
                lastt = None
                for j in range(8):
                    ct = bk * 8 + j
                    lastt = t.transpose(bank_slot_bf16(6 + bk % 2, j),
                                        x1bf[qt % 2][:, ct * P:(ct + 1) * P],
                                        idt)
                rec(("x1t", qt, bk), pe.tick(lastt))
        for layer in range(2):
            IN3 = X1T if layer == 0 else B1T
            if layer == 0:
                t.wait_ge(dve.sem, M.get("x1t_all_ev", 0))
                last = None
                for qt in range(NQT):
                    t.wait_ge(dve.sem, M.get(("rs0b", qt), 0))
                    bb = 4 + qt // 4
                    cc = (qt % 4) * P
                    last = t.matmul(
                        bank_row_f32(bb)[0:1, cc:cc + P],
                        rs0b[:, qt:qt + 1], idt,
                        start=True, stop=True)
            else:
                t.wait_ge(dve.sem, M.get(("ffn_ev_done", 0), 0))
                t.wait_ge(act.sem, M.get(("ffn_act_done", 0), 0))
                last = None
                for ct in range(NCT):
                    t.matmul(bank_row_f32(4), onec, IN3[:, ct, 0:512],
                             start=ct == 0, stop=ct == NCT - 1)
                    last = t.matmul(bank_row_f32(5), onec, IN3[:, ct, 512:1024],
                                    start=ct == 0, stop=ct == NCT - 1)
            rec(("rs", layer), pe.tick(last))
            t.wait_ge(dve.sem, M.get(("rshalf", layer), 0))
            t.matmul(bank_f32(6), oner, rsrow[0:1, 0:512], start=True, stop=True)
            last = t.matmul(bank_f32(7), oner, rsrow[0:1, 512:1024],
                            start=True, stop=True)
            rec(("rsb", layer), pe.tick(last))
            for gp in range(NGT // 2):
                par = gp % 2
                t.wait_ge(gm[par].sem, 16 * (layer * 8 + gp // 2 + 1))
                if gp >= 2:
                    t.wait_ge(dve.sem, M.get(("ffn_tt", layer, 2 * gp - 3, 1), 0))
                elif gp == 1:
                    t.wait_ge(dve.sem, M.get(("rsbev", layer), 0))
                bA = 4 * (gp % 2)
                last = None
                for ct in range(NCT):
                    for j2 in range(2):
                        lhs = mkpan2[par][:, j2, ct * P:(ct + 1) * P]
                        t.matmul(bank_f32(bA + 2 * j2), lhs, IN3[:, ct, 0:512],
                                 start=ct == 0, stop=ct == NCT - 1)
                        last = t.matmul(bank_f32(bA + 2 * j2 + 1), lhs,
                                        IN3[:, ct, 512:1024],
                                        start=ct == 0, stop=ct == NCT - 1)
                rec(("ffn_mm", layer, gp), pe.tick(last))
            rec(("ffn_mm_done", layer), pe.n)
        t.wait_ge(dve.sem, M.get(("ffn_ev_done", 1), 0))
        t.wait_ge(act.sem, M.get(("ffn_act_done", 1), 0))
        last = None
        for gt in range(NGT):
            t.matmul(bank_row_f32(4), onec, B2T[:, gt, 0:512],
                     start=gt == 0, stop=gt == NGT - 1)
            last = t.matmul(bank_row_f32(5), onec, B2T[:, gt, 512:1024],
                            start=gt == 0, stop=gt == NGT - 1)
        rec("rs2", pe.tick(last))
        t.wait_ge(dve.sem, M.get("rs2row_ev", 0))
        lastc = None
        for qt in range(NQT):
            lastc = t.matmul(bank_f32(qt, cols=1),
                             rsrow[0:1, qt * P:(qt + 1) * P],
                             oner[0:1, 0:1], start=True, stop=True)
        rec("rs2col_all", pe.tick(lastc))
        pan2 = 0
        for q4 in range(2):
            for cp in range(4):
                for gt in range(NGT):
                    pr = pan2 % 2
                    j2 = gt % 2
                    if j2 == 0:
                        t.wait_ge(gu[pr].sem, 16 * (pan2 // 2 + 1))
                    if gt == 0 and q4 == 0 and cp == 0:
                        t.wait_ge(dve.sem, M.get("bias2_ev", 0))
                    last = None
                    for ql in range(4):
                        qt = q4 * 4 + ql
                        if gt == 0 and (q4 > 0 or cp > 0):
                            pq4, pcp = (q4, cp - 1) if cp > 0 else (q4 - 1, 3)
                            t.wait_ge(act.sem, M.get(("f2act", pq4, pcp, ql), 0))
                        t.matmul(bank_f32(2 * ql),
                                 B2T[:, gt, qt * P:(qt + 1) * P],
                                 u2pan2[pr][:, j2, 0:512],
                                 start=gt == 0, stop=gt == NGT - 1)
                        last = t.matmul(bank_f32(2 * ql + 1),
                                        B2T[:, gt, qt * P:(qt + 1) * P],
                                        u2pan2[pr][:, j2, 512:1024],
                                        start=gt == 0, stop=gt == NGT - 1)
                    rec(("f2g", pan2, j2), pe.tick(last))
                    if j2 == 1:
                        pan2 += 1
                rec(("f2mm", q4, cp), pe.n)
        rec("pe_end", pe.n)

    def gpsimd_body(g, dry):
        for ap_out, ap_in in [(idt, ident_in[:, :]), (bqkv, bqkv_in[:, :]),
                              (onec, onec_in[:, :]), (oner, oner_in[:, :]),
                              (dinh, dinh_in[:, :]), (bsel_sb, bsel_in[:, :])]:
            g16.tick(g.dma_start(out=ap_out, in_=ap_in), 16)
        rec("consts_ld", g16.n)
        g16.tick(g.dma_start(out=wobf[:, :], in_=wo_in[:, :]), 16)
        rec("wobf_ld", g16.n)
        for xt in range(NQT):
            if xt >= 4:
                g.wait_ge(pe.sem, M.get(("x2t", 4 * (xt - 4) + 3), 0))
                g.wait_ge(sxl[2].sem, 16 * (xt - 3))
            gx[xt % 2].tick(g.dma_start(out=xbf[xt % 4],
                                        in_=x_in[xt * P:(xt + 1) * P, :]), 16)
        def gather(gg, b, count, kv, ranks=range(4)):
            base = b * 4 * 1024
            if kv == 0:
                for r in ranks:
                    ins = gg.dma_start(
                        out=KT[:, :, r * 1024:(r + 1) * 1024],
                        in_=cc_out_k[base + r * 1024: base + (r + 1) * 1024, :]
                        .rearrange("(dt p) q -> p dt q", p=P))
                    if count:
                        skt.tick(ins, 16)
                    else:
                        ins.then_inc(skt.sem, 16)
            else:
                for r in ranks:
                    ins = gg.dma_start(
                        out=VF[:, r * 8:(r + 1) * 8, :],
                        in_=cc_out_v[base + r * 1024: base + (r + 1) * 1024, :]
                        .rearrange("(tt p) d -> p tt d", p=P))
                    if count:
                        sv[r].tick(ins, 16)
                    else:
                        ins.then_inc(sv[r].sem, 16)

        def branch_gather(kv, ranks, tag):
            if dry:
                gather(g, 0, True, kv, ranks)
            else:
                with g.register("rb%s" % tag) as rb:
                    g.reg_load(rb, bsel_sb[0:1, 0:1])
                    with g.If_eq(rb, 0):
                        gather(g, 0, True, kv, ranks)
                    with g.If_eq(rb, 1):
                        gather(g, 1, False, kv, ranks)

        for fi, ft in enumerate(QKV_ORDER):
            if fi >= 2:
                g.wait_ge(pe.sem, M.get(("qkv", fi - 2), 0))
            gq[fi % 2].tick(
                g.dma_start(out=wqpan[fi % 2],
                            in_=wq_blk[ft].rearrange("p a b -> p (a b)")), 16)
            if fi == 11:
                g.wait_ge(act.sem, M.get("actK_done", 0))
                g16.tick(g.dma_start(
                    out=cc_in_k[:, :].rearrange("(dt p) q -> p dt q", p=P),
                    in_=kvK[:, :, :]), 16)
                g.wait_ge(g16.sem, g16.n)
                ccc.tick(g.collective_compute(
                    "AllGather", OP.bypass, replica_groups=[list(range(8))],
                    ins=[cc_in_k[:, :]], outs=[cc_out_k[:, :]]))
            if fi == 17:
                g.wait_ge(dve.sem, M.get("vstage_done", 0))
                g16.tick(g.dma_start(
                    out=cc_in_v[:, :].rearrange("(tt p) d -> p tt d", p=P),
                    in_=vstage[:, :, :]), 16)
                g.wait_ge(g16.sem, g16.n)
                ccc.tick(g.collective_compute(
                    "AllGather", OP.bypass, replica_groups=[list(range(8))],
                    ins=[cc_in_v[:, :]], outs=[cc_out_v[:, :]]))
            if 18 <= fi <= 21:
                if fi == 18:
                    g.wait_ge(ccc.sem, 1)
                branch_gather(0, [fi - 18], "k%d" % (fi - 18))
        g.wait_ge(ccc.sem, 2)
        g.wait_ge(pe.sem, M.get("qkv_done", 0))
        branch_gather(1, range(4), "v")
        g16.tick(g.dma_start(out=bo_bf, in_=bo_in[0:1, :]), 16)
        g16.tick(g.dma_start(out=ones_bf, in_=oner_in[:, :]), 16)
        rec("bo_ld", g16.n)
        for layer in range(2):
            kblk = k0_blk if layer == 0 else k1_blk
            for gp in range(NGT // 2):
                li = layer * (NGT // 2) + gp
                par = gp % 2
                if li >= 2:
                    pl, pg = divmod(li - 2, NGT // 2)
                    g.wait_ge(pe.sem, M.get(("ffn_mm", pl, pg), 0))
                else:
                    g.wait_ge(pe.sem, M.get("wo_done", 0))
                gm[par].tick(g.dma_start(
                    out=mkpan2[par][:, :, :],
                    in_=kblk[2 * gp:2 * gp + 2].rearrange(
                        "a p c j -> p a (c j)")), 16)
        pan2 = 0
        for q4 in range(2):
            for cp in range(4):
                for g2 in range(NGT // 2):
                    par = pan2 % 2
                    if pan2 >= 2:
                        g.wait_ge(pe.sem, M.get(("f2g", pan2 - 2, 1), 0))
                    else:
                        g.wait_ge(pe.sem, M.get(("ffn_mm", 1, 14), 0))
                    gu[par].tick(g.dma_start(
                        out=u2pan2[par][:, :, :],
                        in_=k2_in[g2 * 256:(g2 + 1) * 256,
                                  cp * 1024:(cp + 1) * 1024].rearrange(
                                      "(a p) c -> p a c", p=P)), 16)
                    pan2 += 1

    def sync_body(sy, dry):
        for xt in range(NQT):
            sy.wait_ge(gx[xt % 2].sem, 16 * (xt // 2 + 1))
            sxl[2].tick(sy.dma_start(out=xbfd[xt * P:(xt + 1) * P, :],
                                     in_=xbf[xt % 4]), 16)
        pan_i = 0
        for cq in range(8):
            for pp in range(2):
                par = pan_i % 2
                if pan_i == 0:
                    sy.wait_ge(g16.sem, M.get("wobf_ld", 0))
                    sy.wait_ge(pe.sem, M.get("scores_done", 0))
                if pan_i >= 2:
                    sy.wait_ge(pe.sem, M.get(("wo_pan", pan_i - 2), 0))
                sw[par].tick(sy.dma_start(
                    out=wopan3[par][:, :, :],
                    in_=wobf[pp * 512:(pp + 1) * 512,
                             cq * 512:(cq + 1) * 512].rearrange(
                                 "(a p) c -> p a c", p=P)), 16)
                pan_i += 1
        for qt in range(NQT):
            sy.wait_ge(dve.sem, M.get(("x1bf_c", qt, 3), 0))
            gspl[qt % 2].tick(sy.dma_start(out=x1sp[qt * P:(qt + 1) * P, :],
                                           in_=x1bf[qt % 2]), 16)
            rec(("spill", qt), gspl[qt % 2].n)
        for _g in range(2):
            rec(("spill_last", _g), gspl[_g].n)
        rec("wopan_done", s16.n)
        for q4 in range(2):
            for ql in (0, 1):
                qt = q4 * 4 + ql
                if q4 > 0:
                    sy.wait_ge(so[ql].sem, 16 * q4)
                else:
                    sy.wait_ge(pe.sem, M.get(("ffn_mm_done", 1), 0))
                    for _g in range(2):
                        sy.wait_ge(gspl[_g].sem, M.get(("spill_last", _g), 0))
                s16.tick(sy.dma_start(out=x1rel[ql],
                                      in_=x1sp[qt * P:(qt + 1) * P, :]), 16)
            rec(("x1rel_q4y", q4), s16.n)
            for ql in (0, 2):
                qt = q4 * 4 + ql
                sy.wait_ge(dve.sem, M.get(("out_ready", qt), 0))
                so[ql].tick(sy.dma_start(out=out_dr[qt * P:(qt + 1) * P, :],
                                         in_=z2g[ql]), 16)
        for ql in range(4):
            sy.wait_ge(so[ql].sem, so[ql].n)

    def scalar_body(s, dry):
        for fi, ft in enumerate(QKV_ORDER):
            s.wait_ge(pe.sem, M.get(("qkv", fi), 0))
            bA = 2 + 2 * (fi % 2)
            if 8 <= ft < 16:
                d0 = kvK[:, ft - 8, :]
            elif ft >= 16:
                if ft >= 18:
                    s.wait_ge(pe.sem, M.get(("vt", ft - 18), 0))
                d0 = vt_tmp[fi % 2][:, :]
            else:
                d0 = QT[:, ft, :]
            s.activation(d0[:, 0:512], bank_f32(bA), AF.Identity,
                         bias=bqkv[:, ft:ft + 1], scale=np.float32(1.0))
            a2 = s.activation(d0[:, 512:1024], bank_f32(bA + 1), AF.Identity,
                              bias=bqkv[:, ft:ft + 1], scale=np.float32(1.0))
            rec(("qkvev", fi), act.tick(a2))
            if fi == 7:
                rec("actK_done", act.n)
        if debug:
            s.wait_ge(dve.sem, M.get("vstage_done", 0))
            tp.tick(s.dma_start(out=taps["tap_qt"][:, :],
                                in_=QT.rearrange("p a b -> p (a b)")), 16)
            tp.tick(s.dma_start(out=taps["tap_kv"][:, :],
                                in_=kvK.rearrange("p a b -> p (a b)")), 16)
            tp.tick(s.dma_start(out=taps["tap_vs"][:, :],
                                in_=vstage.rearrange("p a b -> p (a b)")), 16)
            s.wait_ge(dve.sem, M.get(("x2tev", 31), 0))
            tp.tick(s.dma_start(out=taps["tap_xt"][:, :],
                                in_=XT.rearrange("p a b -> p (a b)")), 16)
            s.wait_ge(skt.sem, 64)
            for _r in range(4):
                s.wait_ge(sv[_r].sem, 16)
            tp.tick(s.dma_start(out=taps["tap_kt"][:, :],
                                in_=KT.rearrange("p a b -> p (a b)")), 16)
            tp.tick(s.dma_start(out=taps["tap_vf"][:, :],
                                in_=VF.rearrange("p a b -> p (a b)")), 16)
        for qt in range(NQT):
            s.wait_ge(dve.sem, M.get(("negm_t", qt), 0))
            for j in range(8):
                a = s.activation(P_sb[:, j * 512:(j + 1) * 512], bank_f32(j),
                                 AF.Exp, bias=negm[:, 0:1], scale=np.float32(SCALE_ATT),
                                 accum_out=sep[:, j:j + 1])
                rec(("exp", qt, j), act.tick(a))
        if debug:
            tp.tick(s.dma_start(out=taps["tap_p"][:, :], in_=P_sb), 16)
            s.wait_ge(dve.sem, M.get(("otev", 7), 0))
            tp.tick(s.dma_start(out=taps["tap_ot"][:, :],
                                in_=OT.rearrange("p a b -> p (a b)")), 16)
        def xgs_load(cq):
            if cq < 2:
                s.wait_ge(pe.sem, M.get(("attnv", NQT - 1), 0))
            else:
                s.wait_ge(dve.sem, M.get(("zadd", cq - 2, NQT - 1), 0))
            sxl[cq % 2].tick(s.dma_start(
                out=xgs[cq % 2][:, :, :],
                in_=xbfd[:, cq * 512:(cq + 1) * 512].rearrange(
                    "(a p) c -> p a c", p=P)), 16)
        for cq in range(4):
            xgs_load(cq)
        for cp in range(4):
            if cp < 2:
                xgs_load(4 + 2 * cp)
                xgs_load(5 + 2 * cp)
            for qt in range(NQT):
                s.wait_ge(dve.sem, M.get(("zadd", 2 * cp + 1, qt), 0))
                a = s.activation(sq_dump1[:, 0:1024],
                                 zbf8[qt][:, cp * 1024:(cp + 1) * 1024],
                                 AF.Copy, bias=0.0, scale=np.float32(1.0),
                                 accum_out=msp[:, qt * 4 + cp:qt * 4 + cp + 1])
                rec(("part1", qt, cp), act.tick(a))
        for qt in range(NQT):
            s.wait_ge(dve.sem, M.get(("x1bf_t", qt), 0))
            a = s.activation(sq_dump1, x1bf[qt % 2], AF.Copy, bias=0.0,
                             scale=np.float32(1.0),
                             accum_out=rs0c[:, qt:qt + 1])
            rec(("rs0a", qt), act.tick(a))
        if debug:
            for _g in range(2):
                s.wait_ge(gspl[_g].sem, M.get(("spill_last", _g), 0))
            tp.tick(s.dma_start(out=taps["tap_x1"][:, :], in_=x1sp[:, :]), 16)
        for layer in range(2):
            s_lay = s0 if layer == 0 else s1
            OUT3 = B1T if layer == 0 else B2T
            first = True
            for gt in range(NGT):
                if gt > BT_GT:
                    continue
                for ch in range(2):
                    if first:
                        for _g in range(2):
                            s.wait_ge(gspl[_g].sem,
                                      M.get(("spill_last", _g), 0))
                        first = False
                    s.wait_ge(dve.sem, M.get(("ffn_tt", layer, gt, ch), 0))
                    a = s.activation(OUT3[:, gt, ch * 512:(ch + 1) * 512],
                                     tmp_ffn[2 * (gt % 2) + ch], AF.Relu,
                                     bias=np.float32(0.0), scale=np.float32(4.0 * s_lay))
                    rec(("ffn_relu", layer, gt, ch), act.tick(a))
            rec(("ffn_act_done", layer), act.n)
        if debug:
            s.wait_ge(dve.sem, M.get(("ffn_ev_done", 0), 0))
            tp.tick(s.dma_start(out=taps["tap_b1"][:, :],
                                in_=B1T.rearrange("p a b -> p (a b)")), 16)
            s.wait_ge(dve.sem, M.get(("ffn_ev_done", 1), 0))
            tp.tick(s.dma_start(out=taps["tap_b2"][:, :],
                                in_=B2T.rearrange("p a b -> p (a b)")), 16)
            s.wait_ge(dve.sem, M.get("rs2row_ev", 0))
            tp.tick(s.dma_start(out=taps["tap_rs"][:, :], in_=rsrow), 16)
        for q4 in range(2):
            for ql in (2, 3):
                qt = q4 * 4 + ql
                if q4 > 0:
                    s.wait_ge(so[ql].sem, 16 * q4)
                else:
                    s.wait_ge(pe.sem, M.get(("ffn_mm_done", 1), 0))
                    for _g in range(2):
                        s.wait_ge(gspl[_g].sem, M.get(("spill_last", _g), 0))
                if q4 == 0:
                    s.wait_ge(dve.sem, M.get(("ffn_ev_done", 1), 0))
                sx.tick(s.dma_start(out=x1rel[ql],
                                    in_=x1sp[qt * P:(qt + 1) * P, :]), 16)
            rec(("x1rel_q4s", q4), sx.n)
            for cp in range(4):
                for ql in range(4):
                    qt = q4 * 4 + ql
                    fl = (q4 * 4 + cp) * 4 + ql
                    s.wait_ge(pe.sem, M.get(("f2mm", q4, cp), 0))
                    s.wait_ge(dve.sem, M.get("bias2_ev", 0))
                    if fl >= 2:
                        s.wait_ge(dve.sem, M.get(("f2tt", fl - 2), 0))
                    a = s.activation(ffn_t[ql % 2], bank_f32(2 * ql, cols=1024),
                                     AF.Identity, bias=bias2[:, qt:qt + 1],
                                     scale=np.float32(2.0 * s2))
                    rec(("f2act", q4, cp, ql), act.tick(a))
                for ql in range(4):
                    qt = q4 * 4 + ql
                    fl = (q4 * 4 + cp) * 4 + ql
                    col = ql * 4 + cp
                    s.wait_ge(dve.sem, M.get(("f2tt", fl), 0))
                    sli = z2g[ql][:, cp * 1024:(cp + 1) * 1024]
                    a = s.activation(sq_dump2[:, 0:1024], sli, AF.Copy,
                                     bias=0.0, scale=np.float32(1.0),
                                     accum_out=msp[:, col:col + 1])
                    act.tick(a)
                    a = s.activation(sq_dump2[:, 0:1024], sli, AF.Square,
                                     accum_out=sqp[:, col:col + 1])
                    rec(("part2", qt, cp), act.tick(a))
            for ql in range(4):
                qt = q4 * 4 + ql
                s.wait_ge(dve.sem, M.get(("var2_t", qt), 0))
                a0 = s.activation(var_sb, var_sb, AF.Ln, bias=np.float32(EPS), scale=np.float32(1.0))
                n_ = act.tick(a0)
                s.wait_ge(act.sem, n_)
                a = s.activation(rstd, var_sb, AF.Exp, bias=np.float32(0.0), scale=np.float32(-0.5))
                rec(("rstd2", qt), act.tick(a))
            for ql in (1, 3):
                qt = q4 * 4 + ql
                s.wait_ge(dve.sem, M.get(("out_ready", qt), 0))
                so[ql].tick(s.dma_start(out=out_dr[qt * P:(qt + 1) * P, :],
                                        in_=z2g[ql]), 16)
        if debug:
            s.wait_ge(tp.sem, tp.n)

    def vector_body(v, dry):
        def vs(ins):
            n = dve.tick(ins)
            v.wait_ge(dve.sem, n)
        for xt in range(NQT):
            for bk in range(4):
                ti = 4 * xt + bk
                v.wait_ge(pe.sem, M.get(("x2t", ti), 0))
                c = v.tensor_copy(XT[:, bk * 8:(bk + 1) * 8, xt * P:(xt + 1) * P],
                                  bank3_bf16(bk % 2, 8, P))
                rec(("x2tev", ti), dve.tick(c))
        for dtv in range(8):
            v.wait_ge(pe.sem, M.get(("vt", dtv), 0))
            c = v.tensor_copy(vstage[:, :, dtv * P:(dtv + 1) * P],
                              bank3_bf16(dtv % 2, 8, P))
            rec(("vtev", dtv), dve.tick(c))
        rec("vstage_done", dve.n)
        for qt in range(NQT):
            for kcg in range(4):
                v.wait_ge(pe.sem, M.get(("scores_p", qt, kcg), 0))
                r_ = v.tensor_reduce(maxp[:, kcg:kcg + 1],
                                     bank_f32(2 * kcg, cols=1024), AX.X, OP.max)
                if kcg == 3:
                    vs(r_)
            vs(v.tensor_reduce(negm, maxp[:, 0:4], AX.X, OP.max))
            c = v.tensor_scalar(negm, negm, float(-SCALE_ATT), None,
                                op0=OP.mult)
            rec(("negm_t", qt), dve.tick(c))
            for rnd in range(4):
                v.wait_ge(pe.sem, M.get(("pt", qt, rnd), 0))
                c = v.tensor_copy(PT[:, rnd * 8:(rnd + 1) * 8, :],
                                  bank3_bf16(rnd, 8, P))
                rec(("ptev", qt, rnd), dve.tick(c))
            v.wait_ge(act.sem, M.get(("exp", qt, 7), 0))
            vs(v.tensor_reduce(sumexp, sep[:, 0:8], AX.X, OP.add))
            vs(v.reciprocal(recip, sumexp))
            v.wait_ge(pe.sem, M.get(("attnv", qt), 0))
            v.tensor_scalar(O_bf[:, 0:512], bank_f32(4), recip[:, 0:1], None,
                            op0=OP.mult)
            c = v.tensor_scalar(O_bf[:, 512:1024], bank_f32(5), recip[:, 0:1],
                                None, op0=OP.mult)
            rec(("oev", qt), dve.tick(c))
            v.wait_ge(pe.sem, M.get(("ot", qt), 0))
            c = v.tensor_copy(
                OT[:, qt, :],
                bank3_bf16(6, 8, P).rearrange("p a b -> p (a b)"))
            rec(("otev", qt), dve.tick(c))
        for cq in range(8):
            v.wait_ge(sxl[cq % 2].sem, 16 * (cq // 2 + 1))
            v.wait_ge(pe.sem, M.get(("wo_cq", cq), 0))
            for qt in range(NQT):
                cnt = v.tensor_add(zbf8[qt][:, cq * 512:(cq + 1) * 512],
                                   xgs[cq % 2][:, qt, :], bank_f32(qt))
                rec(("zadd", cq, qt), dve.tick(cnt))

        def emit_x1tev(qt):
            for bk in range(4):
                ti = 4 * qt + bk
                v.wait_ge(pe.sem, M.get(("x1t", qt, bk), 0))
                c = v.tensor_copy(
                    X1T[:, bk * 8:(bk + 1) * 8, qt * P:(qt + 1) * P],
                    bank3_bf16(6 + bk % 2, 8, P))
                rec(("x1tev", ti), dve.tick(c))

        for qt in range(NQT):
            v.wait_ge(act.sem, M.get(("part1", qt, 3), 0))
            vs(v.tensor_reduce(msum, msp[:, 4 * qt:4 * qt + 4], AX.X,
                               OP.add))
            vs(v.tensor_scalar(m_sb, msum, 1.0 / 4096.0, None, op0=OP.mult))
            if qt >= 2:
                v.wait_ge(pe.sem, M.get(("x1t", qt - 2, 3), 0))
                v.wait_ge(gspl[qt % 2].sem, M.get(("spill", qt - 2), 0))
            for bk in range(4):
                c = v.tensor_scalar(
                    x1bf[qt % 2][:, bk * 1024:(bk + 1) * 1024],
                    zbf8[qt][:, bk * 1024:(bk + 1) * 1024], m_sb[:, 0:1],
                    None, op0=OP.subtract)
                rec(("x1bf_c", qt, bk), dve.tick(c))
            rec(("x1bf_t", qt), dve.n)
            if qt > 0:
                emit_x1tev(qt - 1)
                v.wait_ge(act.sem, M.get(("rs0a", qt - 1), 0))
                c = v.tensor_copy(rs0b[:, qt - 1:qt],
                                  rs0c[:, qt - 1:qt])
                rec(("rs0b", qt - 1), dve.tick(c))
        emit_x1tev(NQT - 1)
        v.wait_ge(act.sem, M.get(("rs0a", NQT - 1), 0))
        c = v.tensor_copy(rs0b[:, NQT - 1:NQT],
                          rs0c[:, NQT - 1:NQT])
        rec(("rs0b", NQT - 1), dve.tick(c))
        rec("x1t_all_ev", dve.n)
        for layer in range(2):
            s_lay = s0 if layer == 0 else s1
            OUT3 = B1T if layer == 0 else B2T
            v.wait_ge(pe.sem, M.get(("rs", layer), 0))
            v.tensor_scalar(rsrow[0:1, 0:512], bank_row_f32(4), 0.5, None,
                            op0=OP.mult)
            c = v.tensor_scalar(rsrow[0:1, 512:1024], bank_row_f32(5), 0.5,
                                None, op0=OP.mult)
            rec(("rshalf", layer), dve.tick(c))
            v.wait_ge(pe.sem, M.get(("rsb", layer), 0))
            vs(v.tensor_copy(rsb_sb, bank_f32(6, cols=1024)))
            rec(("rsbev", layer), dve.n)
            for gp in range(NGT // 2):
                v.wait_ge(pe.sem, M.get(("ffn_mm", layer, gp), 0))
                bA = 4 * (gp % 2)
                for j2 in range(2):
                    gt = 2 * gp + j2
                    for ch in range(2):
                        tb = tmp_ffn[2 * (gt % 2) + ch]
                        if gt >= 2 and gt - 2 <= BT_GT:
                            v.wait_ge(act.sem,
                                      M.get(("ffn_relu", layer, gt - 2, ch), 0))
                        c = v.tensor_sub(tb, bank_f32(bA + 2 * j2 + ch),
                                         rsb_sb[:, ch * 512:(ch + 1) * 512])
                        rec(("ffn_tt", layer, gt, ch), dve.tick(c))
                    if gt == BT_GT:
                        for ch in range(2):
                            v.wait_ge(act.sem,
                                      M.get(("ffn_relu", layer, gt, ch), 0))
                            c = v.tensor_scalar(
                                OUT3[:, gt, ch * 512:(ch + 1) * 512],
                                OUT3[:, gt, ch * 512:(ch + 1) * 512],
                                dinh[:, 0:1], None, op0=OP.mult)
                            dve.tick(c)
                    elif gt > BT_GT:
                        v.wait_ge(dve.sem, M.get(("ffn_tt", layer, gt, 1), 0))
                        for ch in range(2):
                            c = v.tensor_scalar(
                                OUT3[:, gt, ch * 512:(ch + 1) * 512],
                                tmp_ffn[2 * (gt % 2) + ch],
                                float(-16.0 * s_lay), 0.0,
                                op0=OP.mult, op1=OP.min)
                            dve.tick(c)
            rec(("ffn_ev_done", layer), dve.n)
        v.wait_ge(pe.sem, M.get("rs2", 0))
        v.tensor_copy(rsrow[0:1, 0:512], bank_row_f32(4))
        c = v.tensor_copy(rsrow[0:1, 512:1024], bank_row_f32(5))
        rec("rs2row_ev", dve.tick(c))
        v.wait_ge(pe.sem, M.get("rs2col_all", 0))
        for qt in range(NQT):
            c = v.tensor_scalar(bias2[:, qt:qt + 1], bank_f32(qt, cols=1),
                                float(-s2), None, op0=OP.mult)
            dve.tick(c)
        rec("bias2_ev", dve.n)
        for q4 in range(2):
            for cp in range(4):
                for ql in range(4):
                    qt = q4 * 4 + ql
                    fl = (q4 * 4 + cp) * 4 + ql
                    v.wait_ge(act.sem, M.get(("f2act", q4, cp, ql), 0))
                    v.wait_ge(sx.sem, M.get(("x1rel_q4s", q4), 0))
                    v.wait_ge(s16.sem, M.get(("x1rel_q4y", q4), 0))
                    c = v.tensor_add(z2g[ql][:, cp * 1024:(cp + 1) * 1024],
                                     x1rel[ql][:, cp * 1024:(cp + 1) * 1024],
                                     ffn_t[ql % 2])
                    rec(("f2tt", fl), dve.tick(c))
            for ql in range(4):
                qt = q4 * 4 + ql
                v.wait_ge(act.sem, M.get(("part2", qt, 3), 0))
                vs(v.tensor_reduce(msum, msp[:, 4 * ql:4 * ql + 4], AX.X,
                                   OP.add))
                vs(v.tensor_reduce(sqs, sqp[:, 4 * ql:4 * ql + 4], AX.X,
                                   OP.add))
                vs(v.tensor_scalar(m_sb, msum, 1.0 / 4096.0, None, op0=OP.mult))
                vs(v.tensor_mul(m2_sb, m_sb, m_sb))
                if qt > 0:
                    v.wait_ge(act.sem, M.get(("rstd2", qt - 1), 0))
                c = v.tensor_scalar(var_sb, sqs, 1.0 / 4096.0, m2_sb[:, 0:1],
                                    op0=OP.mult, op1=OP.subtract)
                rec(("var2_t", qt), dve.tick(c))
                v.wait_ge(act.sem, M.get(("rstd2", qt), 0))
                c = v.tensor_scalar(z2g[ql], z2g[ql], m_sb[:, 0:1], rstd[:, 0:1],
                                    op0=OP.subtract, op1=OP.mult)
                rec(("out_ready", qt), dve.tick(c))

    bodies = [("tensor", tensor_body), ("gpsimd", gpsimd_body),
              ("sync", sync_body), ("scalar", scalar_body),
              ("vector", vector_body)]

    for _ in range(2):
        for c in all_ctrs:
            c.reset()
        dry_eng = _DryEng()
        for _, body in bodies:
            body(dry_eng, True)
    totals = {id(c): c.n for c in all_ctrs}
    for c in all_ctrs:
        c.reset()
    streams = {}
    for nm, body in bodies:
        streams[nm] = []
        body(_RecEng(streams[nm]), True)
    semnames = {id(c.sem): n for n, c in
                [("g16", g16), ("gx0", gx[0]), ("gx1", gx[1]),
                 ("gq0", gq[0]), ("gq1", gq[1]), ("gm0", gm[0]),
                 ("gm1", gm[1]), ("gu0", gu[0]), ("gu1", gu[1]),
                 ("skt", skt), ("svf", svf), ("s16", s16), ("sw0", sw[0]),
                 ("sw1", sw[1]), ("st", st), ("cc", ccc), ("pe", pe),
                 ("act", act), ("dve", dve), ("tp", tp), ("sx", sx),
                 ("gsp0", gspl[0]), ("gsp1", gspl[1]), ("gsp2", gspl[2]),
                 ("sx0", sxl[0]), ("sx1", sxl[1]), ("sx2", sxl[2]),
                 ("gp1", gp1), ("so0", so[0]), ("so1", so[1]),
                 ("so2", so[2]), ("so3", so[3]), ("sv0", sv[0]),
                 ("sv1", sv[1]), ("sv2", sv[2]), ("sv3", sv[3])]}
    _simulate(streams, semnames)
    for c in all_ctrs:
        c.reset()
    with nc.Block("main", no_gpsimd_drain=True) as blk:
        blk.tensor(lambda e: tensor_body(e, False))
        blk.gpsimd(lambda e: gpsimd_body(e, False))
        blk.sync(lambda e: sync_body(e, False))
        blk.scalar(lambda e: scalar_body(e, False))
        blk.vector(lambda e: vector_body(e, False))
    for c in all_ctrs:
        assert totals[id(c)] == c.n, "emission mismatch vs dry run"
    stack.close()
    return nc


def kernel(L0_attn_x, L0_attn_Wqkv, L0_attn_bqkv, L0_attn_Wo, L0_attn_bo,
           L0_ffn_k0, L0_ffn_s0, L0_ffn_k1, L0_ffn_s1, L0_ffn_k2, L0_ffn_s2):
    x = np.asarray(L0_attn_x, dtype=np.float32)
    wq = np.ascontiguousarray(np.asarray(L0_attn_Wqkv, dtype=np.float32))
    bqkv = np.asarray(L0_attn_bqkv, dtype=np.float32)
    wo = np.ascontiguousarray(np.asarray(L0_attn_Wo, dtype=np.float32))
    bo = np.asarray(L0_attn_bo, dtype=np.float32)
    k0 = np.asarray(L0_ffn_k0)
    k1 = np.asarray(L0_ffn_k1)
    k2 = np.asarray(L0_ffn_k2)
    s0 = float(np.asarray(L0_ffn_s0))
    s1 = float(np.asarray(L0_ffn_s1))
    s2 = float(np.asarray(L0_ffn_s2))

    def blk4(w, nft):
        return np.ascontiguousarray(
            w.reshape(NCT, P, nft, P).transpose(2, 1, 0, 3))

    wq_blkA = blk4(wq, NFT)
    k0_blkA = blk4(np.ascontiguousarray(k0).astype(np.uint8), NGT)
    k1_blkA = blk4(np.ascontiguousarray(k1).astype(np.uint8), NGT)
    k2_u8 = np.ascontiguousarray(np.asarray(k2).astype(np.uint8))
    bqkv_col = np.ascontiguousarray(bqkv.reshape(NFT, P).T)
    bo_repA = np.ascontiguousarray(np.broadcast_to(bo, (P, 4096)))
    ident = np.eye(P, dtype=ml_dtypes.bfloat16)
    ones_col = np.ones((P, 1), dtype=ml_dtypes.bfloat16)
    ones_row = np.ones((1, P), dtype=np.float32)
    dinh_v = np.where(np.arange(P)[:, None] < BT_P, 1.0, -4.0).astype(np.float32)

    nc = build_nc(s0, s1, s2)
    in_maps = []
    for c in range(8):
        b, qs = c // 4, (c % 4) * 1024
        in_maps.append({
            "x": np.ascontiguousarray(x[b, qs:qs + 1024, :]),
            "wq_blk": wq_blkA, "wo": wo,
            "k0_blk": k0_blkA, "k1_blk": k1_blkA, "k2": k2_u8,
            "bqkv_col": bqkv_col, "bo_rep": bo_repA,
            "ident": ident, "ones_col": ones_col, "ones_row": ones_row,
            "dinh": dinh_v,
            "bsel": np.array([[b]], dtype=np.int32),
        })
    res = run_bass_kernel_spmd(nc, in_maps, core_ids=list(range(8)))
    out = np.empty((2, 4096, 4096), dtype=np.float32)
    for c in range(8):
        b, qs = c // 4, (c % 4) * 1024
        out[b, qs:qs + 1024, :] = res.results[c]["out"]
    return out


# version 24
# speedup vs baseline: 1.0363x; 1.0028x over previous
import numpy as np
import ml_dtypes
import concourse.bass as bass
import concourse.mybir as mybir
from concourse.bass_utils import run_bass_kernel_spmd

F32 = mybir.dt.float32
BF16 = mybir.dt.bfloat16
U8 = mybir.dt.uint8
I32 = mybir.dt.int32
AF = mybir.ActivationFunctionType
OP = mybir.AluOpType
AX = mybir.AxisListType

P = 128
NQT = 8
NCT = 32
NDT = 8
NFT = 24
NGT = 32
SCALE_ATT = 0.03125
EPS = 1e-5
BT_GT = 25
BT_P = 77
KB = 1024

OFF_IDENT = 0
OFF_BQKV = 256
OFF_ONEC = 384
OFF_ONER = 416
OFF_DINH = 928
OFF_BSEL = 960
OFF_MAXP = 992
OFF_NEGM = 1024
OFF_SEP = 1056
OFF_SE = 1088
OFF_RECIP = 1120
OFF_MSUM = 1152
OFF_M = 1184
OFF_M2 = 1216
OFF_VAR = 1248
OFF_RSTD = 1280
OFF_SQS = 1312
OFF_RS0 = 1312
OFF_MSP = 1344
OFF_RSROW = 1472
OFF_BIAS2 = 5568
OFF_SQP = 5696
CONSTS_END = 6144
R0 = CONSTS_END
R1 = R0 + 64 * KB
R2 = R1 + 64 * KB
R3 = R2 + 16 * KB
R4 = R3 + 16 * KB
R5 = R4 + 24 * KB
SB_END = R5 + 8 * KB + 512

QKV_ORDER = list(range(8, 16)) + list(range(16, 24)) + list(range(0, 8))


class _DryIns:
    def then_inc(self, sem, k=1):
        return self


_DRY = _DryIns()


class _DryEng:

    def __getattr__(self, name):
        return lambda *a, **k: _DRY


class _RecIns:
    def __init__(self, log):
        self.log = log

    def then_inc(self, sem, k=1):
        self.log.append(("inc", id(sem), k))
        return self


class _RecEng:

    def __init__(self, stream):
        self.stream = stream

    def __getattr__(self, name):
        if name == "wait_ge":
            def w(sem, n):
                self.stream.append(("wait", id(sem), n))
            return w

        def op(*a, **kw):
            log = []
            self.stream.append(("op", log))
            return _RecIns(log)
        return op


def _simulate(streams, semnames):
    vals = {}
    pcs = {e: 0 for e in streams}
    progress = True
    while progress:
        progress = False
        for e, ops in streams.items():
            while pcs[e] < len(ops):
                kind, a, *rest = ops[pcs[e]]
                if kind == "wait":
                    if vals.get(a, 0) < rest[0]:
                        break
                    pcs[e] += 1
                    progress = True
                else:
                    for _, semid, k in a:
                        vals[semid] = vals.get(semid, 0) + k
                    pcs[e] += 1
                    progress = True
    stuck = {e: pcs[e] for e in streams if pcs[e] < len(streams[e])}
    if stuck:
        msg = ["DEADLOCK DETECTED in program graph:"]
        for e, pc in stuck.items():
            kind, semid, n = streams[e][pc]
            msg.append(f"  {e} blocked at op#{pc} waiting "
                       f"{semnames.get(semid, semid)} >= {n} "
                       f"(currently {vals.get(semid, 0)})")
        raise RuntimeError("\n".join(msg))


class Ctr:
    def __init__(self, sem):
        self.sem = sem
        self.n = 0

    def tick(self, ins, k=1):
        ins.then_inc(self.sem, k)
        self.n += k
        return self.n

    def reset(self):
        self.n = 0


def build_nc(s0, s1, s2, debug=False):
    nc = bass.Bass("TRN2", target_bir_lowering=False)

    x_in = nc.dram_tensor("x", [1024, 4096], F32, kind="ExternalInput")
    wq_blk = nc.dram_tensor("wq_blk", [NFT, P, NCT, P], F32, kind="ExternalInput")
    wo_in = nc.dram_tensor("wo", [1024, 4096], F32, kind="ExternalInput")
    k0_blk = nc.dram_tensor("k0_blk", [NGT, P, NCT, P], U8, kind="ExternalInput")
    k1_blk = nc.dram_tensor("k1_blk", [NGT, P, NCT, P], U8, kind="ExternalInput")
    k2_in = nc.dram_tensor("k2", [4096, 4096], U8, kind="ExternalInput")
    bqkv_in = nc.dram_tensor("bqkv_col", [P, NFT], F32, kind="ExternalInput")
    bo_in = nc.dram_tensor("bo_rep", [P, 4096], F32, kind="ExternalInput")
    ident_in = nc.dram_tensor("ident", [P, P], BF16, kind="ExternalInput")
    onec_in = nc.dram_tensor("ones_col", [P, 1], BF16, kind="ExternalInput")
    oner_in = nc.dram_tensor("ones_row", [1, P], F32, kind="ExternalInput")
    dinh_in = nc.dram_tensor("dinh", [P, 1], F32, kind="ExternalInput")
    bsel_in = nc.dram_tensor("bsel", [1, 1], I32, kind="ExternalInput")
    out_dr = nc.dram_tensor("out", [1024, 4096], F32, kind="ExternalOutput")

    taps = {}
    if debug:
        for nm, shp, dt_ in [
                ("tap_xt", [P, 32768], BF16), ("tap_qt", [P, 8192], BF16),
                ("tap_kv", [P, 8192], BF16), ("tap_vs", [P, 8192], BF16),
                ("tap_kt", [P, 32768], BF16), ("tap_vf", [P, 32768], BF16),
                ("tap_p", [P, 4096], BF16), ("tap_ot", [P, 8192], BF16),
                ("tap_x1", [1024, 4096], BF16), ("tap_b1", [P, 32768], BF16),
                ("tap_b2", [P, 32768], BF16), ("tap_rs", [1, 1024], F32)]:
            taps[nm] = nc.dram_tensor(nm, shp, dt_, kind="ExternalOutput")
    wobf = nc.dram_tensor("wobf", [1024, 4096], BF16)
    cc_in_k = nc.dram_tensor("cc_in_k", [1024, 1024], BF16)
    cc_in_v = nc.dram_tensor("cc_in_v", [1024, 1024], BF16)
    cc_out_k = nc.dram_tensor("cc_out_k", [8 * 1024, 1024], BF16,
                              addr_space="Shared")
    cc_out_v = nc.dram_tensor("cc_out_v", [8 * 1024, 1024], BF16,
                              addr_space="Shared")
    x1sp = nc.dram_tensor("x1sp", [1024, 4096], BF16)
    xbfd = nc.dram_tensor("xbfd", [1024, 4096], BF16)

    sb = nc.alloc_sbuf_tensor("sbarena", [P, SB_END], U8)
    psa = nc.alloc_psum_tensor("psarena", [P, 16384], U8)

    def sbv(off, cols, dt, p0=0, np_=P):
        bpe = mybir.dt.size(dt)
        return sb[p0:p0 + np_, off:off + cols * bpe].bitcast(dt)

    def sb3(off, n0, n1, dt):
        bpe = mybir.dt.size(dt)
        return sb[:, off:off + n0 * n1 * bpe].bitcast(dt).rearrange(
            "p (a b) -> p a b", a=n0)

    def bank_f32(b, cols=512):
        return psa[:, b * 2048: b * 2048 + cols * 4].bitcast(F32)

    def bank_slot_bf16(b, slot):
        o = b * 2048 + slot * 256
        return psa[:, o:o + 256].bitcast(BF16)

    def bank_row_f32(b, cols=512):
        return psa[0:1, b * 2048: b * 2048 + cols * 4].bitcast(F32)

    def bank3_bf16(b, n0, n1):
        return psa[:, b * 2048: b * 2048 + n0 * n1 * 2].bitcast(BF16).rearrange(
            "p (a b) -> p a b", a=n0)

    idt = sbv(OFF_IDENT, P, BF16)
    bqkv = sbv(OFF_BQKV, NFT, F32)
    onec = sbv(OFF_ONEC, 1, BF16)
    oner = sbv(OFF_ONER, P, F32, np_=1)
    dinh = sbv(OFF_DINH, 1, F32)
    bsel_sb = sb[0:1, OFF_BSEL:OFF_BSEL + 4].bitcast(I32)
    maxp = sbv(OFF_MAXP, 8, F32)
    negm = sbv(OFF_NEGM, 1, F32)
    sep = sbv(OFF_SEP, 8, F32)
    sumexp = sbv(OFF_SE, 1, F32)
    recip = sbv(OFF_RECIP, 1, F32)
    msum = sbv(OFF_MSUM, 1, F32)
    m_sb = sbv(OFF_M, 1, F32)
    m2_sb = sbv(OFF_M2, 1, F32)
    var_sb = sbv(OFF_VAR, 1, F32)
    rstd = sbv(OFF_RSTD, 1, F32)
    sqs = sbv(OFF_SQS, 1, F32)
    rs0c = sbv(5600, 8, F32)
    rs0b = sbv(5664, 8, BF16)
    msp = sbv(OFF_MSP, 32, F32)
    sqp = sbv(OFF_SQP, 16, F32)
    rsrow = sbv(OFF_RSROW, 1024, F32, np_=1)
    bias2 = sbv(OFF_BIAS2, 8, F32)

    XT = sb3(R0, NCT, 1024, BF16)
    VF = sb3(R0, NCT, 1024, BF16)
    X1T = sb3(R0, NCT, 1024, BF16)
    B2T = sb3(R0, NGT, 1024, BF16)

    xbf = [sbv(R1 + i * 8 * KB, 4096, BF16) for i in range(4)]
    kvK = sb3(R1 + 32 * KB, NDT, 1024, BF16)
    vstage = sb3(R1 + 48 * KB, NDT, 1024, BF16)
    KT = sb3(R1, NDT, 4096, BF16)
    zbf8 = [sbv(R1 + i * 8 * KB, 4096, BF16) for i in range(NQT)]
    wopan3 = [sb3(R2 + i * 4 * KB, 4, 512, BF16) for i in range(2)]
    B1T = sb3(R1, NGT, 1024, BF16)
    z2g = [sbv(R1 + i * 16 * KB, 4096, F32) for i in range(4)]

    QT = sb3(R2, NDT, 1024, BF16)
    bo_rep = sbv(R2, 4096, F32)
    sq_dump2 = sbv(R2, 4096, BF16)

    bo_bf = sbv(R5, 4096, BF16, np_=1)
    ones_bf = sbv(R5 + 8 * KB, P, BF16, np_=1)
    OT = sb3(R3, NQT, NDT * P, BF16)
    mkpan2 = [sb3(R3, 2, 4096, BF16), sb3(R2, 2, 4096, BF16)]
    wqpan = [sbv(R3 + i * 8 * KB, 4096, BF16) for i in range(2)]

    vt_tmp = [sbv(R4 + i * 2 * KB, 1024, BF16) for i in range(2)]
    P_sb = sbv(R4, 4096, BF16)
    PT = sb3(R4 + 8 * KB, NCT, P, BF16)
    O_bf = sbv(R4 + 16 * KB, 1024, BF16)
    xgs = [sb3(R4 + i * 8 * KB, NQT, 512, BF16) for i in range(2)]
    x1bf = [sbv(R4 + i * 8 * KB, 4096, BF16) for i in range(2)]
    sq_dump1 = sbv(R4 + 16 * KB, 4096, BF16)
    rsb_sb = sbv(R4, 1024, F32)
    tmp_ffn = [sbv(R4 + 4 * KB + i * 2 * KB, 512, F32) for i in range(4)]
    u2pan2 = [sb3(R3 + i * 4 * KB, 2, 1024, BF16) for i in range(2)]
    x1rel = [sbv(R3 + 8 * KB, 4096, BF16), sbv(R2 + 8 * KB, 4096, BF16),
             sbv(R4, 4096, BF16), sbv(R4 + 8 * KB, 4096, BF16)]
    ffn_t = [sbv(R4 + 16 * KB + i * 4 * KB, 1024, F32) for i in range(2)]

    import contextlib
    stack = contextlib.ExitStack()
    sems = {}
    for name in ["g16", "gx0", "gx1", "gq0", "gq1", "gm0", "gm1", "gu0",
                 "gu1", "skt", "svf", "s16", "sw0", "sw1", "st", "cc", "pe",
                 "act", "dve", "tp", "sx", "gsp0", "gsp1", "gsp2",
                 "sx0", "sx1", "sx2", "gp1", "so0", "so1", "so2", "so3",
                 "sv0", "sv1", "sv2", "sv3"]:
        sems[name] = stack.enter_context(nc.semaphore(name))
    g16 = Ctr(sems["g16"])
    gx = [Ctr(sems["gx0"]), Ctr(sems["gx1"])]
    gq = [Ctr(sems["gq0"]), Ctr(sems["gq1"])]
    gm = [Ctr(sems["gm0"]), Ctr(sems["gm1"])]
    gu = [Ctr(sems["gu0"]), Ctr(sems["gu1"])]
    skt = Ctr(sems["skt"])
    svf = Ctr(sems["svf"])
    s16 = Ctr(sems["s16"])
    sw = [Ctr(sems["sw0"]), Ctr(sems["sw1"])]
    st = Ctr(sems["st"])
    ccc = Ctr(sems["cc"])
    pe = Ctr(sems["pe"])
    act = Ctr(sems["act"])
    dve = Ctr(sems["dve"])
    tp = Ctr(sems["tp"])
    sx = Ctr(sems["sx"])
    gspl = [Ctr(sems["gsp0"]), Ctr(sems["gsp1"]), Ctr(sems["gsp2"])]
    sxl = [Ctr(sems["sx0"]), Ctr(sems["sx1"]), Ctr(sems["sx2"])]
    gp1 = Ctr(sems["gp1"])
    so = [Ctr(sems["so%d" % i]) for i in range(4)]
    sv = [Ctr(sems["sv%d" % i]) for i in range(4)]
    all_ctrs = [g16, gx[0], gx[1], gq[0], gq[1], gm[0], gm[1], gu[0], gu[1],
                skt, svf, s16, sw[0], sw[1], st, ccc, pe, act, dve, tp, sx,
                gspl[0], gspl[1], gspl[2], sxl[0], sxl[1], sxl[2], gp1,
                so[0], so[1], so[2], so[3], sv[0], sv[1], sv[2], sv[3]]

    M = {}

    def rec(key, val):
        if key in M:
            assert M[key] == val, f"milestone {key} changed: {M[key]} -> {val}"
        M[key] = val

    def tensor_body(t, dry):
        for xt in range(NQT):
            par = xt % 2
            if xt == 0:
                t.wait_ge(g16.sem, M.get("consts_ld", 0))
            t.wait_ge(gx[par].sem, 16 * (xt // 2 + 1))
            for bk in range(4):
                ti = 4 * xt + bk
                if ti >= 2:
                    t.wait_ge(dve.sem, M.get(("x2tev", ti - 2), 0))
                last = None
                for j in range(8):
                    ct = bk * 8 + j
                    last = t.transpose(bank_slot_bf16(bk % 2, j),
                                       xbf[xt % 4][:, ct * P:(ct + 1) * P], idt)
                rec(("x2t", ti), pe.tick(last))
        t.wait_ge(dve.sem, M.get(("x2tev", 31), 0))
        pend_vt = []
        for fi, ft in enumerate(QKV_ORDER):
            t.wait_ge(gq[fi % 2].sem, 16 * (fi // 2 + 1))
            if fi >= 2:
                t.wait_ge(act.sem, M.get(("qkvev", fi - 2), 0))
            bA = 2 + 2 * (fi % 2)
            last = None
            for ct in range(NCT):
                lhs = wqpan[fi % 2][:, ct * P:(ct + 1) * P]
                t.matmul(bank_f32(bA), lhs, XT[:, ct, 0:512],
                         start=ct == 0, stop=ct == NCT - 1)
                last = t.matmul(bank_f32(bA + 1), lhs, XT[:, ct, 512:1024],
                                start=ct == 0, stop=ct == NCT - 1)
            rec(("qkv", fi), pe.tick(last))
            if pend_vt:
                fv, fvi = pend_vt.pop(0)
                dtv = fv - 16
                t.wait_ge(act.sem, M.get(("qkvev", fvi), 0))
                if dtv >= 2:
                    t.wait_ge(dve.sem, M.get(("vtev", dtv - 2), 0))
                lastv = None
                for tt in range(8):
                    lastv = t.transpose(bank_slot_bf16(dtv % 2, tt),
                                        vt_tmp[fvi % 2][:, tt * P:(tt + 1) * P],
                                        idt)
                rec(("vt", dtv), pe.tick(lastv))
            if 16 <= ft < 24:
                pend_vt.append((ft, fi))
        while pend_vt:
            fv, fvi = pend_vt.pop(0)
            dtv = fv - 16
            t.wait_ge(act.sem, M.get(("qkvev", fvi), 0))
            if dtv >= 2:
                t.wait_ge(dve.sem, M.get(("vtev", dtv - 2), 0))
            lastv = None
            for tt in range(8):
                lastv = t.transpose(bank_slot_bf16(dtv % 2, tt),
                                    vt_tmp[fvi % 2][:, tt * P:(tt + 1) * P], idt)
            rec(("vt", dtv), pe.tick(lastv))
        rec("qkv_done", pe.n)
        for qt in range(NQT):
            if qt == 0:
                t.wait_ge(skt.sem, 64)
                t.wait_ge(act.sem, M.get(("qkvev", 23), 0))
                t.wait_ge(dve.sem, M.get("vstage_done", 0))
            else:
                t.wait_ge(act.sem, M.get(("exp", qt - 1, 7), 0))
                t.wait_ge(dve.sem, M.get(("otev", qt - 1), 0))
            for kcg in range(4):
                last = None
                for dt_ in range(NDT):
                    lhs = QT[:, dt_, qt * P:(qt + 1) * P]
                    t.matmul(bank_f32(2 * kcg), lhs,
                             KT[:, dt_, kcg * 1024:kcg * 1024 + 512],
                             start=dt_ == 0, stop=dt_ == NDT - 1)
                    last = t.matmul(bank_f32(2 * kcg + 1), lhs,
                                    KT[:, dt_, kcg * 1024 + 512:(kcg + 1) * 1024],
                                    start=dt_ == 0, stop=dt_ == NDT - 1)
                rec(("scores_p", qt, kcg), pe.tick(last))
            if qt == NQT - 1:
                rec("scores_done", pe.n)
            for rnd in range(4):
                t.wait_ge(act.sem, M.get(("exp", qt, 2 * rnd + 1), 0))
                lastp = None
                for j in range(8):
                    kt = rnd * 8 + j
                    lastp = t.transpose(bank_slot_bf16(rnd, j),
                                        P_sb[:, kt * P:(kt + 1) * P], idt)
                rec(("pt", qt, rnd), pe.tick(lastp))
            t.wait_ge(act.sem, M.get(("exp", qt, 7), 0))
            last = None
            for kt in range(NCT):
                if kt % 8 == 0:
                    t.wait_ge(dve.sem, M.get(("ptev", qt, kt // 8), 0))
                    if qt == 0:
                        t.wait_ge(sv[kt // 8].sem, 16)
                t.matmul(bank_f32(4), PT[:, kt, :], VF[:, kt, 0:512],
                         start=kt == 0, stop=kt == NCT - 1)
                last = t.matmul(bank_f32(5), PT[:, kt, :], VF[:, kt, 512:1024],
                                start=kt == 0, stop=kt == NCT - 1)
            rec(("attnv", qt), pe.tick(last))
            t.wait_ge(dve.sem, M.get(("oev", qt), 0))
            lasto = None
            for dt_ in range(NDT):
                lasto = t.transpose(bank_slot_bf16(6, dt_),
                                    O_bf[:, dt_ * P:(dt_ + 1) * P], idt)
            rec(("ot", qt), pe.tick(lasto))
        rec("att_done", pe.n)
        pan_i = 0
        for cq in range(8):
            for pp in range(2):
                par = pan_i % 2
                t.wait_ge(sw[par].sem, 16 * (pan_i // 2 + 1))
                if pan_i == 0:
                    t.wait_ge(g16.sem, M.get("bo_ld", 0))
                    t.wait_ge(dve.sem, M.get(("otev", NQT - 1), 0))
                if pp == 0:
                    for qt in range(NQT):
                        if cq >= 1:
                            t.wait_ge(dve.sem, M.get(("zadd", cq - 1, qt), 0))
                        t.matmul(bank_f32(qt), ones_bf,
                                 bo_bf[0:1, cq * 512:(cq + 1) * 512],
                                 start=True, stop=False)
                lastm = None
                for a in range(4):
                    dt_ = 4 * pp + a
                    for qt in range(NQT):
                        lhs = OT[:, qt, dt_ * P:(dt_ + 1) * P]
                        lastm = t.matmul(bank_f32(qt), lhs,
                                         wopan3[par][:, a, :],
                                         start=False, stop=dt_ == NDT - 1)
                rec(("wo_pan", pan_i), pe.tick(lastm))
                pan_i += 1
            rec(("wo_cq", cq), pe.n)
        rec("wo_done", pe.n)
        for qt in range(NQT):
            for bk in range(4):
                t.wait_ge(dve.sem, M.get(("x1bf_c", qt, bk), 0))
                ti = 4 * qt + bk
                if ti >= 2:
                    t.wait_ge(dve.sem, M.get(("x1tev", ti - 2), 0))
                lastt = None
                for j in range(8):
                    ct = bk * 8 + j
                    lastt = t.transpose(bank_slot_bf16(6 + bk % 2, j),
                                        x1bf[qt % 2][:, ct * P:(ct + 1) * P],
                                        idt)
                rec(("x1t", qt, bk), pe.tick(lastt))
        for layer in range(2):
            IN3 = X1T if layer == 0 else B1T
            if layer == 0:
                t.wait_ge(dve.sem, M.get("x1t_all_ev", 0))
                last = None
                for qt in range(NQT):
                    t.wait_ge(dve.sem, M.get(("rs0b", qt), 0))
                    bb = 4 + qt // 4
                    cc = (qt % 4) * P
                    last = t.matmul(
                        bank_row_f32(bb)[0:1, cc:cc + P],
                        rs0b[:, qt:qt + 1], idt,
                        start=True, stop=True)
            else:
                t.wait_ge(dve.sem, M.get(("ffn_ev_done", 0), 0))
                t.wait_ge(act.sem, M.get(("ffn_act_done", 0), 0))
                last = None
                for ct in range(NCT):
                    t.matmul(bank_row_f32(4), onec, IN3[:, ct, 0:512],
                             start=ct == 0, stop=ct == NCT - 1)
                    last = t.matmul(bank_row_f32(5), onec, IN3[:, ct, 512:1024],
                                    start=ct == 0, stop=ct == NCT - 1)
            rec(("rs", layer), pe.tick(last))
            t.wait_ge(dve.sem, M.get(("rshalf", layer), 0))
            t.matmul(bank_f32(6), oner, rsrow[0:1, 0:512], start=True, stop=True)
            last = t.matmul(bank_f32(7), oner, rsrow[0:1, 512:1024],
                            start=True, stop=True)
            rec(("rsb", layer), pe.tick(last))
            for gp in range(NGT // 2):
                par = gp % 2
                t.wait_ge(gm[par].sem, 16 * (layer * 8 + gp // 2 + 1))
                if gp >= 2:
                    t.wait_ge(dve.sem, M.get(("ffn_tt", layer, 2 * gp - 3, 1), 0))
                elif gp == 1:
                    t.wait_ge(dve.sem, M.get(("rsbev", layer), 0))
                bA = 4 * (gp % 2)
                last = None
                for ct in range(NCT):
                    for j2 in range(2):
                        lhs = mkpan2[par][:, j2, ct * P:(ct + 1) * P]
                        t.matmul(bank_f32(bA + 2 * j2), lhs, IN3[:, ct, 0:512],
                                 start=ct == 0, stop=ct == NCT - 1)
                        last = t.matmul(bank_f32(bA + 2 * j2 + 1), lhs,
                                        IN3[:, ct, 512:1024],
                                        start=ct == 0, stop=ct == NCT - 1)
                rec(("ffn_mm", layer, gp), pe.tick(last))
            rec(("ffn_mm_done", layer), pe.n)
        t.wait_ge(dve.sem, M.get(("ffn_ev_done", 1), 0))
        t.wait_ge(act.sem, M.get(("ffn_act_done", 1), 0))
        last = None
        for gt in range(NGT):
            t.matmul(bank_row_f32(4), onec, B2T[:, gt, 0:512],
                     start=gt == 0, stop=gt == NGT - 1)
            last = t.matmul(bank_row_f32(5), onec, B2T[:, gt, 512:1024],
                            start=gt == 0, stop=gt == NGT - 1)
        rec("rs2", pe.tick(last))
        t.wait_ge(dve.sem, M.get("rs2row_ev", 0))
        lastc = None
        for qt in range(NQT):
            lastc = t.matmul(bank_f32(qt, cols=1),
                             rsrow[0:1, qt * P:(qt + 1) * P],
                             oner[0:1, 0:1], start=True, stop=True)
        rec("rs2col_all", pe.tick(lastc))
        pan2 = 0
        for q4 in range(2):
            for cp in range(4):
                for gt in range(NGT):
                    pr = pan2 % 2
                    j2 = gt % 2
                    if j2 == 0:
                        t.wait_ge(gu[pr].sem, 16 * (pan2 // 2 + 1))
                    if gt == 0 and q4 == 0 and cp == 0:
                        t.wait_ge(dve.sem, M.get("bias2_ev", 0))
                    last = None
                    for ql in range(4):
                        qt = q4 * 4 + ql
                        if gt == 0 and (q4 > 0 or cp > 0):
                            pq4, pcp = (q4, cp - 1) if cp > 0 else (q4 - 1, 3)
                            t.wait_ge(act.sem, M.get(("f2act", pq4, pcp, ql), 0))
                        t.matmul(bank_f32(2 * ql),
                                 B2T[:, gt, qt * P:(qt + 1) * P],
                                 u2pan2[pr][:, j2, 0:512],
                                 start=gt == 0, stop=gt == NGT - 1)
                        last = t.matmul(bank_f32(2 * ql + 1),
                                        B2T[:, gt, qt * P:(qt + 1) * P],
                                        u2pan2[pr][:, j2, 512:1024],
                                        start=gt == 0, stop=gt == NGT - 1)
                    rec(("f2g", pan2, j2), pe.tick(last))
                    if j2 == 1:
                        pan2 += 1
                rec(("f2mm", q4, cp), pe.n)
        rec("pe_end", pe.n)

    def gpsimd_body(g, dry):
        for ap_out, ap_in in [(idt, ident_in[:, :]), (bqkv, bqkv_in[:, :]),
                              (onec, onec_in[:, :]), (oner, oner_in[:, :]),
                              (dinh, dinh_in[:, :]), (bsel_sb, bsel_in[:, :])]:
            g16.tick(g.dma_start(out=ap_out, in_=ap_in), 16)
        rec("consts_ld", g16.n)
        g16.tick(g.dma_start(out=wobf[:, :], in_=wo_in[:, :]), 16)
        rec("wobf_ld", g16.n)
        for xt in range(NQT):
            if xt >= 4:
                g.wait_ge(pe.sem, M.get(("x2t", 4 * (xt - 4) + 3), 0))
                g.wait_ge(sxl[2].sem, 16 * (xt - 3))
            gx[xt % 2].tick(g.dma_start(out=xbf[xt % 4],
                                        in_=x_in[xt * P:(xt + 1) * P, :]), 16)
        def gather(gg, b, count, kv, ranks=range(4)):
            base = b * 4 * 1024
            if kv == 0:
                for r in ranks:
                    ins = gg.dma_start(
                        out=KT[:, :, r * 1024:(r + 1) * 1024],
                        in_=cc_out_k[base + r * 1024: base + (r + 1) * 1024, :]
                        .rearrange("(dt p) q -> p dt q", p=P))
                    if count:
                        skt.tick(ins, 16)
                    else:
                        ins.then_inc(skt.sem, 16)
            else:
                for r in ranks:
                    ins = gg.dma_start(
                        out=VF[:, r * 8:(r + 1) * 8, :],
                        in_=cc_out_v[base + r * 1024: base + (r + 1) * 1024, :]
                        .rearrange("(tt p) d -> p tt d", p=P))
                    if count:
                        sv[r].tick(ins, 16)
                    else:
                        ins.then_inc(sv[r].sem, 16)

        def branch_gather(kv, ranks, tag):
            if dry:
                gather(g, 0, True, kv, ranks)
            else:
                with g.register("rb%s" % tag) as rb:
                    g.reg_load(rb, bsel_sb[0:1, 0:1])
                    with g.If_eq(rb, 0):
                        gather(g, 0, True, kv, ranks)
                    with g.If_eq(rb, 1):
                        gather(g, 1, False, kv, ranks)

        for fi, ft in enumerate(QKV_ORDER):
            if fi >= 2:
                g.wait_ge(pe.sem, M.get(("qkv", fi - 2), 0))
            gq[fi % 2].tick(
                g.dma_start(out=wqpan[fi % 2],
                            in_=wq_blk[ft].rearrange("p a b -> p (a b)")), 16)
            if fi == 11:
                g.wait_ge(act.sem, M.get("actK_done", 0))
                g16.tick(g.dma_start(
                    out=cc_in_k[:, :].rearrange("(dt p) q -> p dt q", p=P),
                    in_=kvK[:, :, :]), 16)
                g.wait_ge(g16.sem, g16.n)
                ccc.tick(g.collective_compute(
                    "AllGather", OP.bypass, replica_groups=[list(range(8))],
                    ins=[cc_in_k[:, :]], outs=[cc_out_k[:, :]]))
            if fi == 17:
                g.wait_ge(dve.sem, M.get("vstage_done", 0))
                g16.tick(g.dma_start(
                    out=cc_in_v[:, :].rearrange("(tt p) d -> p tt d", p=P),
                    in_=vstage[:, :, :]), 16)
                g.wait_ge(g16.sem, g16.n)
                ccc.tick(g.collective_compute(
                    "AllGather", OP.bypass, replica_groups=[list(range(8))],
                    ins=[cc_in_v[:, :]], outs=[cc_out_v[:, :]]))
            if 18 <= fi <= 21:
                if fi == 18:
                    g.wait_ge(ccc.sem, 1)
                branch_gather(0, [fi - 18], "k%d" % (fi - 18))
        g.wait_ge(ccc.sem, 2)
        g.wait_ge(pe.sem, M.get("qkv_done", 0))
        branch_gather(1, range(4), "v")
        g16.tick(g.dma_start(out=bo_bf, in_=bo_in[0:1, :]), 16)
        g16.tick(g.dma_start(out=ones_bf, in_=oner_in[:, :]), 16)
        rec("bo_ld", g16.n)
        for layer in range(2):
            kblk = k0_blk if layer == 0 else k1_blk
            for gp in range(NGT // 2):
                li = layer * (NGT // 2) + gp
                par = gp % 2
                if li >= 2:
                    pl, pg = divmod(li - 2, NGT // 2)
                    g.wait_ge(pe.sem, M.get(("ffn_mm", pl, pg), 0))
                else:
                    g.wait_ge(pe.sem, M.get("wo_done", 0))
                gm[par].tick(g.dma_start(
                    out=mkpan2[par][:, :, :],
                    in_=kblk[2 * gp:2 * gp + 2].rearrange(
                        "a p c j -> p a (c j)")), 16)
        pan2 = 0
        for q4 in range(2):
            for cp in range(4):
                for g2 in range(NGT // 2):
                    par = pan2 % 2
                    if pan2 >= 2:
                        g.wait_ge(pe.sem, M.get(("f2g", pan2 - 2, 1), 0))
                    else:
                        g.wait_ge(pe.sem, M.get(("ffn_mm", 1, 14), 0))
                    gu[par].tick(g.dma_start(
                        out=u2pan2[par][:, :, :],
                        in_=k2_in[g2 * 256:(g2 + 1) * 256,
                                  cp * 1024:(cp + 1) * 1024].rearrange(
                                      "(a p) c -> p a c", p=P)), 16)
                    pan2 += 1

    def sync_body(sy, dry):
        for xt in range(NQT):
            sy.wait_ge(gx[xt % 2].sem, 16 * (xt // 2 + 1))
            sxl[2].tick(sy.dma_start(out=xbfd[xt * P:(xt + 1) * P, :],
                                     in_=xbf[xt % 4]), 16)
        pan_i = 0
        for cq in range(8):
            for pp in range(2):
                par = pan_i % 2
                if pan_i == 0:
                    sy.wait_ge(g16.sem, M.get("wobf_ld", 0))
                    sy.wait_ge(pe.sem, M.get("scores_done", 0))
                if pan_i >= 2:
                    sy.wait_ge(pe.sem, M.get(("wo_pan", pan_i - 2), 0))
                sw[par].tick(sy.dma_start(
                    out=wopan3[par][:, :, :],
                    in_=wobf[pp * 512:(pp + 1) * 512,
                             cq * 512:(cq + 1) * 512].rearrange(
                                 "(a p) c -> p a c", p=P)), 16)
                pan_i += 1
        for qt in range(NQT):
            sy.wait_ge(dve.sem, M.get(("x1bf_c", qt, 3), 0))
            gspl[qt % 2].tick(sy.dma_start(out=x1sp[qt * P:(qt + 1) * P, :],
                                           in_=x1bf[qt % 2]), 16)
            rec(("spill", qt), gspl[qt % 2].n)
        for _g in range(2):
            rec(("spill_last", _g), gspl[_g].n)
        rec("wopan_done", s16.n)
        for q4 in range(2):
            for ql in (0, 1):
                qt = q4 * 4 + ql
                if q4 > 0:
                    sy.wait_ge(so[ql].sem, 16 * q4)
                else:
                    sy.wait_ge(pe.sem, M.get(("ffn_mm_done", 1), 0))
                    for _g in range(2):
                        sy.wait_ge(gspl[_g].sem, M.get(("spill_last", _g), 0))
                s16.tick(sy.dma_start(out=x1rel[ql],
                                      in_=x1sp[qt * P:(qt + 1) * P, :]), 16)
            rec(("x1rel_q4y", q4), s16.n)
            for ql in (0, 2):
                qt = q4 * 4 + ql
                sy.wait_ge(dve.sem, M.get(("out_ready", qt), 0))
                so[ql].tick(sy.dma_start(out=out_dr[qt * P:(qt + 1) * P, :],
                                         in_=z2g[ql]), 16)
        for ql in range(4):
            sy.wait_ge(so[ql].sem, so[ql].n)

    def scalar_body(s, dry):
        for fi, ft in enumerate(QKV_ORDER):
            s.wait_ge(pe.sem, M.get(("qkv", fi), 0))
            bA = 2 + 2 * (fi % 2)
            if 8 <= ft < 16:
                d0 = kvK[:, ft - 8, :]
            elif ft >= 16:
                if ft >= 18:
                    s.wait_ge(pe.sem, M.get(("vt", ft - 18), 0))
                d0 = vt_tmp[fi % 2][:, :]
            else:
                d0 = QT[:, ft, :]
            s.activation(d0[:, 0:512], bank_f32(bA), AF.Identity,
                         bias=bqkv[:, ft:ft + 1], scale=np.float32(1.0))
            a2 = s.activation(d0[:, 512:1024], bank_f32(bA + 1), AF.Identity,
                              bias=bqkv[:, ft:ft + 1], scale=np.float32(1.0))
            rec(("qkvev", fi), act.tick(a2))
            if fi == 7:
                rec("actK_done", act.n)
        if debug:
            s.wait_ge(dve.sem, M.get("vstage_done", 0))
            tp.tick(s.dma_start(out=taps["tap_qt"][:, :],
                                in_=QT.rearrange("p a b -> p (a b)")), 16)
            tp.tick(s.dma_start(out=taps["tap_kv"][:, :],
                                in_=kvK.rearrange("p a b -> p (a b)")), 16)
            tp.tick(s.dma_start(out=taps["tap_vs"][:, :],
                                in_=vstage.rearrange("p a b -> p (a b)")), 16)
            s.wait_ge(dve.sem, M.get(("x2tev", 31), 0))
            tp.tick(s.dma_start(out=taps["tap_xt"][:, :],
                                in_=XT.rearrange("p a b -> p (a b)")), 16)
            s.wait_ge(skt.sem, 64)
            for _r in range(4):
                s.wait_ge(sv[_r].sem, 16)
            tp.tick(s.dma_start(out=taps["tap_kt"][:, :],
                                in_=KT.rearrange("p a b -> p (a b)")), 16)
            tp.tick(s.dma_start(out=taps["tap_vf"][:, :],
                                in_=VF.rearrange("p a b -> p (a b)")), 16)
        for qt in range(NQT):
            s.wait_ge(dve.sem, M.get(("negm_t", qt), 0))
            for j in range(8):
                a = s.activation(P_sb[:, j * 512:(j + 1) * 512], bank_f32(j),
                                 AF.Exp, bias=negm[:, 0:1], scale=np.float32(SCALE_ATT),
                                 accum_out=sep[:, j:j + 1])
                rec(("exp", qt, j), act.tick(a))
        if debug:
            tp.tick(s.dma_start(out=taps["tap_p"][:, :], in_=P_sb), 16)
            s.wait_ge(dve.sem, M.get(("otev", 7), 0))
            tp.tick(s.dma_start(out=taps["tap_ot"][:, :],
                                in_=OT.rearrange("p a b -> p (a b)")), 16)
        def xgs_load(cq):
            if cq < 2:
                s.wait_ge(pe.sem, M.get(("attnv", NQT - 1), 0))
            else:
                s.wait_ge(dve.sem, M.get(("zadd", cq - 2, NQT - 1), 0))
            sxl[cq % 2].tick(s.dma_start(
                out=xgs[cq % 2][:, :, :],
                in_=xbfd[:, cq * 512:(cq + 1) * 512].rearrange(
                    "(a p) c -> p a c", p=P)), 16)
        for cq in range(4):
            xgs_load(cq)
        for cp in range(4):
            if cp < 2:
                xgs_load(4 + 2 * cp)
                xgs_load(5 + 2 * cp)
            for qt in range(NQT):
                s.wait_ge(dve.sem, M.get(("zadd", 2 * cp + 1, qt), 0))
                a = s.activation(sq_dump1[:, 0:1024],
                                 zbf8[qt][:, cp * 1024:(cp + 1) * 1024],
                                 AF.Copy, bias=0.0, scale=np.float32(1.0),
                                 accum_out=msp[:, qt * 4 + cp:qt * 4 + cp + 1])
                rec(("part1", qt, cp), act.tick(a))
        for qt in range(NQT):
            s.wait_ge(dve.sem, M.get(("x1bf_t", qt), 0))
            a = s.activation(sq_dump1, x1bf[qt % 2], AF.Copy, bias=0.0,
                             scale=np.float32(1.0),
                             accum_out=rs0c[:, qt:qt + 1])
            rec(("rs0a", qt), act.tick(a))
        if debug:
            for _g in range(2):
                s.wait_ge(gspl[_g].sem, M.get(("spill_last", _g), 0))
            tp.tick(s.dma_start(out=taps["tap_x1"][:, :], in_=x1sp[:, :]), 16)
        for layer in range(2):
            s_lay = s0 if layer == 0 else s1
            OUT3 = B1T if layer == 0 else B2T
            first = True
            for gt in range(NGT):
                if gt > BT_GT:
                    continue
                for ch in range(2):
                    if first:
                        for _g in range(2):
                            s.wait_ge(gspl[_g].sem,
                                      M.get(("spill_last", _g), 0))
                        first = False
                    s.wait_ge(dve.sem, M.get(("ffn_tt", layer, gt, ch), 0))
                    a = s.activation(OUT3[:, gt, ch * 512:(ch + 1) * 512],
                                     tmp_ffn[2 * (gt % 2) + ch], AF.Relu,
                                     bias=np.float32(0.0), scale=np.float32(4.0 * s_lay))
                    rec(("ffn_relu", layer, gt, ch), act.tick(a))
            rec(("ffn_act_done", layer), act.n)
        if debug:
            s.wait_ge(dve.sem, M.get(("ffn_ev_done", 0), 0))
            tp.tick(s.dma_start(out=taps["tap_b1"][:, :],
                                in_=B1T.rearrange("p a b -> p (a b)")), 16)
            s.wait_ge(dve.sem, M.get(("ffn_ev_done", 1), 0))
            tp.tick(s.dma_start(out=taps["tap_b2"][:, :],
                                in_=B2T.rearrange("p a b -> p (a b)")), 16)
            s.wait_ge(dve.sem, M.get("rs2row_ev", 0))
            tp.tick(s.dma_start(out=taps["tap_rs"][:, :], in_=rsrow), 16)
        for q4 in range(2):
            for ql in (2, 3):
                qt = q4 * 4 + ql
                if q4 > 0:
                    s.wait_ge(so[ql].sem, 16 * q4)
                else:
                    s.wait_ge(pe.sem, M.get(("ffn_mm_done", 1), 0))
                    for _g in range(2):
                        s.wait_ge(gspl[_g].sem, M.get(("spill_last", _g), 0))
                if q4 == 0:
                    s.wait_ge(dve.sem, M.get(("ffn_ev_done", 1), 0))
                sx.tick(s.dma_start(out=x1rel[ql],
                                    in_=x1sp[qt * P:(qt + 1) * P, :]), 16)
            rec(("x1rel_q4s", q4), sx.n)
            for cp in range(4):
                for ql in range(4):
                    qt = q4 * 4 + ql
                    fl = (q4 * 4 + cp) * 4 + ql
                    s.wait_ge(pe.sem, M.get(("f2mm", q4, cp), 0))
                    s.wait_ge(dve.sem, M.get("bias2_ev", 0))
                    if fl >= 2:
                        s.wait_ge(dve.sem, M.get(("f2tt", fl - 2), 0))
                    a = s.activation(ffn_t[ql % 2], bank_f32(2 * ql, cols=1024),
                                     AF.Identity, bias=bias2[:, qt:qt + 1],
                                     scale=np.float32(2.0 * s2))
                    rec(("f2act", q4, cp, ql), act.tick(a))
                for ql in range(4):
                    qt = q4 * 4 + ql
                    fl = (q4 * 4 + cp) * 4 + ql
                    col = ql * 4 + cp
                    s.wait_ge(dve.sem, M.get(("f2tt", fl), 0))
                    sli = z2g[ql][:, cp * 1024:(cp + 1) * 1024]
                    a = s.activation(sq_dump2[:, 0:1024], sli, AF.Copy,
                                     bias=0.0, scale=np.float32(1.0),
                                     accum_out=msp[:, col:col + 1])
                    act.tick(a)
                    a = s.activation(sq_dump2[:, 0:1024], sli, AF.Square,
                                     accum_out=sqp[:, col:col + 1])
                    rec(("part2", qt, cp), act.tick(a))
            for ql in range(4):
                qt = q4 * 4 + ql
                s.wait_ge(dve.sem, M.get(("var2_t", qt), 0))
                a0 = s.activation(var_sb, var_sb, AF.Ln, bias=np.float32(EPS), scale=np.float32(1.0))
                n_ = act.tick(a0)
                s.wait_ge(act.sem, n_)
                a = s.activation(rstd, var_sb, AF.Exp, bias=np.float32(0.0), scale=np.float32(-0.5))
                rec(("rstd2", qt), act.tick(a))
            for ql in (1, 3):
                qt = q4 * 4 + ql
                s.wait_ge(dve.sem, M.get(("out_ready", qt), 0))
                so[ql].tick(s.dma_start(out=out_dr[qt * P:(qt + 1) * P, :],
                                        in_=z2g[ql]), 16)
        if debug:
            s.wait_ge(tp.sem, tp.n)

    def vector_body(v, dry):
        def vs(ins):
            n = dve.tick(ins)
            v.wait_ge(dve.sem, n)
        for xt in range(NQT):
            for bk in range(4):
                ti = 4 * xt + bk
                v.wait_ge(pe.sem, M.get(("x2t", ti), 0))
                c = v.tensor_copy(XT[:, bk * 8:(bk + 1) * 8, xt * P:(xt + 1) * P],
                                  bank3_bf16(bk % 2, 8, P))
                rec(("x2tev", ti), dve.tick(c))
        for dtv in range(8):
            v.wait_ge(pe.sem, M.get(("vt", dtv), 0))
            c = v.tensor_copy(vstage[:, :, dtv * P:(dtv + 1) * P],
                              bank3_bf16(dtv % 2, 8, P))
            rec(("vtev", dtv), dve.tick(c))
        rec("vstage_done", dve.n)
        for qt in range(NQT):
            for kcg in range(4):
                v.wait_ge(pe.sem, M.get(("scores_p", qt, kcg), 0))
                r_ = v.tensor_reduce(maxp[:, kcg:kcg + 1],
                                     bank_f32(2 * kcg, cols=1024), AX.X, OP.max)
                if kcg == 3:
                    vs(r_)
            vs(v.tensor_reduce(negm, maxp[:, 0:4], AX.X, OP.max))
            c = v.tensor_scalar(negm, negm, float(-SCALE_ATT), None,
                                op0=OP.mult)
            rec(("negm_t", qt), dve.tick(c))
            for rnd in range(4):
                v.wait_ge(pe.sem, M.get(("pt", qt, rnd), 0))
                c = v.tensor_copy(PT[:, rnd * 8:(rnd + 1) * 8, :],
                                  bank3_bf16(rnd, 8, P))
                rec(("ptev", qt, rnd), dve.tick(c))
            v.wait_ge(act.sem, M.get(("exp", qt, 7), 0))
            vs(v.tensor_reduce(sumexp, sep[:, 0:8], AX.X, OP.add))
            vs(v.reciprocal(recip, sumexp))
            v.wait_ge(pe.sem, M.get(("attnv", qt), 0))
            v.tensor_scalar(O_bf[:, 0:512], bank_f32(4), recip[:, 0:1], None,
                            op0=OP.mult)
            c = v.tensor_scalar(O_bf[:, 512:1024], bank_f32(5), recip[:, 0:1],
                                None, op0=OP.mult)
            rec(("oev", qt), dve.tick(c))
            v.wait_ge(pe.sem, M.get(("ot", qt), 0))
            c = v.tensor_copy(
                OT[:, qt, :],
                bank3_bf16(6, 8, P).rearrange("p a b -> p (a b)"))
            rec(("otev", qt), dve.tick(c))
        for cq in range(8):
            v.wait_ge(sxl[cq % 2].sem, 16 * (cq // 2 + 1))
            v.wait_ge(pe.sem, M.get(("wo_cq", cq), 0))
            for qt in range(NQT):
                cnt = v.tensor_add(zbf8[qt][:, cq * 512:(cq + 1) * 512],
                                   xgs[cq % 2][:, qt, :], bank_f32(qt))
                rec(("zadd", cq, qt), dve.tick(cnt))

        def emit_x1tev(qt):
            for bk in range(4):
                ti = 4 * qt + bk
                v.wait_ge(pe.sem, M.get(("x1t", qt, bk), 0))
                c = v.tensor_copy(
                    X1T[:, bk * 8:(bk + 1) * 8, qt * P:(qt + 1) * P],
                    bank3_bf16(6 + bk % 2, 8, P))
                rec(("x1tev", ti), dve.tick(c))

        for qt in range(NQT):
            v.wait_ge(act.sem, M.get(("part1", qt, 3), 0))
            vs(v.tensor_reduce(msum, msp[:, 4 * qt:4 * qt + 4], AX.X,
                               OP.add))
            vs(v.tensor_scalar(m_sb, msum, 1.0 / 4096.0, None, op0=OP.mult))
            if qt >= 2:
                v.wait_ge(pe.sem, M.get(("x1t", qt - 2, 3), 0))
                v.wait_ge(gspl[qt % 2].sem, M.get(("spill", qt - 2), 0))
            for bk in range(4):
                c = v.tensor_scalar(
                    x1bf[qt % 2][:, bk * 1024:(bk + 1) * 1024],
                    zbf8[qt][:, bk * 1024:(bk + 1) * 1024], m_sb[:, 0:1],
                    None, op0=OP.subtract)
                rec(("x1bf_c", qt, bk), dve.tick(c))
            rec(("x1bf_t", qt), dve.n)
            if qt > 0:
                emit_x1tev(qt - 1)
                v.wait_ge(act.sem, M.get(("rs0a", qt - 1), 0))
                c = v.tensor_copy(rs0b[:, qt - 1:qt],
                                  rs0c[:, qt - 1:qt])
                rec(("rs0b", qt - 1), dve.tick(c))
        emit_x1tev(NQT - 1)
        v.wait_ge(act.sem, M.get(("rs0a", NQT - 1), 0))
        c = v.tensor_copy(rs0b[:, NQT - 1:NQT],
                          rs0c[:, NQT - 1:NQT])
        rec(("rs0b", NQT - 1), dve.tick(c))
        rec("x1t_all_ev", dve.n)
        for layer in range(2):
            s_lay = s0 if layer == 0 else s1
            OUT3 = B1T if layer == 0 else B2T
            v.wait_ge(pe.sem, M.get(("rs", layer), 0))
            v.tensor_scalar(rsrow[0:1, 0:512], bank_row_f32(4), 0.5, None,
                            op0=OP.mult)
            c = v.tensor_scalar(rsrow[0:1, 512:1024], bank_row_f32(5), 0.5,
                                None, op0=OP.mult)
            rec(("rshalf", layer), dve.tick(c))
            v.wait_ge(pe.sem, M.get(("rsb", layer), 0))
            vs(v.tensor_copy(rsb_sb, bank_f32(6, cols=1024)))
            rec(("rsbev", layer), dve.n)
            for gp in range(NGT // 2):
                v.wait_ge(pe.sem, M.get(("ffn_mm", layer, gp), 0))
                bA = 4 * (gp % 2)
                for j2 in range(2):
                    gt = 2 * gp + j2
                    for ch in range(2):
                        tb = tmp_ffn[2 * (gt % 2) + ch]
                        if gt >= 2 and gt - 2 <= BT_GT:
                            v.wait_ge(act.sem,
                                      M.get(("ffn_relu", layer, gt - 2, ch), 0))
                        c = v.tensor_sub(tb, bank_f32(bA + 2 * j2 + ch),
                                         rsb_sb[:, ch * 512:(ch + 1) * 512])
                        rec(("ffn_tt", layer, gt, ch), dve.tick(c))
                    if gt == BT_GT:
                        for ch in range(2):
                            v.wait_ge(act.sem,
                                      M.get(("ffn_relu", layer, gt, ch), 0))
                            c = v.tensor_scalar(
                                OUT3[:, gt, ch * 512:(ch + 1) * 512],
                                OUT3[:, gt, ch * 512:(ch + 1) * 512],
                                dinh[:, 0:1], None, op0=OP.mult)
                            dve.tick(c)
                    elif gt > BT_GT:
                        v.wait_ge(dve.sem, M.get(("ffn_tt", layer, gt, 1), 0))
                        for ch in range(2):
                            c = v.tensor_scalar(
                                OUT3[:, gt, ch * 512:(ch + 1) * 512],
                                tmp_ffn[2 * (gt % 2) + ch],
                                float(-16.0 * s_lay), 0.0,
                                op0=OP.mult, op1=OP.min)
                            dve.tick(c)
            rec(("ffn_ev_done", layer), dve.n)
        v.wait_ge(pe.sem, M.get("rs2", 0))
        v.tensor_copy(rsrow[0:1, 0:512], bank_row_f32(4))
        c = v.tensor_copy(rsrow[0:1, 512:1024], bank_row_f32(5))
        rec("rs2row_ev", dve.tick(c))
        v.wait_ge(pe.sem, M.get("rs2col_all", 0))
        for qt in range(NQT):
            c = v.tensor_scalar(bias2[:, qt:qt + 1], bank_f32(qt, cols=1),
                                float(-s2), None, op0=OP.mult)
            dve.tick(c)
        rec("bias2_ev", dve.n)
        for q4 in range(2):
            for cp in range(4):
                for ql in range(4):
                    qt = q4 * 4 + ql
                    fl = (q4 * 4 + cp) * 4 + ql
                    v.wait_ge(act.sem, M.get(("f2act", q4, cp, ql), 0))
                    v.wait_ge(sx.sem, M.get(("x1rel_q4s", q4), 0))
                    v.wait_ge(s16.sem, M.get(("x1rel_q4y", q4), 0))
                    c = v.tensor_add(z2g[ql][:, cp * 1024:(cp + 1) * 1024],
                                     x1rel[ql][:, cp * 1024:(cp + 1) * 1024],
                                     ffn_t[ql % 2])
                    rec(("f2tt", fl), dve.tick(c))
            for ql in range(4):
                qt = q4 * 4 + ql
                v.wait_ge(act.sem, M.get(("part2", qt, 3), 0))
                vs(v.tensor_reduce(msum, msp[:, 4 * ql:4 * ql + 4], AX.X,
                                   OP.add))
                vs(v.tensor_reduce(sqs, sqp[:, 4 * ql:4 * ql + 4], AX.X,
                                   OP.add))
                vs(v.tensor_scalar(m_sb, msum, 1.0 / 4096.0, None, op0=OP.mult))
                vs(v.tensor_mul(m2_sb, m_sb, m_sb))
                if qt > 0:
                    v.wait_ge(act.sem, M.get(("rstd2", qt - 1), 0))
                c = v.tensor_scalar(var_sb, sqs, 1.0 / 4096.0, m2_sb[:, 0:1],
                                    op0=OP.mult, op1=OP.subtract)
                rec(("var2_t", qt), dve.tick(c))
                v.wait_ge(act.sem, M.get(("rstd2", qt), 0))
                c = v.tensor_scalar(z2g[ql], z2g[ql], m_sb[:, 0:1], rstd[:, 0:1],
                                    op0=OP.subtract, op1=OP.mult)
                rec(("out_ready", qt), dve.tick(c))

    bodies = [("tensor", tensor_body), ("gpsimd", gpsimd_body),
              ("sync", sync_body), ("scalar", scalar_body),
              ("vector", vector_body)]

    for _ in range(2):
        for c in all_ctrs:
            c.reset()
        dry_eng = _DryEng()
        for _, body in bodies:
            body(dry_eng, True)
    totals = {id(c): c.n for c in all_ctrs}
    for c in all_ctrs:
        c.reset()
    streams = {}
    for nm, body in bodies:
        streams[nm] = []
        body(_RecEng(streams[nm]), True)
    semnames = {id(c.sem): n for n, c in
                [("g16", g16), ("gx0", gx[0]), ("gx1", gx[1]),
                 ("gq0", gq[0]), ("gq1", gq[1]), ("gm0", gm[0]),
                 ("gm1", gm[1]), ("gu0", gu[0]), ("gu1", gu[1]),
                 ("skt", skt), ("svf", svf), ("s16", s16), ("sw0", sw[0]),
                 ("sw1", sw[1]), ("st", st), ("cc", ccc), ("pe", pe),
                 ("act", act), ("dve", dve), ("tp", tp), ("sx", sx),
                 ("gsp0", gspl[0]), ("gsp1", gspl[1]), ("gsp2", gspl[2]),
                 ("sx0", sxl[0]), ("sx1", sxl[1]), ("sx2", sxl[2]),
                 ("gp1", gp1), ("so0", so[0]), ("so1", so[1]),
                 ("so2", so[2]), ("so3", so[3]), ("sv0", sv[0]),
                 ("sv1", sv[1]), ("sv2", sv[2]), ("sv3", sv[3])]}
    _simulate(streams, semnames)
    for c in all_ctrs:
        c.reset()
    with nc.Block("main", no_gpsimd_drain=True) as blk:
        blk.tensor(lambda e: tensor_body(e, False))
        blk.gpsimd(lambda e: gpsimd_body(e, False))
        blk.sync(lambda e: sync_body(e, False))
        blk.scalar(lambda e: scalar_body(e, False))
        blk.vector(lambda e: vector_body(e, False))
    for c in all_ctrs:
        assert totals[id(c)] == c.n, "emission mismatch vs dry run"
    stack.close()
    return nc


def kernel(L0_attn_x, L0_attn_Wqkv, L0_attn_bqkv, L0_attn_Wo, L0_attn_bo,
           L0_ffn_k0, L0_ffn_s0, L0_ffn_k1, L0_ffn_s1, L0_ffn_k2, L0_ffn_s2):
    x = np.asarray(L0_attn_x, dtype=np.float32)
    wq = np.ascontiguousarray(np.asarray(L0_attn_Wqkv, dtype=np.float32))
    bqkv = np.asarray(L0_attn_bqkv, dtype=np.float32)
    wo = np.ascontiguousarray(np.asarray(L0_attn_Wo, dtype=np.float32))
    bo = np.asarray(L0_attn_bo, dtype=np.float32)
    k0 = np.asarray(L0_ffn_k0)
    k1 = np.asarray(L0_ffn_k1)
    k2 = np.asarray(L0_ffn_k2)
    s0 = float(np.asarray(L0_ffn_s0))
    s1 = float(np.asarray(L0_ffn_s1))
    s2 = float(np.asarray(L0_ffn_s2))

    def blk4(w, nft):
        return np.ascontiguousarray(
            w.reshape(NCT, P, nft, P).transpose(2, 1, 0, 3))

    wq_blkA = blk4(wq, NFT)
    k0_blkA = blk4(np.ascontiguousarray(k0).astype(np.uint8), NGT)
    k1_blkA = blk4(np.ascontiguousarray(k1).astype(np.uint8), NGT)
    k2_u8 = np.ascontiguousarray(np.asarray(k2).astype(np.uint8))
    bqkv_col = np.ascontiguousarray(bqkv.reshape(NFT, P).T)
    bo_repA = np.ascontiguousarray(np.broadcast_to(bo, (P, 4096)))
    ident = np.eye(P, dtype=ml_dtypes.bfloat16)
    ones_col = np.ones((P, 1), dtype=ml_dtypes.bfloat16)
    ones_row = np.ones((1, P), dtype=np.float32)
    dinh_v = np.where(np.arange(P)[:, None] < BT_P, 1.0, -4.0).astype(np.float32)

    nc = build_nc(s0, s1, s2)
    in_maps = []
    for c in range(8):
        b, qs = c // 4, (c % 4) * 1024
        in_maps.append({
            "x": np.ascontiguousarray(x[b, qs:qs + 1024, :]),
            "wq_blk": wq_blkA, "wo": wo,
            "k0_blk": k0_blkA, "k1_blk": k1_blkA, "k2": k2_u8,
            "bqkv_col": bqkv_col, "bo_rep": bo_repA,
            "ident": ident, "ones_col": ones_col, "ones_row": ones_row,
            "dinh": dinh_v,
            "bsel": np.array([[b]], dtype=np.int32),
        })
    res = run_bass_kernel_spmd(nc, in_maps, core_ids=list(range(8)))
    out = np.empty((2, 4096, 4096), dtype=np.float32)
    for c in range(8):
        b, qs = c // 4, (c % 4) * 1024
        out[b, qs:qs + 1024, :] = res.results[c]["out"]
    return out
